# version 19
# speedup vs baseline: 1.0488x; 1.0036x over previous
import numpy as np
import contextlib
import ml_dtypes
import concourse.bass as bass
import concourse.mybir as mybir
from concourse.aot_env import allow_ici_all_to_all

F32 = mybir.dt.float32
BF16 = mybir.dt.bfloat16
I32 = mybir.dt.int32
ALU = mybir.AluOpType
ACT = mybir.ActivationFunctionType
AX = mybir.AxisListType
BF = ml_dtypes.bfloat16

NCOR = 8
H, E, NST, RNK, KC = 1024, 2048, 16, 64, 4
EC = E // NCOR
NET = EC // 128
HK = H // 128
NEXP, TOPK, CAP = 8, 2, 4096
EPS = 1e-5
RG = [list(range(NCOR))]


class P:
    def __init__(self, L=8192):
        self.L = L
        self.T = 2 * L
        self.NSL = NCOR
        self.TS = self.T // NCOR
        self.TC = self.TS
        assert self.L % self.TS == 0
        self.SLB = self.L // self.TS
        self.NCH = max(1, self.TS // 512)
        self.NCK = min(512, self.TS)
        self.NTT = self.TC // 128


class Auto:
    ENG_SEMS = {"tensor": "c_mm", "vector": "c_ve", "scalar": "c_sc",
                "gpsimd": "c_gp", "sync": "c_sy"}

    def __init__(self, nc, block, ctx):
        self.nc, self.block, self.ctx = nc, block, ctx
        self.sems, self.ct = {}, {}
        for name in set(self.ENG_SEMS.values()):
            self.sems[name] = ctx.enter_context(nc.semaphore(name))
            self.ct[name] = 0
        self.last_w = {}
        self.last_r = {}
        self.waited = {e: {} for e in self.ENG_SEMS}
        self._cur = None

    def dsem(self, name):
        if name not in self.sems:
            self.sems[name] = self.ctx.enter_context(self.nc.semaphore(name))
            self.ct[name] = 0
        return self.sems[name]

    def on(self, engine_name):
        def deco(fn):
            def wrapper(eng):
                prev = self._cur
                self._cur = (engine_name, eng)
                try:
                    fn()
                finally:
                    self._cur = prev
            getattr(self.block, engine_name)(wrapper)
        return deco

    def _wait(self, eng_name, eng, sem, val):
        if val <= 0:
            return
        if self.waited[eng_name].get(sem, -1) >= val:
            return
        eng.wait_ge(self.sems[sem], val)
        self.waited[eng_name][sem] = val

    def emit(self, fn, reads=(), writes=(), dma=False, cc=False, dkey=None):
        eng_name, eng = self._cur
        needs = {}
        for b in reads:
            if b in self.last_w:
                s, v = self.last_w[b]
                needs[s] = max(needs.get(s, 0), v)
        for b in writes:
            if b in self.last_w:
                s, v = self.last_w[b]
                needs[s] = max(needs.get(s, 0), v)
            for s, v in self.last_r.get(b, {}).items():
                needs[s] = max(needs.get(s, 0), v)
        for s, v in needs.items():
            self._wait(eng_name, eng, s, v)
        inst = fn(eng)
        insts = inst if isinstance(inst, list) else [inst]
        if cc:
            sem_name, step = "cc_" + (writes[0] if writes else "x"), 1
            self.dsem(sem_name)
        elif dma:
            sem_name, step = "qd_" + (dkey or (writes[0] if writes else "x")), 16
            self.dsem(sem_name)
        else:
            sem_name, step = self.ENG_SEMS[eng_name], 1
        for it in insts:
            self.ct[sem_name] += step
            it.then_inc(self.sems[sem_name], step)
        v = self.ct[sem_name]
        inst = insts[-1]
        for b in writes:
            self.last_w[b] = (sem_name, v)
            self.last_r.pop(b, None)
        for b in reads:
            self.last_r.setdefault(b, {})[sem_name] = v
        return inst

    def fence(self):
        snap = [(s, v) for s, v in self.ct.items() if v > 0]
        for eng_name in self.ENG_SEMS:
            def mk(eng_name=eng_name):
                def body():
                    eng = self._cur[1]
                    for s, v in snap:
                        if self.waited[eng_name].get(s, -1) < v:
                            eng.wait_ge(self.sems[s], v)
                            self.waited[eng_name][s] = v
                return body
            self.on(eng_name)(mk())

    def final_dma_waits(self):
        @self.on("sync")
        def _():
            eng = self._cur[1]
            for s_name, v in sorted(self.ct.items()):
                if (s_name.startswith("qd_") or s_name.startswith("cc_")) and v > 0:
                    eng.wait_ge(self.sems[s_name], v)


def build(p: P, debug=False):
    nc = bass.Bass("TRN2", target_bir_lowering=False)
    ctx = contextlib.ExitStack()
    def mk_sb(ctx_):
        def sb(name, shape, dtype=F32):
            esz = 2 if dtype == BF16 else 4
            nb = esz
            for dd in shape[1:]:
                nb *= dd
            al = 128 if nb >= 2048 else 32
            return ctx_.enter_context(nc.sbuf_tensor(name, shape, dtype,
                                                     align_bytes=al))
        return sb
    TS, NCH, NCK, NTT, NSL = p.TS, p.NCH, p.NCK, p.NTT, p.NSL
    CH4 = 4 if p.TC % (4 * 512) == 0 else 1
    CSZ = p.TC // CH4

    dram = lambda n, s, d=F32, k="Internal": nc.dram_tensor(n, s, d, kind=k)
    x_rows = dram("x_rows", [p.TC, H], F32, "ExternalInput")
    w_ip_x = dram("w_ip_x", [H, EC], F32, "ExternalInput")
    w_ip_g = dram("w_ip_g", [H, EC], F32, "ExternalInput")
    b_ip_x = dram("b_ip_x", [1, EC], F32, "ExternalInput")
    b_ip_g = dram("b_ip_g", [1, EC], F32, "ExternalInput")
    ln_w = dram("ln_w", [1, H], F32, "ExternalInput")
    ln_b = dram("ln_b", [1, H], F32, "ExternalInput")
    conv_w = dram("conv_w", [EC, KC], F32, "ExternalInput")
    conv_b = dram("conv_b", [EC, 1], F32, "ExternalInput")
    xproj_w = dram("xproj_w", [EC, 96], F32, "ExternalInput")
    dtproj_w = dram("dtproj_w", [RNK, EC], F32, "ExternalInput")
    dtproj_b = dram("dtproj_b", [EC, 1], F32, "ExternalInput")
    a_log = dram("a_log", [EC, NST], F32, "ExternalInput")
    d_in = dram("d_in", [EC, 1], F32, "ExternalInput")

    xbf_loc = dram("xbf_loc", [p.TC, H], BF16)
    xbf = dram("xbf", [p.T, H], BF16)
    st_loc = dram("st_loc", [2, p.TC], F32)
    st_all = dram("st_all", [2 * NCOR, p.TC], F32)
    dbc_par = dram("dbc_par", [NSL, 96, TS], F32)
    dbc_bf = dram("dbc_bf", [2, 32, TS], BF16)
    x3_sp = dram("x3_sp", [p.TC, H], BF16)
    dbc_sum = dram("dbc_sum", [NSL, 96, TS], F32)
    a2a_in = dram("a2a_in", [NSL, EC, TS], BF16)
    a2a_out = dram("a2a_out", [E, p.TC], BF16)

    wo_shard = dram("wo_shard", [EC, H], F32, "ExternalInput")
    wob_in = dram("wob_in", [1, H], F32, "ExternalInput")
    rw1_shard = dram("rw1_shard", [128, 4096], F32, "ExternalInput")
    rw2_in = dram("rw2_in", [4096, 8], F32, "ExternalInput")
    rb1_in = dram("rb1_in", [4096, 1], F32, "ExternalInput")
    rb2_in = dram("rb2_in", [1, 8], F32, "ExternalInput")
    myw1 = dram("myw1", [H, 4096], F32, "ExternalInput")
    myw2 = dram("myw2", [4096, H], F32, "ExternalInput")
    myb1 = dram("myb1", [4096, 1], F32, "ExternalInput")
    myb2 = dram("myb2", [H, 1], F32, "ExternalInput")
    pfxmask = dram("pfxmask", [8, 1], F32, "ExternalInput")

    NSLOT = NEXP * CAP
    myw1_bf = dram("myw1_bf", [H, 4096], BF16)
    myw2_bf = dram("myw2_bf", [4096, H], BF16)
    wo_loc = dram("wo_loc", [EC, H], BF16)
    wo_bf = dram("wo_bf", [E, H], BF16)
    rw1_loc = dram("rw1_loc", [128, 4096], BF16)
    rw1_bf = dram("rw1_bf", [H, 4096], BF16)
    x1_sp = dram("x1_sp", [p.TC, H], F32)
    stg = dram("stg", [NSLOT + 128, H], BF16)
    rs_out = dram("rs_out", [CAP, H], BF16)
    eo_loc = dram("eo_loc", [CAP, H], BF16)
    eo_g = dram("eo_g", [NSLOT, H], BF16)
    cnt_loc = dram("cnt_loc", [1, 8], F32)
    cnt_all = dram("cnt_all", [NCOR, 8], F32)
    o_out = dram("o_out", [p.TC, H], F32, "ExternalOutput")

    outs = {}
    outs["o_out"] = o_out
    if debug:
        outs["o_xscan"] = dram("o_xscan", [EC, p.T], BF16, "ExternalOutput")
        outs["o_dt"] = dram("o_dt", [EC, p.T], BF16, "ExternalOutput")
        outs["o_dbc"] = dram("o_dbc", [96, p.T], F32, "ExternalOutput")
        outs["o_yg"] = dram("o_yg", [E, p.TC], BF16, "ExternalOutput")
        outs["o_x1"] = dram("o_x1", [p.TC, H], F32, "ExternalOutput")
        outs["o_lg"] = dram("o_lg", [p.TC, 8], F32, "ExternalOutput")
        outs["o_top"] = dram("o_top", [p.TC, 8], F32, "ExternalOutput")

    c_ones1 = nc.inline_tensor(np.ones((1, 128), np.float32), "c_ones1")
    c_onesr = nc.inline_tensor(np.ones((1, NCK), np.float32), "c_onesr")
    c_idf = nc.inline_tensor(np.eye(128, dtype=np.float32), "c_idf")
    c_trib = nc.inline_tensor(np.triu(np.ones((128, 128))).astype(BF), "c_trib")
    c_idb = nc.inline_tensor(np.eye(128).astype(BF), "c_idb")
    c_onescol = nc.inline_tensor(np.ones((128, 1)).astype(BF), "c_onescol")
    c_onesrow = nc.inline_tensor(np.ones((1, TS)).astype(BF), "c_onesrow")

    sbG = mk_sb(ctx)
    psb = lambda n, s: ctx.enter_context(nc.psum_tensor(n, s, F32))
    s_mu = sbG("s_mu", [128, NTT])
    s_r1 = sbG("s_r1", [128, NTT])
    s_ones1 = sbG("s_ones1", [1, 128])
    s_onesr = sbG("s_onesr", [1, NCK])
    s_idf = sbG("s_idf", [128, 128])
    s_trib = sbG("s_trib", [128, 128], BF16)
    s_idb = sbG("s_idb", [128, 128], BF16)
    s_onec = sbG("s_onec", [128, 1], BF16)
    s_wobr = sbG("s_wobr", [1, H])
    s_pfm = sbG("s_pfm", [8, 1])
    s_rw2b = sbG("s_rw2b", [128, 32, 8], BF16)
    s_rb1c = sbG("s_rb1c", [128, 32])
    s_rb2r = sbG("s_rb2r", [1, 8])

    ps_m = [psb("ps_m0", [128, 512]), psb("ps_m1", [128, 512])]
    ps_s = psb("ps_s", [128, 512])

    mctx = contextlib.ExitStack()
    sbM = mk_sb(mctx)
    psM = lambda n, s, d=F32: mctx.enter_context(nc.psum_tensor(n, s, d))
    s_wxb = sbM("s_wxb", [128, HK, EC], BF16)
    s_wgb = sbM("s_wgb", [128, HK, EC], BF16)
    s_lnwc = sbM("s_lnwc", [128, HK])
    s_lnbc = sbM("s_lnbc", [128, HK])
    s_s1x = sbM("s_s1x", [1, EC])
    s_s2x = sbM("s_s2x", [1, EC])
    s_s1g = sbM("s_s1g", [1, EC])
    s_s2g = sbM("s_s2g", [1, EC])
    s_s12x = sbM("s_s12x", [2, EC], BF16)
    s_s12g = sbM("s_s12g", [2, EC], BF16)
    s_bxr = sbM("s_bxr", [1, EC])
    s_bgr = sbM("s_bgr", [1, EC])
    s_cw = sbM("s_cw", [128, NET, KC])
    s_cbc = sbM("s_cbc", [128, NET])
    s_dpb = sbM("s_dpb", [128, NET])
    s_A = sbM("s_A", [128, NET, NST])
    s_D = sbM("s_D", [128, NET])
    s_xpw = sbM("s_xpw", [128, NET, 96], BF16)
    s_dpw = sbM("s_dpw", [64, EC], BF16)
    s_hst = sbM("s_hst", [128, NET, NST])
    s_halo = sbM("s_halo", [128, NET, KC - 1])
    ps_db = psM("ps_db", [96, 512])
    block = ctx.enter_context(nc.Block())
    A = Auto(nc, block, ctx)
    E_ = A.emit

    pctx = contextlib.ExitStack()
    sbP = mk_sb(pctx)
    s_xr = sbP("s_xr", [128, H])
    s_xrb = sbP("s_xrb", [128, H], BF16)
    s_scr = sbP("s_scr", [128, H])
    s_sq = sbP("s_sq", [128, NTT])
    s_var = sbP("s_var", [128, NTT])
    s_t0 = sbP("s_t0", [128, NTT])
    s_ac = sbP("s_ac", [128, 2 * NTT])
    s_acT = sbP("s_acT", [2 * NTT, 128])
    s_wf = sbP("s_wf", [128, HK, EC])

    @A.on("sync")
    def _():
        for dst, key, src in [
            (s_wf[:], "s_wf", w_ip_x.ap().rearrange("(a p) m -> p a m", p=128)),
            (s_lnwc[:], "s_lnwc", ln_w.ap().rearrange("o (a p) -> p (o a)", p=128)),
            (s_lnbc[:], "s_lnbc", ln_b.ap().rearrange("o (a p) -> p (o a)", p=128)),
            (s_bxr[:], "s_bxr", b_ip_x.ap()),
            (s_bgr[:], "s_bgr", b_ip_g.ap()),
            (s_cw[:], "s_cw", conv_w.ap().rearrange("(a p) k -> p a k", p=128)),
            (s_cbc[:], "s_cbc", conv_b.ap().rearrange("(a p) o -> p (a o)", p=128)),
            (s_dpb[:], "s_dpb", dtproj_b.ap().rearrange("(a p) o -> p (a o)", p=128)),
            (s_A[:], "s_A", a_log.ap().rearrange("(a p) n -> p a n", p=128)),
            (s_D[:], "s_D", d_in.ap().rearrange("(a p) o -> p (a o)", p=128)),
            (s_ones1[:], "s_ones1", c_ones1.ap()),
            (s_onesr[:], "s_onesr", c_onesr.ap()),
            (s_idf[:], "s_idf", c_idf.ap()),
            (s_trib[:], "s_trib", c_trib.ap()),
            (s_idb[:], "s_idb", c_idb.ap()),
            (s_onec[:], "s_onec", c_onescol.ap()),
            (s_wobr[:], "s_wobr", wob_in.ap()),
            (s_pfm[:], "s_pfm", pfxmask.ap()),
            (s_rb1c[:], "s_rb1c", rb1_in.ap().rearrange("(a p) o -> p (a o)", p=128)),
            (s_rb2r[:], "s_rb2r", rb2_in.ap()),
        ][:0]:
            pass
        items = [
            (s_wf[:], "s_wf", w_ip_x.ap().rearrange("(a p) m -> p a m", p=128)),
            (s_lnwc[:], "s_lnwc", ln_w.ap().rearrange("o (a p) -> p (o a)", p=128)),
            (s_lnbc[:], "s_lnbc", ln_b.ap().rearrange("o (a p) -> p (o a)", p=128)),
            (s_bxr[:], "s_bxr", b_ip_x.ap()),
            (s_bgr[:], "s_bgr", b_ip_g.ap()),
            (s_cw[:], "s_cw", conv_w.ap().rearrange("(a p) k -> p a k", p=128)),
            (s_cbc[:], "s_cbc", conv_b.ap().rearrange("(a p) o -> p (a o)", p=128)),
            (s_dpb[:], "s_dpb", dtproj_b.ap().rearrange("(a p) o -> p (a o)", p=128)),
            (s_A[:], "s_A", a_log.ap().rearrange("(a p) n -> p a n", p=128)),
            (s_D[:], "s_D", d_in.ap().rearrange("(a p) o -> p (a o)", p=128)),
            (s_ones1[:], "s_ones1", c_ones1.ap()),
            (s_onesr[:], "s_onesr", c_onesr.ap()),
            (s_idf[:], "s_idf", c_idf.ap()),
            (s_trib[:], "s_trib", c_trib.ap()),
            (s_idb[:], "s_idb", c_idb.ap()),
            (s_onec[:], "s_onec", c_onescol.ap()),
            (s_wobr[:], "s_wobr", wob_in.ap()),
            (s_pfm[:], "s_pfm", pfxmask.ap()),
            (s_rb1c[:], "s_rb1c", rb1_in.ap().rearrange("(a p) o -> p (a o)", p=128)),
            (s_rb2r[:], "s_rb2r", rb2_in.ap()),
        ]
        def ldall(e):
            return [e.dma_start(out=d, in_=s2, allow_slow_non_contiguous=True)
                    for d, _k, s2 in items]
        E_(ldall, writes=[k for _d, k, _s in items], dma=True, dkey="ldc")

    @A.on("gpsimd")
    def _():
        def ldg(e):
            return [
                e.dma_start(out=s_dpw[:], in_=dtproj_w.ap()),
                e.dma_start(out=s_xpw[:], in_=xproj_w.ap().rearrange("(a p) m -> p a m", p=128)),
                e.dma_start(out=s_rw2b[:], in_=rw2_in.ap().rearrange("(a p) m -> p a m", p=128)),
                e.dma_start(out=wo_loc.ap(), in_=wo_shard.ap()),
                e.dma_start(out=rw1_loc.ap(), in_=rw1_shard.ap()),
                e.dma_start(out=myw1_bf.ap(), in_=myw1.ap()),
                e.dma_start(out=myw2_bf.ap(), in_=myw2.ap()),
            ]
        E_(ldg, writes=["s_dpw", "s_xpw", "s_rw2b", "wo_loc", "rw1_loc",
                        "myw1_bf", "myw2_bf"], dma=True, dkey="ldg")

    @A.on("scalar")
    def _():
        for m in range(NET):
            E_(lambda e, m=m: e.activation(s_A[:, m, :], s_A[:, m, :], ACT.Exp),
               reads=["s_A"], writes=["s_A"])
    @A.on("vector")
    def _():
        E_(lambda e: e.tensor_scalar_mul(s_A[:], s_A[:], -1.0),
           reads=["s_A"], writes=["s_A"])

    for i in range(NTT):
        @A.on("sync")
        def _(i=i):
            E_(lambda e: e.dma_start(out=s_xr[:, :],
                                     in_=x_rows.ap()[128 * i:128 * (i + 1), :]),
               writes=["s_xr"], dma=True)
        @A.on("vector")
        def _(i=i):
            E_(lambda e: e.tensor_copy(s_xrb[:, :], s_xr[:, :]),
               reads=["s_xr"], writes=["s_xrb"])
            E_(lambda e: e.reduce_sum(s_mu[:, i:i + 1], s_xr[:, :], axis=AX.X),
               reads=["s_xr"], writes=["s_mu"])
        @A.on("scalar")
        def _(i=i):
            E_(lambda e: e.activation(s_scr[:], s_xr[:, :], ACT.Square,
                                      accum_out=s_sq[:, i:i + 1]),
               reads=["s_xr"], writes=["s_scr", "s_sq"])
        @A.on("sync")
        def _(i=i):
            E_(lambda e: e.dma_start(
                out=xbf_loc.ap()[128 * i:128 * (i + 1), :], in_=s_xrb[:, :]),
               reads=["s_xrb"], writes=[f"xbf_loc{(128 * i) * CH4 // p.TC}"],
               dma=True, dkey="xbl")
    @A.on("vector")
    def _():
        E_(lambda e: e.tensor_scalar_mul(s_mu[:], s_mu[:], 1.0 / H),
           reads=["s_mu"], writes=["s_mu"])
        E_(lambda e: e.tensor_scalar_mul(s_sq[:], s_sq[:], 1.0 / H),
           reads=["s_sq"], writes=["s_sq"])
        E_(lambda e: e.tensor_mul(s_t0[:], s_mu[:], s_mu[:]),
           reads=["s_mu"], writes=["s_t0"])
        E_(lambda e: e.tensor_sub(s_var[:], s_sq[:], s_t0[:]),
           reads=["s_sq", "s_t0"], writes=["s_var"])
        E_(lambda e: e.tensor_scalar_add(s_t0[:], s_var[:], EPS),
           reads=["s_var"], writes=["s_t0"])
    @A.on("scalar")
    def _():
        E_(lambda e: e.activation(s_r1[:], s_t0[:], ACT.Ln),
           reads=["s_t0"], writes=["s_r1"])
        E_(lambda e: e.activation(s_r1[:], s_r1[:], ACT.Exp, scale=-0.5),
           reads=["s_r1"], writes=["s_r1"])
    @A.on("vector")
    def _():
        E_(lambda e: e.tensor_mul(s_t0[:], s_var[:], s_r1[:]),
           reads=["s_var", "s_r1"], writes=["s_t0"])
        E_(lambda e: e.tensor_mul(s_t0[:], s_t0[:], s_r1[:]),
           reads=["s_t0", "s_r1"], writes=["s_t0"])
        E_(lambda e: e.tensor_scalar_add(s_t0[:], s_t0[:], EPS),
           reads=["s_t0"], writes=["s_t0"])
    @A.on("scalar")
    def _():
        E_(lambda e: e.activation(s_t0[:], s_t0[:], ACT.Ln),
           reads=["s_t0"], writes=["s_t0"])
        E_(lambda e: e.activation(s_t0[:], s_t0[:], ACT.Exp, scale=-0.5),
           reads=["s_t0"], writes=["s_t0"])
    @A.on("vector")
    def _():
        E_(lambda e: e.tensor_mul(s_ac[:, 0:NTT], s_r1[:], s_t0[:]),
           reads=["s_r1", "s_t0"], writes=["s_ac"])
        E_(lambda e: e.tensor_mul(s_ac[:, NTT:2 * NTT], s_ac[:, 0:NTT], s_mu[:]),
           reads=["s_ac", "s_mu"], writes=["s_ac"])
        E_(lambda e: e.tensor_scalar_mul(s_ac[:, NTT:2 * NTT], s_ac[:, NTT:2 * NTT], -1.0),
           reads=["s_ac"], writes=["s_ac"])
    @A.on("tensor")
    def _():
        E_(lambda e: e.transpose(ps_s[0:2 * NTT, 0:128], s_ac[:, 0:2 * NTT], s_idf[:]),
           reads=["s_ac", "s_idf"], writes=["ps_s"])
    @A.on("scalar")
    def _():
        E_(lambda e: e.copy(s_acT[:, :], ps_s[0:2 * NTT, 0:128]),
           reads=["ps_s"], writes=["s_acT"])
    @A.on("sync")
    def _():
        E_(lambda e: e.dma_start(out=st_loc.ap().rearrange("j (i q) -> (j i) q", q=128),
                                 in_=s_acT[:, :]),
           reads=["s_acT"], writes=["st_loc"], dma=True)
    @A.on("gpsimd")
    def _():
        for j4 in range(CH4):
            E_(lambda e, j4=j4: e.collective_compute(
                "AllGather", ALU.bypass, replica_groups=RG,
                ins=[xbf_loc.ap()[CSZ * j4:CSZ * (j4 + 1), :]],
                outs=[xbf.ap()[NCOR * CSZ * j4:NCOR * CSZ * (j4 + 1), :]]),
               reads=[f"xbf_loc{j4}"], writes=[f"xbf{j4}"], cc=True, dkey=f"agx{j4}")
            if j4 == 0:
                E_(lambda e: e.collective_compute(
                    "AllGather", ALU.bypass, replica_groups=RG,
                    ins=[st_loc.ap()], outs=[st_all.ap()]),
                   reads=["st_loc"], writes=["st_all"], cc=True)
        E_(lambda e: e.collective_compute("AllGather", ALU.bypass, replica_groups=RG,
                                          ins=[wo_loc.ap()], outs=[wo_bf.ap()]),
           reads=["wo_loc"], writes=["wo_bf"], cc=True)
        E_(lambda e: e.collective_compute("AllGather", ALU.bypass, replica_groups=RG,
                                          ins=[rw1_loc.ap()], outs=[rw1_bf.ap()]),
           reads=["rw1_loc"], writes=["rw1_bf"], cc=True)

    NZB = 16
    s_zz = sbP("s_zz", [128, NZB, H], BF16)
    @A.on("vector")
    def _():
        E_(lambda e: e.memset(s_zz[:, :, :], 0.0), writes=["s_zz"])
    @A.on("sync")
    def _():
        nrows = (NSLOT + 128) // 128
        j = 0
        while j < nrows:
            nb = min(NZB, nrows - j)
            E_(lambda e, j=j, nb=nb: e.dma_start(
                out=stg.ap()[128 * j:128 * (j + nb), :].rearrange("(a p) m -> p a m", p=128),
                in_=s_zz[:, 0:nb, :]),
               reads=["s_zz"], writes=["stg"], dma=True, dkey="stg_z")
            j += nb

    def fold_w(tag, s_out, obuf, s1_row, s2_row, b_row, bkey, w_dram):
        if tag == "g":
            @A.on("sync")
            def _():
                E_(lambda e: e.dma_start(out=s_wf[:],
                                         in_=w_dram.ap().rearrange("(a p) m -> p a m", p=128)),
                   writes=["s_wf"], dma=True)
        @A.on("tensor")
        def _():
            def mm_rows(e):
                for ki in range(HK):
                    e.matmul(ps_s[0:1, 0:EC], s_lnwc[:, ki:ki + 1], s_wf[:, ki, :],
                             start=(ki == 0), stop=(ki == HK - 1))
                for ki in range(HK):
                    last = e.matmul(ps_s[0:1, EC:2 * EC], s_lnbc[:, ki:ki + 1],
                                    s_wf[:, ki, :],
                                    start=(ki == 0), stop=(ki == HK - 1))
                return last
            E_(mm_rows, reads=["s_lnwc", "s_lnbc", "s_wf"], writes=["ps_s"])
        @A.on("scalar")
        def _():
            E_(lambda e: e.copy(s1_row[:], ps_s[0:1, 0:EC]),
               reads=["ps_s"], writes=[f"s1{tag}"])
            E_(lambda e: e.copy(s2_row[:], ps_s[0:1, EC:2 * EC]),
               reads=["ps_s"], writes=[f"s2{tag}"])
        @A.on("vector")
        def _():
            E_(lambda e: e.tensor_add(s2_row[:], s2_row[:], b_row[:]),
               reads=[f"s2{tag}", bkey], writes=[f"s2{tag}"])
        s12 = s_s12x if tag == "x" else s_s12g
        @A.on("gpsimd")
        def _():
            E_(lambda e: e.dma_start(out=s12[0:1, :], in_=s1_row[:]),
               reads=[f"s1{tag}"], writes=[f"s_s12{tag}"], dma=True)
            E_(lambda e: e.dma_start(out=s12[1:2, :], in_=s2_row[:]),
               reads=[f"s2{tag}"], writes=[f"s_s12{tag}"], dma=True)
            for ki in range(HK):
                E_(lambda e, ki=ki: e.tensor_scalar(s_out[:, ki, :], s_wf[:, ki, :],
                                                    s_lnwc[:, ki:ki + 1], None, ALU.mult),
                   reads=["s_wf", "s_lnwc"], writes=[obuf])

    fold_w("x", s_wxb, "s_wxb", s_s1x, s_s2x, s_bxr, "s_bxr", w_ip_x)
    fold_w("g", s_wgb, "s_wgb", s_s1g, s_s2g, s_bgr, "s_bgr", w_ip_g)

    A.fence()
    pctx.close()

    dctx = contextlib.ExitStack()
    sbD = mk_sb(dctx)
    s_xT = sbD("s_xT", [128, HK, TS], BF16)
    s_u0 = sbD("s_u0", [128, TS + KC - 1])
    s_cvt = sbD("s_cvt", [128, TS])
    s_xsb = sbD("s_xsb", [128, 2, NET, TS], BF16)
    s_dt = sbD("s_dt", [128, 2, NET, TS], BF16)
    s_dtx = sbD("s_dtx", [128, NET, TS], BF16)
    s_gs = sbD("s_gs", [128, 2, NET, TS], BF16)
    s_ab = sbD("s_ab", [128, TS], BF16)
    s_db64 = sbD("s_db64", [64, TS], BF16)
    s_h = sbD("s_h", [128, TS], BF16)
    s_yac = sbD("s_yac", [128, NET, TS], BF16)
    s_pb = sbD("s_pb", [128, TS], BF16)
    s_cb = sbD("s_cb", [128, 2, TS], BF16)
    s_co = sbD("s_co", [2, TS], BF16)
    s_dA2 = sbD("s_dA2", [128, TS])
    s_bb = sbD("s_bb", [128, 2, TS], BF16)
    s_ub = sbD("s_ub", [128, TS], BF16)
    s_dts = sbD("s_dts", [128, NCK], BF16)
    s_dA = s_u0
    s_u = s_cvt
    s_dbst = s_cvt

    @A.on("sync")
    def _():
        E_(lambda e: e.dma_start(out=s_co[1:2, :], in_=c_onesrow.ap()),
           writes=["s_co1"], dma=True)


    def chs(ch):
        return slice(ch * NCK, (ch + 1) * NCK)

    def load_xT(s):
        @A.on("sync")
        def _():
            for ch in range(NCH):
                tok0 = ch * NCK
                j4 = tok0 // CSZ
                off = tok0 % CSZ
                row0 = j4 * NCOR * CSZ + s * CSZ + off
                for ki in range(HK):
                    E_(lambda e, ki=ki, ch=ch, row0=row0: e.dma_start(
                        out=s_xT[:, ki, chs(ch)],
                        in_=xbf.ap()[row0:row0 + NCK, 128 * ki:128 * (ki + 1)],
                        transpose=True),
                       reads=[f"xbf{j4}"], writes=["s_xT"], dma=True)

    def emit_abcast(s):
        @A.on("gpsimd")
        def _():
            E_(lambda e, s=s: e.dma_start(
                out=s_ab[:, :],
                in_=st_all.ap()[2 * s:2 * s + 1, :].to_broadcast([128, TS])),
               reads=["st_all"], writes=["s_ab"], dma=True)
            E_(lambda e, s=s: e.dma_start(
                out=s_co[0:1, :], in_=st_all.ap()[2 * s + 1:2 * s + 2, :]),
               reads=["st_all"], writes=["s_co0"], dma=True)
    def emit_conv(s, m):
        par = s % 2
        @A.on("vector")
        def _():
            if s % p.SLB == 0:
                E_(lambda e: e.memset(s_u0[:, 0:KC - 1], 0.0), writes=["s_u0"])
            else:
                E_(lambda e, m=m: e.tensor_copy(s_u0[:, 0:KC - 1], s_halo[:, m, :]),
                   reads=[f"s_halo_{m}"], writes=["s_u0"])
            E_(lambda e, m=m: e.tensor_copy(s_halo[:, m, :], s_u0[:, TS:TS + KC - 1]),
               reads=["s_u0"], writes=[f"s_halo_{m}"])
            E_(lambda e, m=m: e.tensor_scalar(s_cvt[:, :], s_u0[:, 0:TS],
                                              s_cw[:, m, 0:1], None, ALU.mult),
               reads=["s_u0", "s_cw"], writes=["s_cvt"])
            for k in range(1, KC):
                E_(lambda e, k=k, m=m: e.scalar_tensor_tensor(
                    s_cvt[:, :], s_u0[:, k:k + TS], s_cw[:, m, k:k + 1],
                    s_cvt[:, :], ALU.mult, ALU.add),
                   reads=["s_u0", "s_cw", "s_cvt"], writes=["s_cvt"])
        @A.on("scalar")
        def _():
            E_(lambda e, m=m, par=par: e.activation(
                s_xsb[:, par, m, :], s_cvt[:, :], ACT.Silu, bias=s_cbc[:, m:m + 1]),
               reads=["s_cvt", "s_cbc"], writes=[f"s_xsb{par}"])

    def emit_proj(s, kind):
        par = s % 2
        wb, wkey = (s_wxb, "s_wxb") if kind == "x" else (s_wgb, "s_wgb")
        s12, s12k = (s_s12x, "s_s12x") if kind == "x" else (s_s12g, "s_s12g")
        for m in range(NET):
            for ch in range(NCH):
                bank = (m * NCH + ch) % 2
                pm, pk = ps_m[bank], f"ps_m{bank}"
                @A.on("tensor")
                def _(m=m, ch=ch, pm=pm, pk=pk):
                    def mms(e):
                        for ki in range(HK):
                            e.matmul(pm[:, 0:NCK], wb[:, ki, 128 * m:128 * (m + 1)],
                                     s_xT[:, ki, chs(ch)], start=(ki == 0), stop=False)
                        return e.matmul(pm[:, 0:NCK], s12[0:2, 128 * m:128 * (m + 1)],
                                        s_co[0:2, chs(ch)], start=False, stop=True)
                    E_(mms, reads=[wkey, "s_xT", s12k, "s_co0", "s_co1"],
                       writes=[pk])
                @A.on("vector")
                def _(m=m, ch=ch, pm=pm, pk=pk):
                    if kind == "x":
                        E_(lambda e: e.tensor_tensor(
                            s_u0[:, KC - 1 + ch * NCK: KC - 1 + (ch + 1) * NCK],
                            pm[:, 0:NCK], s_ab[:, chs(ch)], ALU.mult),
                           reads=[pk, "s_ab"], writes=["s_u0"])
                    else:
                        E_(lambda e: e.tensor_tensor(
                            s_cvt[:, chs(ch)], pm[:, 0:NCK], s_ab[:, chs(ch)], ALU.mult),
                           reads=[pk, "s_ab"], writes=["s_cvt"])
            if kind == "x":
                emit_conv(s, m)
            else:
                @A.on("scalar")
                def _(m=m, par=par):
                    E_(lambda e: e.activation(
                        s_gs[:, par, m, :], s_cvt[:, :], ACT.Sigmoid),
                       reads=["s_cvt"], writes=[f"s_gs{par}"])

    def emit_dbc(s):
        par = s % 2
        for ch in range(NCH):
            @A.on("tensor")
            def _(ch=ch):
                def mms(e):
                    e.matmul(ps_db[:, 0:NCK], s_xpw[:, 0, :],
                             s_xsb[:, par, 0, chs(ch)], start=True, stop=(NET == 1))
                    if NET > 1:
                        return e.matmul(ps_db[:, 0:NCK], s_xpw[:, 1, :],
                                        s_xsb[:, par, 1, chs(ch)],
                                        start=False, stop=True)
                E_(mms, reads=["s_xpw", f"s_xsb{par}"], writes=["ps_db"])
            @A.on("scalar")
            def _(ch=ch):
                E_(lambda e: e.copy(s_dbst[0:96, chs(ch)], ps_db[:, 0:NCK]),
                   reads=["ps_db"], writes=["s_cvt"])
        @A.on("sync")
        def _():
            E_(lambda e, s=s: e.dma_start(out=dbc_par.ap()[s], in_=s_dbst[0:96, :]),
               reads=["s_cvt"], writes=["dbc_par"], dma=True)
        @A.on("gpsimd")
        def _():
            E_(lambda e, s=s: e.collective_compute(
                "AllReduce", ALU.add, replica_groups=RG,
                ins=[dbc_par.ap()[s]], outs=[dbc_sum.ap()[s]]),
               reads=["dbc_par"], writes=["dbc_sum"], cc=True)
        if debug:
            @A.on("sync")
            def _():
                E_(lambda e, s=s: e.dma_start(out=outs["o_dbc"].ap()[:, s * TS:(s + 1) * TS],
                                              in_=dbc_sum.ap()[s]),
                   reads=["dbc_sum"], writes=["o_dbc"], dma=True)

    def emit_M(s):
        load_xT(s)
        emit_abcast(s)
        emit_proj(s, "x")
        emit_proj(s, "g")
        emit_dbc(s)
        if debug:
            @A.on("sync")
            def _():
                E_(lambda e, s=s: e.dma_start(
                    out=outs["o_xscan"].ap().rearrange("(a p) t -> p a t", p=128)[:, :, s * TS:(s + 1) * TS],
                    in_=s_xsb[:, s % 2, :, :]),
                   reads=[f"s_xsb{s % 2}"], writes=["o_xscan"], dma=True)

    def dt_pieces(s):
        par = s % 2
        pieces = []
        def head():
            @A.on("gpsimd")
            def _():
                E_(lambda e, s=s: e.dma_start(out=s_db64[:, :], in_=dbc_sum.ap()[s][0:64, :]),
                   reads=["dbc_sum"], writes=["s_db64"], dma=True)
                E_(lambda e, s=s: e.dma_start(out=dbc_bf.ap()[s % 2],
                                              in_=dbc_sum.ap()[s][64:96, :]),
                   reads=["dbc_sum"], writes=[f"dbc_bf{s % 2}"], dma=True)
        pieces.append(head)
        def chunk(m, ch):
            bank = (m * NCH + ch) % 2
            pm, pk = ps_m[bank], f"ps_m{bank}"
            @A.on("tensor")
            def _(m=m, ch=ch, pm=pm, pk=pk):
                E_(lambda e: e.matmul(
                    pm[:, 0:NCK], s_dpw[:, 128 * m:128 * (m + 1)],
                    s_db64[:, chs(ch)], start=True, stop=True),
                   reads=["s_dpw", "s_db64"], writes=[pk])
            @A.on("scalar")
            def _(m=m, ch=ch, pm=pm, pk=pk):
                E_(lambda e: e.activation(
                    s_u[:, 0:NCK], pm[:, 0:NCK], ACT.Abs, bias=s_dpb[:, m:m + 1]),
                   reads=[pk, "s_dpb"], writes=["s_cvt"])
                E_(lambda e: e.activation(s_u[:, 0:NCK], s_u[:, 0:NCK], ACT.Exp,
                                          scale=-1.0),
                   reads=["s_cvt"], writes=["s_cvt"])
                E_(lambda e: e.activation(s_u[:, 0:NCK], s_u[:, 0:NCK], ACT.Ln,
                                          bias=1.0),
                   reads=["s_cvt"], writes=["s_cvt"])
                E_(lambda e: e.activation(
                    s_dts[:, :], pm[:, 0:NCK], ACT.Relu, bias=s_dpb[:, m:m + 1]),
                   reads=[pk, "s_dpb"], writes=["s_dts"])
            @A.on("vector")
            def _(m=m, ch=ch, par=par):
                v2 = lambda ap: ap.rearrange("p (a b) -> p a b", a=2)
                E_(lambda e: e.tensor_tensor(
                    v2(s_dt[:, par, m, chs(ch)]), v2(s_u[:, 0:NCK]),
                    v2(s_dts[:, :]), ALU.add),
                   reads=["s_cvt", "s_dts"], writes=[f"s_dt{par}"])
        for m in range(NET):
            for ch in range(NCH):
                pieces.append(lambda m=m, ch=ch: chunk(m, ch))
        return pieces

    def emit_dt_debug(s):
        par = s % 2
        if debug:
            @A.on("sync")
            def _():
                E_(lambda e, s=s: e.dma_start(
                    out=outs["o_dt"].ap().rearrange("(a p) t -> p a t", p=128)[:, :, s * TS:(s + 1) * TS],
                    in_=s_dt[:, par, :, :]),
                   reads=[f"s_dt{par}"], writes=["o_dt"], dma=True)

    def emit_scan(s, inject=()):
        par = s % 2
        inject = list(inject)
        @A.on("vector")
        def _():
            v2 = lambda ap: ap.rearrange("p (a b) -> p a b", a=2)
            for m in range(NET):
                E_(lambda e, m=m: e.tensor_tensor(
                    v2(s_dtx[:, m, :]), v2(s_dt[:, par, m, :]),
                    v2(s_xsb[:, par, m, :]), ALU.mult),
                   reads=[f"s_dt{par}", f"s_xsb{par}"], writes=["s_dtx"])
            if s % p.SLB == 0:
                E_(lambda e: e.memset(s_hst[:], 0.0), writes=["s_hst"])
        for n in range(NST):
            if NST - len(inject) <= n:
                inject.pop(0)()
            npar = n % 2
            @A.on("sync")
            def _(n=n, npar=npar):
                E_(lambda e: e.dma_start(
                    out=s_bb[:, npar, :],
                    in_=dbc_bf.ap()[par, n:n + 1, :].to_broadcast([128, TS])),
                   reads=[f"dbc_bf{par}"], writes=[f"s_bb{npar}"], dma=True)
            @A.on("scalar")
            def _(n=n, npar=npar):
                E_(lambda e: e.dma_start(
                    out=s_cb[:, npar, :],
                    in_=dbc_bf.ap()[par, 16 + n:17 + n, :].to_broadcast([128, TS])),
                   reads=[f"dbc_bf{par}"], writes=[f"s_cb{npar}"], dma=True)
            dabs = [(s_dA[:, 0:TS], "s_u0"), (s_dA2[:, :], "s_dA2")][:NET]
            @A.on("scalar")
            def _(n=n, dabs=dabs):
                def exps(e):
                    r = None
                    for m, (dAb, dAk) in enumerate(dabs):
                        r = e.activation(dAb, s_dt[:, par, m, :], ACT.Exp,
                                         scale=s_A[:, m, n:n + 1])
                    return r
                E_(exps, reads=[f"s_dt{par}", "s_A"],
                   writes=[k for _, k in dabs])
            for m in range(NET):
                dAb, dAk = dabs[m]
                v2 = lambda ap: ap.rearrange("p (a b) -> p a b", a=2)
                @A.on("vector")
                def _(n=n, m=m, dAb=dAb, dAk=dAk, npar=npar, v2=v2):
                    E_(lambda e: e.tensor_tensor(
                        v2(s_ub[:, :]), v2(s_dtx[:, m, :]), v2(s_bb[:, npar, :]),
                        ALU.mult),
                       reads=["s_dtx", f"s_bb{npar}"], writes=["s_ub"])
                    E_(lambda e: e.tensor_tensor_scan(
                        s_h[:, :], dAb, s_ub[:, :], s_hst[:, m, n:n + 1],
                        ALU.mult, ALU.add),
                       reads=[dAk, "s_ub", "s_hst"], writes=["s_h"])
                v2 = lambda ap: ap.rearrange("p (a b) -> p a b", a=2)
                @A.on("vector")
                def _(n=n, m=m, npar=npar, v2=v2):
                    E_(lambda e: e.tensor_copy(s_hst[:, m, n:n + 1], s_h[:, TS - 1:TS]),
                       reads=["s_h"], writes=["s_hst"])
                    if n == 0:
                        E_(lambda e: e.tensor_tensor(
                            v2(s_yac[:, m, :]), v2(s_h[:, :]), v2(s_cb[:, npar, :]),
                            ALU.mult),
                           reads=["s_h", f"s_cb{npar}"], writes=[f"s_yac{m}"])
                    else:
                        E_(lambda e: e.tensor_tensor(
                            v2(s_pb[:, :]), v2(s_h[:, :]), v2(s_cb[:, npar, :]),
                            ALU.mult),
                           reads=["s_h", f"s_cb{npar}"], writes=["s_pb"])
                        E_(lambda e: e.tensor_tensor(
                            v2(s_yac[:, m, :]), v2(s_yac[:, m, :]), v2(s_pb[:, :]),
                            ALU.add),
                           reads=[f"s_yac{m}", "s_pb"], writes=[f"s_yac{m}"])

    def emit_ytail(s):
        par = s % 2
        @A.on("vector")
        def _():
            v2 = lambda ap: ap.rearrange("p (a b) -> p a b", a=2)
            for m in range(NET):
                E_(lambda e, m=m: e.scalar_tensor_tensor(
                    v2(s_yac[:, m, :]), v2(s_xsb[:, par, m, :]), s_D[:, m:m + 1],
                    v2(s_yac[:, m, :]), ALU.mult, ALU.add),
                   reads=[f"s_xsb{par}", "s_D", f"s_yac{m}"], writes=[f"s_yac{m}"])
                E_(lambda e, m=m: e.tensor_tensor(
                    v2(s_yac[:, m, :]), v2(s_yac[:, m, :]), v2(s_gs[:, par, m, :]),
                    ALU.mult),
                   reads=[f"s_yac{m}", f"s_gs{par}"], writes=[f"s_yac{m}"])
        @A.on("gpsimd")
        def _():
            E_(lambda e, s=s: e.dma_start(
                out=a2a_in.ap()[s].rearrange("(a p) t -> p a t", p=128),
                in_=s_yac[:, :, :]),
               reads=["s_yac0", "s_yac1"], writes=["a2a_in"], dma=True)

    emit_M(0)
    for piece in dt_pieces(0):
        piece()
    emit_dt_debug(0)
    for s in range(NSL):
        if s + 1 < NSL:
            emit_M(s + 1)
            emit_scan(s, inject=dt_pieces(s + 1))
            emit_dt_debug(s + 1)
        else:
            emit_scan(s)
        emit_ytail(s)

    @A.on("gpsimd")
    def _():
        with allow_ici_all_to_all():
            E_(lambda e: e.collective_compute(
                "AllToAll", ALU.bypass, replica_groups=RG,
                ins=[a2a_in.ap()], outs=[a2a_out.ap()]),
               reads=["a2a_in"], writes=["a2a_out"], cc=True)
    if debug:
        @A.on("sync")
        def _():
            E_(lambda e: e.dma_start(out=outs["o_yg"].ap(), in_=a2a_out.ap()),
               reads=["a2a_out"], writes=["o_yg"], dma=True)

    A.fence()
    dctx.close()
    mctx.close()

    NTT8 = NTT * 8
    fctx = contextlib.ExitStack()
    sbF = mk_sb(fctx)
    psF = lambda n, s, d=F32: fctx.enter_context(nc.psum_tensor(n, s, d))
    ps_f = psF("ps_f", [96, 512])
    ps_f2 = [psF("ps_f2_0", [128, 512]), psF("ps_f2_1", [128, 512])]
    ps_t = psF("ps_t", [128, 512], BF16)
    s_m3 = sbF("s_m3", [128, NTT])
    s_q3 = sbF("s_q3", [128, NTT])
    s_r3 = sbF("s_r3", [128, NTT])
    s_mgA = sbF("s_mgA", [128, NTT])
    s_mgB = sbF("s_mgB", [128, NTT])
    s_wA = sbF("s_wA", [128, NTT])
    s_wB = sbF("s_wB", [128, NTT])
    s_edt = sbF("s_edt", [128, NTT])
    s_siA = sbF("s_siA", [128, NTT], I32)
    s_siB = sbF("s_siB", [128, NTT], I32)
    s_giA = sbF("s_giA", [128, NTT], I32)
    s_giB = sbF("s_giB", [128, NTT], I32)
    s_kA = sbF("s_kA", [128, NTT])
    s_kB = sbF("s_kB", [128, NTT])
    s_tvA = sbF("s_tvA", [128, NTT])
    s_tvB = sbF("s_tvB", [128, NTT])
    s_g0 = sbF("s_g0", [128, 2, H], BF16)
    s_g1 = sbF("s_g1", [128, 2, H], BF16)
    s_ot = sbF("s_ot", [128, 2, H])
    s_wE = sbF("s_wE", [128, 2 * 4096], BF16)
    @A.on("sync")
    def _():
        for k in range(2):
            E_(lambda e, k=k: e.dma_start(
                out=s_wE[:, 4096 * k:4096 * (k + 1)],
                in_=myw1_bf.ap()[128 * k:128 * (k + 1), :]),
               reads=["myw1_bf"], writes=["s_wE"], dma=True)
    if debug:
        s_dbg = sbF("s_dbg", [128, NTT, 8])
    f2ctx = contextlib.ExitStack()
    sbF2 = mk_sb(f2ctx)
    s_io8 = sbF2("s_io8", [128, NTT, 8])
    s_mx1 = sbF2("s_mx1", [128, NTT])
    s_mx2 = sbF2("s_mx2", [128, NTT])
    s_i1 = sbF2("s_i1", [128, NTT])
    s_i2 = sbF2("s_i2", [128, NTT])
    s_ohA = sbF2("s_ohA", [128, NTT, 8])
    s_ohB = sbF2("s_ohB", [128, NTT, 8])
    s_ohAb = sbF2("s_ohAb", [128, NTT, 8], BF16)
    s_ohBb = sbF2("s_ohBb", [128, NTT, 8], BF16)
    s_sel = sbF2("s_sel", [128, NTT, 8])
    s_S = sbF2("s_S", [128, NTT, 8])
    s_pfA = sbF2("s_pfA", [128, NTT, 8])
    s_pfB = sbF2("s_pfB", [128, NTT, 8])
    s_off = sbF2("s_off", [1, 2 * NTT * 8])
    s_run = sbF2("s_run", [1, 16])
    s_offb = sbF2("s_offb", [128, 2 * NTT * 8])
    s_cnt = sbF2("s_cnt", [1, 8])
    s_ca = sbF2("s_ca", [8, 8])
    s_gb = sbF2("s_gb", [1, 8])
    s_gbr = sbF2("s_gbr", [1, NTT * 8])
    s_gbb = sbF2("s_gbb", [128, NTT * 8])
    fxctx = contextlib.ExitStack()
    s_x3b = mk_sb(fxctx)("s_x3b", [128, NTT, H], BF16)
    f0ctx = contextlib.ExitStack()
    sbF0 = mk_sb(f0ctx)
    s_yg2 = sbF0("s_yg2", [128, 16, p.TC], BF16)
    s_wo = sbF0("s_wo", [128, 16, H], BF16)
    s_xrf = sbF0("s_xrf", [128, H])
    s_x1t = sbF0("s_x1t", [128, H])
    s_zt = s_ot[:, 0, :]

    @A.on("sync")
    def _():
        for k in range(16):
            E_(lambda e, k=k: e.dma_start(out=s_yg2[:, k, :],
                                          in_=a2a_out.ap()[128 * k:128 * (k + 1), :]),
               reads=["a2a_out"], writes=["s_yg2"], dma=True)
        for k in range(16):
            E_(lambda e, k=k: e.dma_start(out=s_wo[:, k, :],
                                          in_=wo_bf.ap()[128 * k:128 * (k + 1), :]),
               reads=["wo_bf"], writes=["s_wo"], dma=True)

    for tt in range(NTT):
        @A.on("sync")
        def _(tt=tt):
            E_(lambda e: e.dma_start(
                out=s_xrf[:, :],
                in_=x_rows.ap()[128 * tt:128 * (tt + 1), :]),
               writes=["s_xrf"], dma=True)
        for dc in range(2):
            pm, pk = ps_m[dc % 2], f"ps_m{dc % 2}"
            @A.on("tensor")
            def _(tt=tt, dc=dc, pm=pm, pk=pk):
                def mms(e):
                    for k in range(16):
                        e.matmul(pm[:, 0:512], s_yg2[:, k, 128 * tt:128 * (tt + 1)],
                                 s_wo[:, k, 512 * dc:512 * (dc + 1)],
                                 start=(k == 0), stop=False)
                    return e.matmul(pm[:, 0:512], s_ones1[:, :],
                                    s_wobr[0:1, 512 * dc:512 * (dc + 1)],
                                    start=False, stop=True)
                E_(mms, reads=["s_yg2", "s_wo", "s_wobr", "s_ones1"], writes=[pk])
            @A.on("vector")
            def _(tt=tt, dc=dc, pm=pm, pk=pk):
                E_(lambda e: e.tensor_add(s_x1t[:, 512 * dc:512 * (dc + 1)],
                                          pm[:, 0:512], s_xrf[:, 512 * dc:512 * (dc + 1)]),
                   reads=[pk, "s_xrf"], writes=["s_x1t"])
        @A.on("vector")
        def _(tt=tt):
            E_(lambda e: e.tensor_scalar(s_zt[:, :], s_xrf[:, :],
                                         s_mu[:, tt:tt + 1], s_r1[:, tt:tt + 1],
                                         ALU.subtract, ALU.mult),
               reads=["s_xrf", "s_mu", "s_r1"], writes=["s_ot"])
            E_(lambda e: e.tensor_add(s_x1t[:, :], s_x1t[:, :], s_zt[:, :]),
               reads=["s_x1t", "s_ot"], writes=["s_x1t"])
            E_(lambda e, tt=tt: e.reduce_sum(s_m3[:, tt:tt + 1], s_x1t[:, :], axis=AX.X),
               reads=["s_x1t"], writes=["s_m3"])
        @A.on("scalar")
        def _(tt=tt):
            E_(lambda e: e.activation(s_zt[:, :], s_x1t[:, :], ACT.Square,
                                      accum_out=s_q3[:, tt:tt + 1]),
               reads=["s_x1t"], writes=["s_ot", "s_q3"])
        @A.on("sync")
        def _(tt=tt):
            E_(lambda e: e.dma_start(out=x1_sp.ap()[128 * tt:128 * (tt + 1), :],
                                     in_=s_x1t[:, :]),
               reads=["s_x1t"], writes=["x1_sp"], dma=True)
    @A.on("vector")
    def _():
        E_(lambda e: e.tensor_scalar_mul(s_m3[:], s_m3[:], 1.0 / H),
           reads=["s_m3"], writes=["s_m3"])
        E_(lambda e: e.tensor_scalar_mul(s_q3[:], s_q3[:], 1.0 / H),
           reads=["s_q3"], writes=["s_q3"])
        E_(lambda e: e.tensor_mul(s_r3[:], s_m3[:], s_m3[:]),
           reads=["s_m3"], writes=["s_r3"])
        E_(lambda e: e.tensor_sub(s_r3[:], s_q3[:], s_r3[:]),
           reads=["s_q3", "s_r3"], writes=["s_r3"])
        E_(lambda e: e.tensor_scalar_add(s_r3[:], s_r3[:], EPS),
           reads=["s_r3"], writes=["s_r3"])
    @A.on("scalar")
    def _():
        E_(lambda e: e.activation(s_r3[:], s_r3[:], ACT.Ln),
           reads=["s_r3"], writes=["s_r3"])
        E_(lambda e: e.activation(s_r3[:], s_r3[:], ACT.Exp, scale=-0.5),
           reads=["s_r3"], writes=["s_r3"])
    for tt in range(NTT):
        @A.on("sync")
        def _(tt=tt):
            E_(lambda e: e.dma_start(out=s_x1t[:, :],
                                     in_=x1_sp.ap()[128 * tt:128 * (tt + 1), :]),
               reads=["x1_sp"], writes=["s_x1t"], dma=True)
        @A.on("vector")
        def _(tt=tt):
            E_(lambda e: e.tensor_scalar(s_x3b[:, tt, :], s_x1t[:, :],
                                         s_m3[:, tt:tt + 1], s_r3[:, tt:tt + 1],
                                         ALU.subtract, ALU.mult),
               reads=["s_x1t", "s_m3", "s_r3"], writes=["s_x3b"])
        @A.on("scalar")
        def _(tt=tt):
            E_(lambda e: e.dma_start(out=x3_sp.ap()[128 * tt:128 * (tt + 1), :],
                                     in_=s_x3b[:, tt, :]),
               reads=["s_x3b"], writes=[f"x3_sp{(128 * tt) // 512}"],
               dma=True, dkey="x3s")
    if debug:
        @A.on("sync")
        def _():
            E_(lambda e: e.dma_start(out=outs["o_x1"].ap(), in_=x1_sp.ap()),
               reads=["x1_sp"], writes=["o_x1"], dma=True)

    A.fence()
    f0ctx.close()

    f1ctx = contextlib.ExitStack()
    sbF1 = mk_sb(f1ctx)
    s_x3T = sbF1("s_x3T", [128, HK, p.TC], BF16)
    s_rw1 = sbF1("s_rw1", [128, HK, 4096], BF16)
    s_hr1 = sbF1("s_hr1", [128, 512], BF16)
    s_lg8 = sbF1("s_lg8", [8, p.TC])
    s_lgt = sbF1("s_lgt", [128, NTT, 8])

    @A.on("sync")
    def _():
        for k in range(HK):
            E_(lambda e, k=k: e.dma_start(out=s_rw1[:, k, :],
                                          in_=rw1_bf.ap()[128 * k:128 * (k + 1), :]),
               reads=["rw1_bf"], writes=["s_rw1"], dma=True)
    @A.on("sync")
    def _():
        NC2 = (p.TC + 511) // 512
        for c2 in range(NC2):
            def ldx3(e, c2=c2):
                w = min(512, p.TC - 512 * c2)
                return [e.dma_start(
                    out=s_x3T[:, ki, 512 * c2:512 * c2 + w],
                    in_=x3_sp.ap()[512 * c2:512 * c2 + w,
                                   128 * ki:128 * (ki + 1)],
                    transpose=True) for ki in range(HK)]
            E_(ldx3, reads=[f"x3_sp{c2}"], writes=[f"s_x3T{c2}"], dma=True,
               dkey=f"ldx3{c2 % 2}")
    NCK2 = min(512, p.TC)
    NCH2 = p.TC // NCK2
    for nch in range(NCH2):
        for mt in range(32):
            pm, pk = ps_m[mt % 2], f"ps_m{mt % 2}"
            @A.on("tensor")
            def _(nch=nch, mt=mt, pm=pm, pk=pk):
                def mms(e):
                    for k in range(HK):
                        r = e.matmul(pm[:, 0:NCK2], s_rw1[:, k, 128 * mt:128 * (mt + 1)],
                                     s_x3T[:, k, NCK2 * nch:NCK2 * (nch + 1)],
                                     start=(k == 0), stop=(k == HK - 1))
                    return r
                E_(mms, reads=["s_rw1", f"s_x3T{nch}"], writes=[pk])
            @A.on("scalar")
            def _(nch=nch, mt=mt, pm=pm, pk=pk):
                E_(lambda e: e.activation(s_hr1[:, 0:NCK2], pm[:, 0:NCK2], ACT.Relu,
                                          bias=s_rb1c[:, mt:mt + 1]),
                   reads=[pk, "s_rb1c"], writes=["s_hr1"])
            @A.on("tensor")
            def _(nch=nch, mt=mt):
                def mm2(e):
                    r = e.matmul(ps_f[0:8, 0:NCK2], s_rw2b[:, mt, :], s_hr1[:, 0:NCK2],
                                 start=(mt == 0), stop=False)
                    if mt == 31:
                        r = e.matmul(ps_f[0:8, 0:NCK2], s_rb2r[0:1, :],
                                     s_onesr[0:1, 0:NCK2], start=False, stop=True)
                    return r
                E_(mm2, reads=["s_rw2b", "s_hr1", "s_rb2r", "s_onesr"], writes=["ps_f"])
        @A.on("scalar")
        def _(nch=nch):
            E_(lambda e: e.copy(s_lg8[:, NCK2 * nch:NCK2 * (nch + 1)], ps_f[0:8, 0:NCK2]),
               reads=["ps_f"], writes=["s_lg8"])
    for tt in range(NTT):
        @A.on("tensor")
        def _(tt=tt):
            E_(lambda e: e.transpose(ps_s[:, 0:8],
                                     s_lg8[0:8, 128 * tt:128 * (tt + 1)],
                                     s_idf[0:8, 0:8]),
               reads=["s_lg8", "s_idf"], writes=["ps_s"])
        @A.on("scalar")
        def _(tt=tt):
            E_(lambda e: e.copy(s_lgt[:, tt, :], ps_s[:, 0:8]),
               reads=["ps_s"], writes=["s_lgt"])
    if debug:
        @A.on("sync")
        def _():
            E_(lambda e: e.dma_start(
                out=outs["o_lg"].ap().rearrange("(a p) m -> p a m", p=128),
                in_=s_lgt[:, :, :]),
               reads=["s_lgt"], writes=["o_lg"], dma=True)

    lg3 = s_lgt[:, :, :]
    @A.on("gpsimd")
    def _():
        E_(lambda e: e.iota(s_io8[:, :, :], [[0, NTT], [1, 8]], channel_multiplier=0,
                            allow_small_or_imprecise_dtypes=True),
           writes=["s_io8"])
    @A.on("vector")
    def _():
        E_(lambda e: e.reduce_max(s_mx1[:, :], lg3, axis=AX.X),
           reads=["s_lgt"], writes=["s_mx1"])
        E_(lambda e: e.tensor_tensor(s_ohA[:, :, :], lg3,
                                     s_mx1[:, :].to_broadcast([128, NTT, 8]),
                                     ALU.is_equal),
           reads=["s_lgt", "s_mx1"], writes=["s_ohA"])
        E_(lambda e: e.scalar_tensor_tensor(s_sel[:, :, :], s_ohA[:, :, :], -99.0,
                                            s_io8[:, :, :], ALU.mult, ALU.add),
           reads=["s_ohA", "s_io8"], writes=["s_sel"])
        E_(lambda e: e.tensor_reduce(s_i1[:, :], s_sel[:, :, :], axis=AX.X, op=ALU.min),
           reads=["s_sel"], writes=["s_i1"])
        E_(lambda e: e.tensor_scalar_add(s_i1[:, :], s_i1[:, :], 99.0),
           reads=["s_i1"], writes=["s_i1"])
        E_(lambda e: e.tensor_tensor(s_ohA[:, :, :], s_io8[:, :, :],
                                     s_i1[:, :].to_broadcast([128, NTT, 8]),
                                     ALU.is_equal),
           reads=["s_io8", "s_i1"], writes=["s_ohA"])
        E_(lambda e: e.scalar_tensor_tensor(s_S[:, :, :], s_ohA[:, :, :], -1e30,
                                            lg3, ALU.mult, ALU.add),
           reads=["s_ohA", "s_lgt"], writes=["s_S"])
        E_(lambda e: e.reduce_max(s_mx2[:, :], s_S[:, :, :], axis=AX.X),
           reads=["s_S"], writes=["s_mx2"])
        E_(lambda e: e.tensor_tensor(s_ohB[:, :, :], s_S[:, :, :],
                                     s_mx2[:, :].to_broadcast([128, NTT, 8]),
                                     ALU.is_equal),
           reads=["s_S", "s_mx2"], writes=["s_ohB"])
        E_(lambda e: e.scalar_tensor_tensor(s_sel[:, :, :], s_ohB[:, :, :], -99.0,
                                            s_io8[:, :, :], ALU.mult, ALU.add),
           reads=["s_ohB", "s_io8"], writes=["s_sel"])
        E_(lambda e: e.tensor_reduce(s_i2[:, :], s_sel[:, :, :], axis=AX.X, op=ALU.min),
           reads=["s_sel"], writes=["s_i2"])
        E_(lambda e: e.tensor_scalar_add(s_i2[:, :], s_i2[:, :], 99.0),
           reads=["s_i2"], writes=["s_i2"])
        E_(lambda e: e.tensor_tensor(s_ohB[:, :, :], s_io8[:, :, :],
                                     s_i2[:, :].to_broadcast([128, NTT, 8]),
                                     ALU.is_equal),
           reads=["s_io8", "s_i2"], writes=["s_ohB"])
        E_(lambda e: e.tensor_mul(s_sel[:, :, :], s_ohA[:, :, :], lg3),
           reads=["s_ohA", "s_lgt"], writes=["s_sel"])
        E_(lambda e: e.reduce_sum(s_tvA[:, :], s_sel[:, :, :], axis=AX.X),
           reads=["s_sel"], writes=["s_tvA"])
        E_(lambda e: e.tensor_mul(s_sel[:, :, :], s_ohB[:, :, :], lg3),
           reads=["s_ohB", "s_lgt"], writes=["s_sel"])
        E_(lambda e: e.reduce_sum(s_tvB[:, :], s_sel[:, :, :], axis=AX.X),
           reads=["s_sel"], writes=["s_tvB"])
        E_(lambda e: e.tensor_sub(s_edt[:, :], s_tvB[:, :], s_tvA[:, :]),
           reads=["s_tvB", "s_tvA"], writes=["s_edt"])
        E_(lambda e: e.tensor_copy(s_ohAb[:, :, :], s_ohA[:, :, :]),
           reads=["s_ohA"], writes=["s_ohAb"])
        E_(lambda e: e.tensor_copy(s_ohBb[:, :, :], s_ohB[:, :, :]),
           reads=["s_ohB"], writes=["s_ohBb"])
    @A.on("scalar")
    def _():
        E_(lambda e: e.activation(s_edt[:, :], s_edt[:, :], ACT.Exp),
           reads=["s_edt"], writes=["s_edt"])
    @A.on("vector")
    def _():
        E_(lambda e: e.tensor_scalar_add(s_wA[:, :], s_edt[:, :], 1.0),
           reads=["s_edt"], writes=["s_wA"])
        E_(lambda e: e.reciprocal(s_wA[:, :], s_wA[:, :]),
           reads=["s_wA"], writes=["s_wA"])
        E_(lambda e: e.tensor_mul(s_wB[:, :], s_edt[:, :], s_wA[:, :]),
           reads=["s_edt", "s_wA"], writes=["s_wB"])
    for tt in range(NTT):
        for st, ohb, pf in [(0, s_ohAb, s_pfA), (1, s_ohBb, s_pfB)]:
            @A.on("tensor")
            def _(tt=tt, st=st, ohb=ohb):
                def mms(e):
                    e.matmul(ps_f2[st][:, 0:8], s_trib[:, :], ohb[:, tt, :],
                             start=True, stop=True)
                    return e.matmul(ps_s[0:1, NTT8 * st + 8 * tt:NTT8 * st + 8 * tt + 8],
                                    s_onec[:, :], ohb[:, tt, :], start=True, stop=True)
                E_(mms, reads=["s_ohAb" if st == 0 else "s_ohBb", "s_trib", "s_onec"],
                   writes=[f"ps_f2{st}", "ps_s"])
            @A.on("vector")
            def _(tt=tt, st=st, pf=pf):
                E_(lambda e: e.tensor_copy(pf[:, tt, :], ps_f2[st][:, 0:8]),
                   reads=[f"ps_f2{st}"], writes=["s_pfA" if st == 0 else "s_pfB"])
    @A.on("vector")
    def _():
        E_(lambda e: e.tensor_reduce(
            s_cnt[0:1, :],
            ps_s[0:1, 0:2 * NTT8].rearrange("o (t e) -> o e t", e=8),
            axis=AX.X, op=ALU.add),
           reads=["ps_s"], writes=["s_cnt"])
    @A.on("sync")
    def _():
        E_(lambda e: e.dma_start(out=cnt_loc.ap(), in_=s_cnt[0:1, :]),
           reads=["s_cnt"], writes=["cnt_loc"], dma=True)
    @A.on("gpsimd")
    def _():
        E_(lambda e: e.collective_compute("AllGather", ALU.bypass, replica_groups=RG,
                                          ins=[cnt_loc.ap()], outs=[cnt_all.ap()]),
           reads=["cnt_loc"], writes=["cnt_all"], cc=True)
    @A.on("vector")
    def _():
        for st in range(2):
            E_(lambda e, st=st: e.memset(s_run[:, 8 * st:8 * st + 8], 0.0),
               writes=["s_run"])
            for tt in range(NTT):
                E_(lambda e, st=st, tt=tt: e.tensor_copy(
                    s_off[0:1, NTT8 * st + 8 * tt:NTT8 * st + 8 * tt + 8],
                    s_run[0:1, 8 * st:8 * st + 8]),
                   reads=["s_run"], writes=["s_off"])
                E_(lambda e, st=st, tt=tt: e.tensor_add(
                    s_run[0:1, 8 * st:8 * st + 8], s_run[0:1, 8 * st:8 * st + 8],
                    ps_s[0:1, NTT8 * st + 8 * tt:NTT8 * st + 8 * tt + 8]),
                   reads=["s_run", "ps_s"], writes=["s_run"])
    @A.on("sync")
    def _():
        E_(lambda e: e.dma_start(out=s_ca[:, :], in_=cnt_all.ap()),
           reads=["cnt_all"], writes=["s_ca"], dma=True)
    @A.on("tensor")
    def _():
        E_(lambda e: e.matmul(ps_s[0:1, 2 * NTT8:2 * NTT8 + 8], s_pfm[:, :], s_ca[:, :],
                              start=True, stop=True),
           reads=["s_pfm", "s_ca"], writes=["ps_s"])
    @A.on("scalar")
    def _():
        E_(lambda e: e.copy(s_gb[0:1, :], ps_s[0:1, 2 * NTT8:2 * NTT8 + 8]),
           reads=["ps_s"], writes=["s_gb"])
    @A.on("vector")
    def _():
        for tt in range(NTT):
            E_(lambda e, tt=tt: e.tensor_copy(s_gbr[0:1, 8 * tt:8 * tt + 8], s_gb[0:1, :]),
               reads=["s_gb"], writes=["s_gbr"])
    for half, mv, mk_, dst, dk in [(0, s_off, "s_off", s_offb, "s_offb"),
                                   (1, s_off, "s_off", s_offb, "s_offb"),
                                   (2, s_gbr, "s_gbr", s_gbb, "s_gbb")]:
        @A.on("tensor")
        def _(half=half, mv=mv, mk_=mk_):
            E_(lambda e: e.matmul(ps_f2[half % 2][:, 0:NTT8], s_ones1[:, :],
                                  mv[0:1, NTT8 * (half % 2):NTT8 * (half % 2) + NTT8]
                                  if half < 2 else mv[0:1, 0:NTT8],
                                  start=True, stop=True),
               reads=[mk_, "s_ones1"], writes=[f"ps_f2{half % 2}"])
        @A.on("scalar")
        def _(half=half, dst=dst, dk=dk):
            E_(lambda e: e.copy(dst[:, NTT8 * (half % 2):NTT8 * (half % 2) + NTT8]
                                if half < 2 else dst[:, 0:NTT8],
                                ps_f2[half % 2][:, 0:NTT8]),
               reads=[f"ps_f2{half % 2}"], writes=[dk])
    @A.on("vector")
    def _():
        E_(lambda e: e.tensor_sub(s_pfA[:, :, :], s_pfA[:, :, :], s_ohA[:, :, :]),
           reads=["s_pfA", "s_ohA"], writes=["s_pfA"])
        E_(lambda e: e.tensor_sub(s_pfB[:, :, :], s_pfB[:, :, :], s_ohB[:, :, :]),
           reads=["s_pfB", "s_ohB"], writes=["s_pfB"])
        E_(lambda e: e.tensor_add(s_S[:, :, :], s_pfA[:, :, :],
                                  s_offb[:, 0:NTT8].rearrange("p (a b) -> p a b", b=8)),
           reads=["s_pfA", "s_offb"], writes=["s_S"])
        E_(lambda e: e.tensor_add(s_S[:, :, :], s_S[:, :, :],
                                  s_offb[:, NTT8:2 * NTT8].rearrange("p (a b) -> p a b", b=8)),
           reads=["s_S", "s_offb"], writes=["s_S"])
        E_(lambda e: e.tensor_add(s_S[:, :, :], s_S[:, :, :], s_pfB[:, :, :]),
           reads=["s_S", "s_pfB"], writes=["s_S"])
        E_(lambda e: e.tensor_mul(s_sel[:, :, :], s_ohA[:, :, :], s_S[:, :, :]),
           reads=["s_ohA", "s_S"], writes=["s_sel"])
        E_(lambda e: e.reduce_sum(s_mgA[:, :], s_sel[:, :, :], axis=AX.X),
           reads=["s_sel"], writes=["s_mgA"])
        E_(lambda e: e.tensor_mul(s_sel[:, :, :], s_ohB[:, :, :], s_S[:, :, :]),
           reads=["s_ohB", "s_S"], writes=["s_sel"])
        E_(lambda e: e.reduce_sum(s_mgB[:, :], s_sel[:, :, :], axis=AX.X),
           reads=["s_sel"], writes=["s_mgB"])
        E_(lambda e: e.tensor_mul(s_sel[:, :, :], s_ohA[:, :, :],
                                  s_gbb[:, 0:NTT8].rearrange("p (a b) -> p a b", b=8)),
           reads=["s_ohA", "s_gbb"], writes=["s_sel"])
        E_(lambda e: e.reduce_sum(s_tvA[:, :], s_sel[:, :, :], axis=AX.X),
           reads=["s_sel"], writes=["s_tvA"])
        E_(lambda e: e.tensor_mul(s_sel[:, :, :], s_ohB[:, :, :],
                                  s_gbb[:, 0:NTT8].rearrange("p (a b) -> p a b", b=8)),
           reads=["s_ohB", "s_gbb"], writes=["s_sel"])
        E_(lambda e: e.reduce_sum(s_tvB[:, :], s_sel[:, :, :], axis=AX.X),
           reads=["s_sel"], writes=["s_tvB"])
        E_(lambda e: e.tensor_add(s_tvA[:, :], s_tvA[:, :], s_mgA[:, :]),
           reads=["s_tvA", "s_mgA"], writes=["s_tvA"])
        E_(lambda e: e.tensor_add(s_tvB[:, :], s_tvB[:, :], s_mgB[:, :]),
           reads=["s_tvB", "s_mgB"], writes=["s_tvB"])
        E_(lambda e: e.tensor_scalar(s_kA[:, :], s_tvA[:, :], float(CAP), None, ALU.is_lt),
           reads=["s_tvA"], writes=["s_kA"])
        E_(lambda e: e.tensor_mul(s_wA[:, :], s_wA[:, :], s_kA[:, :]),
           reads=["s_wA", "s_kA"], writes=["s_wA"])
        E_(lambda e: e.tensor_scalar(s_kB[:, :], s_tvB[:, :], float(CAP), None, ALU.is_lt),
           reads=["s_tvB"], writes=["s_kB"])
        E_(lambda e: e.tensor_mul(s_wB[:, :], s_wB[:, :], s_kB[:, :]),
           reads=["s_wB", "s_kB"], writes=["s_wB"])
        E_(lambda e: e.memset(s_mx1[:, :], 0.0), writes=["s_mx1"])
        E_(lambda e: e.memset(s_mx2[:, :], 0.0), writes=["s_mx2"])
        for k in range(1, CAP // 512):
            E_(lambda e, k=k: e.tensor_scalar(s_edt[:, :], s_tvA[:, :],
                                              float(512 * k), None, ALU.is_ge),
               reads=["s_tvA"], writes=["s_edt"])
            E_(lambda e: e.tensor_add(s_mx1[:, :], s_mx1[:, :], s_edt[:, :]),
               reads=["s_mx1", "s_edt"], writes=["s_mx1"])
            E_(lambda e, k=k: e.tensor_scalar(s_edt[:, :], s_tvB[:, :],
                                              float(512 * k), None, ALU.is_ge),
               reads=["s_tvB"], writes=["s_edt"])
            E_(lambda e: e.tensor_add(s_mx2[:, :], s_mx2[:, :], s_edt[:, :]),
               reads=["s_mx2", "s_edt"], writes=["s_mx2"])
        E_(lambda e: e.scalar_tensor_tensor(s_mgA[:, :], s_i1[:, :], 512.0,
                                            s_tvA[:, :], ALU.mult, ALU.add),
           reads=["s_i1", "s_tvA", "s_mgA"], writes=["s_mgA"])
        E_(lambda e: e.scalar_tensor_tensor(s_mgA[:, :], s_mx1[:, :],
                                            float(512 * (NEXP - 1)),
                                            s_mgA[:, :], ALU.mult, ALU.add),
           reads=["s_mx1", "s_mgA"], writes=["s_mgA"])
        E_(lambda e: e.scalar_tensor_tensor(s_mgB[:, :], s_i2[:, :], 512.0,
                                            s_tvB[:, :], ALU.mult, ALU.add),
           reads=["s_i2", "s_tvB", "s_mgB"], writes=["s_mgB"])
        E_(lambda e: e.scalar_tensor_tensor(s_mgB[:, :], s_mx2[:, :],
                                            float(512 * (NEXP - 1)),
                                            s_mgB[:, :], ALU.mult, ALU.add),
           reads=["s_mx2", "s_mgB"], writes=["s_mgB"])
        E_(lambda e: e.tensor_mul(s_mgA[:, :], s_mgA[:, :], s_kA[:, :]),
           reads=["s_mgA", "s_kA"], writes=["s_mgA"])
        E_(lambda e: e.tensor_mul(s_mgB[:, :], s_mgB[:, :], s_kB[:, :]),
           reads=["s_mgB", "s_kB"], writes=["s_mgB"])
        E_(lambda e: e.tensor_copy(s_giA[:, :], s_mgA[:, :]),
           reads=["s_mgA"], writes=["s_giA"])
        E_(lambda e: e.tensor_copy(s_giB[:, :], s_mgB[:, :]),
           reads=["s_mgB"], writes=["s_giB"])
        E_(lambda e: e.scalar_tensor_tensor(s_edt[:, :], s_kA[:, :], float(-NSLOT),
                                            s_mgA[:, :], ALU.mult, ALU.add),
           reads=["s_kA", "s_mgA"], writes=["s_edt"])
        E_(lambda e: e.tensor_scalar_add(s_edt[:, :], s_edt[:, :], float(NSLOT)),
           reads=["s_edt"], writes=["s_edt"])
        E_(lambda e: e.tensor_copy(s_siA[:, :], s_edt[:, :]),
           reads=["s_edt"], writes=["s_siA"])
        E_(lambda e: e.scalar_tensor_tensor(s_edt[:, :], s_kB[:, :], float(-NSLOT),
                                            s_mgB[:, :], ALU.mult, ALU.add),
           reads=["s_kB", "s_mgB"], writes=["s_edt"])
        E_(lambda e: e.tensor_scalar_add(s_edt[:, :], s_edt[:, :], float(NSLOT)),
           reads=["s_edt"], writes=["s_edt"])
        E_(lambda e: e.tensor_copy(s_siB[:, :], s_edt[:, :]),
           reads=["s_edt"], writes=["s_siB"])
    if debug:
        @A.on("vector")
        def _():
            for j, srcb, k in [(0, s_i1, "s_i1"), (1, s_i2, "s_i2"),
                               (2, s_wA, "s_wA"), (3, s_wB, "s_wB"),
                               (4, s_tvA, "s_tvA"), (5, s_tvB, "s_tvB"),
                               (6, s_mgA, "s_mgA"), (7, s_mgB, "s_mgB")]:
                E_(lambda e, j=j, srcb=srcb, k=k: e.tensor_copy(
                    s_dbg[:, :, j:j + 1].rearrange("p a o -> p (a o)"), srcb[:, :]),
                   reads=[k], writes=["s_dbg"])
        @A.on("sync")
        def _():
            E_(lambda e: e.dma_start(
                out=outs["o_top"].ap().rearrange("(a p) m -> p a m", p=128),
                in_=s_dbg[:, :, :]),
               reads=["s_dbg"], writes=["o_top"], dma=True)

    for tt in range(NTT):
        @A.on("gpsimd")
        def _(tt=tt):
            E_(lambda e: e.indirect_dma_start(
                stg.ap(), bass.IndirectOffsetOnAxis(s_siA[:, tt:tt + 1], 0),
                s_x3b[:, tt, :], None),
               reads=["s_x3b", "s_siA", "stg"], writes=["stg"], dkey="stg_s", dma=True)
            E_(lambda e: e.indirect_dma_start(
                stg.ap(), bass.IndirectOffsetOnAxis(s_siB[:, tt:tt + 1], 0),
                s_x3b[:, tt, :], None),
               reads=["s_x3b", "s_siB", "stg"], writes=["stg"], dkey="stg_s", dma=True)

    A.fence()
    f1ctx.close()
    fxctx.close()
    f2ctx.close()

    f4ctx = contextlib.ExitStack()
    sbF4 = mk_sb(f4ctx)
    s_wF = sbF4("s_wF", [128, 6 * 4096], BF16)
    s_w2 = sbF4("s_w2", [128, 32 * 1024], BF16)
    s_dR = sbF4("s_dR", [128, 4, H], BF16)
    s_dT = sbF4("s_dT", [128, HK, 512], BF16)
    s_hb = sbF4("s_hb", [128, 32, 512], BF16)
    s_eoT = s_dT
    s_eb1c = sbF4("s_eb1c", [128, 32])
    s_eb2c = sbF4("s_eb2c", [128, 8])
    s_xf = sbF4("s_xf", [128, 2, H], BF16)

    @A.on("scalar")
    def _():
        for k in range(6):
            E_(lambda e, k=k: e.dma_start(
                out=s_wF[:, 4096 * k:4096 * (k + 1)],
                in_=myw1_bf.ap()[128 * (2 + k):128 * (3 + k), :]),
               reads=["myw1_bf"], writes=["s_wF"], dma=True)
        for k2 in range(32):
            E_(lambda e, k2=k2: e.dma_start(
                out=s_w2[:, 1024 * k2:1024 * (k2 + 1)],
                in_=myw2_bf.ap()[128 * k2:128 * (k2 + 1), :]),
               reads=["myw2_bf"], writes=["s_w2"], dma=True)
        E_(lambda e: e.dma_start(
            out=s_eb1c[:, :], in_=myb1.ap().rearrange("(a p) o -> p (a o)", p=128),
            allow_slow_non_contiguous=True),
           writes=["s_eb1c"], dma=True)
        E_(lambda e: e.dma_start(
            out=s_eb2c[:, :], in_=myb2.ap().rearrange("(a p) o -> p (a o)", p=128),
            allow_slow_non_contiguous=True),
           writes=["s_eb2c"], dma=True)

    NQ = CAP // 512
    for q in range(NQ):
        @A.on("gpsimd")
        def _(q=q):
            E_(lambda e, q=q: e.collective_compute(
                "ReduceScatter", ALU.add, replica_groups=RG,
                ins=[stg.ap()[NEXP * 512 * q:NEXP * 512 * (q + 1), :]],
                outs=[rs_out.ap()[512 * q:512 * (q + 1), :]]),
               reads=["stg"], writes=[f"rs_out{q}"], cc=True, dkey=f"rs{q}")

    for q in range(8):
        @A.on("sync")
        def _(q=q):
            for j in range(4):
                E_(lambda e, j=j, q=q: e.dma_start(
                    out=s_dR[:, j, :],
                    in_=rs_out.ap()[512 * q + 128 * j:512 * q + 128 * (j + 1), :]),
                   reads=[f"rs_out{q}"], writes=["s_dR"], dma=True)
        for k in range(HK):
            @A.on("tensor")
            def _(k=k):
                def tps(e):
                    for j in range(4):
                        r = e.transpose(ps_t[:, 128 * j:128 * (j + 1)],
                                        s_dR[:, j, 128 * k:128 * (k + 1)], s_idb[:])
                    return r
                E_(tps, reads=["s_dR", "s_idb"], writes=["ps_t"])
            @A.on("scalar")
            def _(k=k):
                E_(lambda e: e.copy(s_dT[:, k, :], ps_t[:, 0:512]),
                   reads=["ps_t"], writes=["s_dT"])
        for hm in range(32):
            pm, pk = ps_m[hm % 2], f"ps_m{hm % 2}"
            @A.on("tensor")
            def _(hm=hm, pm=pm, pk=pk):
                def mms(e):
                    for k in range(HK):
                        kk = k if k < 2 else k - 2
                        wsl = (s_wE if k < 2 else s_wF)[
                            :, 4096 * kk + 128 * hm:4096 * kk + 128 * (hm + 1)]
                        r = e.matmul(pm[:, 0:512], wsl, s_dT[:, k, :],
                                     start=(k == 0), stop=(k == HK - 1))
                    return r
                E_(mms, reads=["s_wE", "s_wF", "s_dT"], writes=[pk])
            @A.on("scalar")
            def _(hm=hm, pm=pm, pk=pk):
                E_(lambda e: e.activation(s_hb[:, hm, :], pm[:, 0:512],
                                          ACT.Relu, bias=s_eb1c[:, hm:hm + 1]),
                   reads=[pk, "s_eb1c"], writes=["s_hb"])
        for dm in range(HK):
            pm, pk = ps_m[dm % 2], f"ps_m{dm % 2}"
            @A.on("tensor")
            def _(dm=dm, pm=pm, pk=pk):
                def mms(e):
                    for k2 in range(32):
                        r = e.matmul(pm[:, 0:512],
                                     s_w2[:, 1024 * k2 + 128 * dm:1024 * k2 + 128 * (dm + 1)],
                                     s_hb[:, k2, :], start=(k2 == 0), stop=(k2 == 31))
                    return r
                E_(mms, reads=["s_w2", "s_hb"], writes=[pk])
            @A.on("scalar")
            def _(dm=dm, pm=pm, pk=pk):
                E_(lambda e: e.activation(s_eoT[:, dm, :], pm[:, 0:512],
                                          ACT.Identity, bias=s_eb2c[:, dm:dm + 1]),
                   reads=[pk, "s_eb2c"], writes=["s_dT"])
        for j in range(4):
            for kg in range(2):
                @A.on("tensor")
                def _(j=j, kg=kg):
                    def tps(e):
                        for k in range(4):
                            r = e.transpose(ps_t[:, 128 * k:128 * (k + 1)],
                                            s_eoT[:, 4 * kg + k, 128 * j:128 * (j + 1)],
                                            s_idb[:])
                        return r
                    E_(tps, reads=["s_dT", "s_idb"], writes=["ps_t"])
                @A.on("scalar")
                def _(j=j, kg=kg):
                    E_(lambda e: e.copy(s_dR[:, j, 512 * kg:512 * (kg + 1)],
                                        ps_t[:, 0:512]),
                       reads=["ps_t"], writes=["s_dR"])
        @A.on("sync")
        def _(q=q):
            E_(lambda e, q=q: e.dma_start(
                out=eo_loc.ap()[512 * q:512 * (q + 1), :].rearrange("(j p) d -> p j d", p=128),
                in_=s_dR[:, :, :]),
               reads=["s_dR"], writes=["eo_loc"], dma=True)
        @A.on("gpsimd")
        def _(q=q):
            E_(lambda e, q=q: e.collective_compute(
                "AllGather", ALU.bypass, replica_groups=RG,
                ins=[eo_loc.ap()[512 * q:512 * (q + 1), :]],
                outs=[eo_g.ap()[NEXP * 512 * q:NEXP * 512 * (q + 1), :]]),
               reads=["eo_loc"], writes=[f"eo_g{q}"], cc=True, dkey="ag")

    for tt in range(NTT):
        fp = tt % 2
        @A.on("gpsimd")
        def _(tt=tt, fp=fp):
            eg = [f"eo_g{q}" for q in range(CAP // 512)]
            E_(lambda e: e.indirect_dma_start(
                s_g0[:, fp, :], None, eo_g.ap(),
                bass.IndirectOffsetOnAxis(s_giA[:, tt:tt + 1], 0)),
               reads=eg + ["s_giA"], writes=[f"s_g0{fp}"], dma=True)
            E_(lambda e: e.indirect_dma_start(
                s_g1[:, fp, :], None, eo_g.ap(),
                bass.IndirectOffsetOnAxis(s_giB[:, tt:tt + 1], 0)),
               reads=eg + ["s_giB"], writes=[f"s_g1{fp}"], dma=True)
        @A.on("gpsimd")
        def _(tt=tt, fp=fp):
            E_(lambda e: e.dma_start(out=s_xf[:, fp, :],
                                     in_=x1_sp.ap()[128 * tt:128 * (tt + 1), :]),
               reads=["x1_sp"], writes=[f"s_xf{fp}"], dma=True)
        @A.on("vector")
        def _(tt=tt, fp=fp):
            E_(lambda e: e.scalar_tensor_tensor(
                s_ot[:, fp, :], s_g0[:, fp, :], s_wA[:, tt:tt + 1],
                s_xf[:, fp, :], ALU.mult, ALU.add),
               reads=[f"s_g0{fp}", "s_wA", f"s_xf{fp}"], writes=["s_ot"])
            E_(lambda e: e.scalar_tensor_tensor(
                s_ot[:, fp, :], s_g1[:, fp, :], s_wB[:, tt:tt + 1], s_ot[:, fp, :],
                ALU.mult, ALU.add),
               reads=[f"s_g1{fp}", "s_wB", "s_ot"], writes=["s_ot"])
        @A.on("sync")
        def _(tt=tt, fp=fp):
            E_(lambda e: e.dma_start(out=o_out.ap()[128 * tt:128 * (tt + 1), :],
                                     in_=s_ot[:, fp, :]),
               reads=["s_ot"], writes=["o_out"], dma=True)

    A.final_dma_waits()
    ctx.pop_all()
    block.__exit__(None, None, None)
    return nc, outs


_CACHE = {}


def kernel(**inputs):
    import numpy as np
    from concourse.bass_utils import run_bass_kernel_spmd

    f32 = lambda a: np.ascontiguousarray(a, dtype=np.float32)
    par = P(L=8192)
    T, TC = par.T, par.TC
    if "nc" not in _CACHE:
        _CACHE["nc"] = build(par, debug=False)[0]
    nc = _CACHE["nc"]

    x = np.asarray(inputs["L0_attn_hidden_states"], dtype=np.float32).reshape(T, H)
    W = np.asarray(inputs["L0_attn_in_proj_w"], dtype=np.float32)
    bip = np.asarray(inputs["L0_attn_in_proj_b"], dtype=np.float32)
    ew1 = np.asarray(inputs["L0_ffn_ew1"], dtype=np.float32)
    ew2 = np.asarray(inputs["L0_ffn_ew2"], dtype=np.float32)
    eb1 = np.asarray(inputs["L0_ffn_eb1"], dtype=np.float32)
    eb2 = np.asarray(inputs["L0_ffn_eb2"], dtype=np.float32)
    maps = []
    for c in range(NCOR):
        es = slice(c * EC, (c + 1) * EC)
        pm = np.zeros((8, 1), np.float32)
        pm[:c] = 1.0
        maps.append({
            "x_rows": f32(x[c * TC:(c + 1) * TC]),
            "w_ip_x": f32(W[:, es]),
            "w_ip_g": f32(W[:, E + c * EC: E + (c + 1) * EC]),
            "b_ip_x": f32(bip[es][None, :]),
            "b_ip_g": f32(bip[E + c * EC: E + (c + 1) * EC][None, :]),
            "ln_w": f32(np.asarray(inputs["L0_attn_ln_w"])[None, :]),
            "ln_b": f32(np.asarray(inputs["L0_attn_ln_b"])[None, :]),
            "conv_w": f32(np.asarray(inputs["L0_attn_conv_w"])[es]),
            "conv_b": f32(np.asarray(inputs["L0_attn_conv_b"])[es][:, None]),
            "xproj_w": f32(np.asarray(inputs["L0_attn_x_proj_w"])[es]),
            "dtproj_w": f32(np.asarray(inputs["L0_attn_dt_proj_w"])[:, es]),
            "dtproj_b": f32(np.asarray(inputs["L0_attn_dt_proj_b"])[es][:, None]),
            "a_log": f32(np.asarray(inputs["L0_attn_A_log"])[es]),
            "d_in": f32(np.asarray(inputs["L0_attn_D"])[es][:, None]),
            "wo_shard": f32(np.asarray(inputs["L0_attn_out_proj_w"])[es]),
            "wob_in": f32(np.asarray(inputs["L0_attn_out_proj_b"])[None, :]),
            "rw1_shard": f32(np.asarray(inputs["L0_ffn_rw1"])[128 * c:128 * (c + 1), :]),
            "rw2_in": f32(inputs["L0_ffn_rw2"]),
            "rb1_in": f32(np.asarray(inputs["L0_ffn_rb1"])[:, None]),
            "rb2_in": f32(np.asarray(inputs["L0_ffn_rb2"])[None, :]),
            "myw1": f32(ew1[c]),
            "myw2": f32(ew2[c]),
            "myb1": f32(eb1[c][:, None]),
            "myb2": f32(eb2[c][:, None]),
            "pfxmask": pm,
        })
    res = run_bass_kernel_spmd(nc, maps, core_ids=list(range(NCOR)))
    out = np.concatenate([np.asarray(res.results[c]["o_out"]) for c in range(NCOR)], 0)
    return out.reshape(2, 8192, 1024).astype(np.float32, copy=False)


# version 20
# speedup vs baseline: 1.1150x; 1.0632x over previous
import numpy as np
import contextlib
import ml_dtypes
import concourse.bass as bass
import concourse.mybir as mybir
from concourse.aot_env import allow_ici_all_to_all

F32 = mybir.dt.float32
BF16 = mybir.dt.bfloat16
I32 = mybir.dt.int32
ALU = mybir.AluOpType
ACT = mybir.ActivationFunctionType
AX = mybir.AxisListType
BF = ml_dtypes.bfloat16

NCOR = 8
H, E, NST, RNK, KC = 1024, 2048, 16, 64, 4
EC = E // NCOR
NET = EC // 128
HK = H // 128
NEXP, TOPK, CAP = 8, 2, 4096
EPS = 1e-5
RG = [list(range(NCOR))]


class P:
    def __init__(self, L=8192):
        self.L = L
        self.T = 2 * L
        self.NSL = NCOR
        self.TS = self.T // NCOR
        self.TC = self.TS
        assert self.L % self.TS == 0
        self.SLB = self.L // self.TS
        self.NCH = max(1, self.TS // 512)
        self.NCK = min(512, self.TS)
        self.NTT = self.TC // 128


class Auto:
    ENG_SEMS = {"tensor": "c_mm", "vector": "c_ve", "scalar": "c_sc",
                "gpsimd": "c_gp", "sync": "c_sy"}

    def __init__(self, nc, block, ctx):
        self.nc, self.block, self.ctx = nc, block, ctx
        self.sems, self.ct = {}, {}
        for name in set(self.ENG_SEMS.values()):
            self.sems[name] = ctx.enter_context(nc.semaphore(name))
            self.ct[name] = 0
        self.last_w = {}
        self.last_r = {}
        self.waited = {e: {} for e in self.ENG_SEMS}
        self._cur = None

    def dsem(self, name):
        if name not in self.sems:
            self.sems[name] = self.ctx.enter_context(self.nc.semaphore(name))
            self.ct[name] = 0
        return self.sems[name]

    def on(self, engine_name):
        def deco(fn):
            def wrapper(eng):
                prev = self._cur
                self._cur = (engine_name, eng)
                try:
                    fn()
                finally:
                    self._cur = prev
            getattr(self.block, engine_name)(wrapper)
        return deco

    def _wait(self, eng_name, eng, sem, val):
        if val <= 0:
            return
        if self.waited[eng_name].get(sem, -1) >= val:
            return
        eng.wait_ge(self.sems[sem], val)
        self.waited[eng_name][sem] = val

    def emit(self, fn, reads=(), writes=(), dma=False, cc=False, dkey=None):
        eng_name, eng = self._cur
        needs = {}
        for b in reads:
            if b in self.last_w:
                s, v = self.last_w[b]
                needs[s] = max(needs.get(s, 0), v)
        for b in writes:
            if b in self.last_w:
                s, v = self.last_w[b]
                needs[s] = max(needs.get(s, 0), v)
            for s, v in self.last_r.get(b, {}).items():
                needs[s] = max(needs.get(s, 0), v)
        for s, v in needs.items():
            self._wait(eng_name, eng, s, v)
        inst = fn(eng)
        insts = inst if isinstance(inst, list) else [inst]
        if cc:
            sem_name, step = "cc_" + (writes[0] if writes else "x"), 1
            self.dsem(sem_name)
        elif dma:
            sem_name, step = "qd_" + (dkey or (writes[0] if writes else "x")), 16
            self.dsem(sem_name)
        else:
            sem_name, step = self.ENG_SEMS[eng_name], 1
        for it in insts:
            self.ct[sem_name] += step
            it.then_inc(self.sems[sem_name], step)
        v = self.ct[sem_name]
        inst = insts[-1]
        for b in writes:
            self.last_w[b] = (sem_name, v)
            self.last_r.pop(b, None)
        for b in reads:
            self.last_r.setdefault(b, {})[sem_name] = v
        return inst

    def fence(self):
        snap = [(s, v) for s, v in self.ct.items() if v > 0]
        for eng_name in self.ENG_SEMS:
            def mk(eng_name=eng_name):
                def body():
                    eng = self._cur[1]
                    for s, v in snap:
                        if self.waited[eng_name].get(s, -1) < v:
                            eng.wait_ge(self.sems[s], v)
                            self.waited[eng_name][s] = v
                return body
            self.on(eng_name)(mk())

    def final_dma_waits(self):
        @self.on("sync")
        def _():
            eng = self._cur[1]
            for s_name, v in sorted(self.ct.items()):
                if (s_name.startswith("qd_") or s_name.startswith("cc_")) and v > 0:
                    eng.wait_ge(self.sems[s_name], v)


def build(p: P, debug=False):
    nc = bass.Bass("TRN2", target_bir_lowering=False)
    ctx = contextlib.ExitStack()
    def mk_sb(ctx_):
        def sb(name, shape, dtype=F32):
            esz = 2 if dtype == BF16 else 4
            nb = esz
            for dd in shape[1:]:
                nb *= dd
            al = 128 if nb >= 2048 else 32
            return ctx_.enter_context(nc.sbuf_tensor(name, shape, dtype,
                                                     align_bytes=al))
        return sb
    TS, NCH, NCK, NTT, NSL = p.TS, p.NCH, p.NCK, p.NTT, p.NSL
    CH4 = 4 if p.TC % (4 * 512) == 0 else 1
    CSZ = p.TC // CH4

    dram = lambda n, s, d=F32, k="Internal": nc.dram_tensor(n, s, d, kind=k)
    x_rows = dram("x_rows", [p.TC, H], F32, "ExternalInput")
    w_ip_x = dram("w_ip_x", [H, EC], F32, "ExternalInput")
    w_ip_g = dram("w_ip_g", [H, EC], F32, "ExternalInput")
    b_ip_x = dram("b_ip_x", [1, EC], F32, "ExternalInput")
    b_ip_g = dram("b_ip_g", [1, EC], F32, "ExternalInput")
    ln_w = dram("ln_w", [1, H], F32, "ExternalInput")
    ln_b = dram("ln_b", [1, H], F32, "ExternalInput")
    conv_w = dram("conv_w", [EC, KC], F32, "ExternalInput")
    conv_b = dram("conv_b", [EC, 1], F32, "ExternalInput")
    xproj_w = dram("xproj_w", [EC, 96], F32, "ExternalInput")
    dtproj_w = dram("dtproj_w", [RNK, EC], F32, "ExternalInput")
    dtproj_b = dram("dtproj_b", [EC, 1], F32, "ExternalInput")
    a_log = dram("a_log", [EC, NST], F32, "ExternalInput")
    d_in = dram("d_in", [EC, 1], F32, "ExternalInput")

    xbf_loc = dram("xbf_loc", [p.TC, H], BF16)
    xbf = dram("xbf", [p.T, H], BF16)
    st_loc = dram("st_loc", [2, p.TC], F32)
    st_all = dram("st_all", [2 * NCOR, p.TC], F32)
    dbc_par = dram("dbc_par", [NSL, 96, TS], F32)
    dbc_bf = dram("dbc_bf", [2, 32, TS], BF16)
    x3_sp = dram("x3_sp", [p.TC, H], BF16)
    dbc_sum = dram("dbc_sum", [NSL, 96, TS], F32)
    a2a_in = dram("a2a_in", [NSL, EC, TS], BF16)
    a2a_out = dram("a2a_out", [E, p.TC], BF16)

    wo_shard = dram("wo_shard", [EC, H], F32, "ExternalInput")
    wob_in = dram("wob_in", [1, H], F32, "ExternalInput")
    rw1_shard = dram("rw1_shard", [128, 4096], F32, "ExternalInput")
    rw2_in = dram("rw2_in", [4096, 8], F32, "ExternalInput")
    rb1_in = dram("rb1_in", [4096, 1], F32, "ExternalInput")
    rb2_in = dram("rb2_in", [1, 8], F32, "ExternalInput")
    myw1 = dram("myw1", [H, 4096], F32, "ExternalInput")
    myw2 = dram("myw2", [4096, H], F32, "ExternalInput")
    myb1 = dram("myb1", [4096, 1], F32, "ExternalInput")
    myb2 = dram("myb2", [H, 1], F32, "ExternalInput")
    pfxmask = dram("pfxmask", [8, 1], F32, "ExternalInput")

    NSLOT = NEXP * CAP
    myw1_bf = dram("myw1_bf", [H, 4096], BF16)
    myw2_bf = dram("myw2_bf", [4096, H], BF16)
    wo_loc = dram("wo_loc", [EC, H], BF16)
    wo_bf = dram("wo_bf", [E, H], BF16)
    rw1_loc = dram("rw1_loc", [128, 4096], BF16)
    rw1_bf = dram("rw1_bf", [H, 4096], BF16)
    x1_sp = dram("x1_sp", [p.TC, H], F32)
    stg = dram("stg", [NSLOT + 128, H], BF16)
    rs_out = dram("rs_out", [CAP, H], BF16)
    eo_loc = dram("eo_loc", [CAP, H], BF16)
    eo_g = dram("eo_g", [NSLOT, H], BF16)
    cnt_loc = dram("cnt_loc", [1, 8], F32)
    cnt_all = dram("cnt_all", [NCOR, 8], F32)
    o_out = dram("o_out", [p.TC, H], F32, "ExternalOutput")

    outs = {}
    outs["o_out"] = o_out
    if debug:
        outs["o_xscan"] = dram("o_xscan", [EC, p.T], BF16, "ExternalOutput")
        outs["o_dt"] = dram("o_dt", [EC, p.T], BF16, "ExternalOutput")
        outs["o_dbc"] = dram("o_dbc", [96, p.T], F32, "ExternalOutput")
        outs["o_yg"] = dram("o_yg", [E, p.TC], BF16, "ExternalOutput")
        outs["o_x1"] = dram("o_x1", [p.TC, H], F32, "ExternalOutput")
        outs["o_lg"] = dram("o_lg", [p.TC, 8], F32, "ExternalOutput")
        outs["o_top"] = dram("o_top", [p.TC, 8], F32, "ExternalOutput")

    c_ones1 = nc.inline_tensor(np.ones((1, 128), np.float32), "c_ones1")
    c_onesr = nc.inline_tensor(np.ones((1, NCK), np.float32), "c_onesr")
    c_idf = nc.inline_tensor(np.eye(128, dtype=np.float32), "c_idf")
    c_trib = nc.inline_tensor(np.triu(np.ones((128, 128))).astype(BF), "c_trib")
    c_idb = nc.inline_tensor(np.eye(128).astype(BF), "c_idb")
    c_onescol = nc.inline_tensor(np.ones((128, 1)).astype(BF), "c_onescol")
    c_onesrow = nc.inline_tensor(np.ones((1, TS)).astype(BF), "c_onesrow")

    sbG = mk_sb(ctx)
    psb = lambda n, s: ctx.enter_context(nc.psum_tensor(n, s, F32))
    s_mu = sbG("s_mu", [128, NTT])
    s_r1 = sbG("s_r1", [128, NTT])
    s_ones1 = sbG("s_ones1", [1, 128])
    s_onesr = sbG("s_onesr", [1, NCK])
    s_idf = sbG("s_idf", [128, 128])
    s_trib = sbG("s_trib", [128, 128], BF16)
    s_idb = sbG("s_idb", [128, 128], BF16)
    s_onec = sbG("s_onec", [128, 1], BF16)
    s_wobr = sbG("s_wobr", [1, H])
    s_pfm = sbG("s_pfm", [8, 1])
    s_rw2b = sbG("s_rw2b", [128, 32, 8], BF16)
    s_rb1c = sbG("s_rb1c", [128, 32])
    s_rb2r = sbG("s_rb2r", [1, 8])

    ps_m = [psb("ps_m0", [128, 512]), psb("ps_m1", [128, 512])]
    ps_s = psb("ps_s", [128, 512])

    mctx = contextlib.ExitStack()
    sbM = mk_sb(mctx)
    psM = lambda n, s, d=F32: mctx.enter_context(nc.psum_tensor(n, s, d))
    s_wxb = sbM("s_wxb", [128, HK, EC], BF16)
    s_wgb = sbM("s_wgb", [128, HK, EC], BF16)
    s_lnwc = sbM("s_lnwc", [128, HK])
    s_lnbc = sbM("s_lnbc", [128, HK])
    s_s1x = sbM("s_s1x", [1, EC])
    s_s2x = sbM("s_s2x", [1, EC])
    s_s1g = sbM("s_s1g", [1, EC])
    s_s2g = sbM("s_s2g", [1, EC])
    s_s12x = sbM("s_s12x", [2, EC], BF16)
    s_s12g = sbM("s_s12g", [2, EC], BF16)
    s_bxr = sbM("s_bxr", [1, EC])
    s_bgr = sbM("s_bgr", [1, EC])
    s_cw = sbM("s_cw", [128, NET, KC])
    s_cbc = sbM("s_cbc", [128, NET])
    s_dpb = sbM("s_dpb", [128, NET])
    s_A = sbM("s_A", [128, NET, NST])
    s_D = sbM("s_D", [128, NET])
    s_xpw = sbM("s_xpw", [128, NET, 96], BF16)
    s_dpw = sbM("s_dpw", [64, EC], BF16)
    s_hst = sbM("s_hst", [128, NET, NST])
    s_halo = sbM("s_halo", [128, NET, KC - 1])
    ps_db = psM("ps_db", [96, 512])
    block = ctx.enter_context(nc.Block())
    A = Auto(nc, block, ctx)
    E_ = A.emit

    pctx = contextlib.ExitStack()
    sbP = mk_sb(pctx)
    s_xr = sbP("s_xr", [128, H])
    s_xrb = sbP("s_xrb", [128, H], BF16)
    s_scr = sbP("s_scr", [128, H])
    s_sq = sbP("s_sq", [128, NTT])
    s_var = sbP("s_var", [128, NTT])
    s_t0 = sbP("s_t0", [128, NTT])
    s_ac = sbP("s_ac", [128, 2 * NTT])
    s_acT = sbP("s_acT", [2 * NTT, 128])
    s_wf = sbP("s_wf", [128, HK, EC])

    @A.on("sync")
    def _():
        for dst, key, src in [
            (s_wf[:], "s_wf", w_ip_x.ap().rearrange("(a p) m -> p a m", p=128)),
            (s_lnwc[:], "s_lnwc", ln_w.ap().rearrange("o (a p) -> p (o a)", p=128)),
            (s_lnbc[:], "s_lnbc", ln_b.ap().rearrange("o (a p) -> p (o a)", p=128)),
            (s_bxr[:], "s_bxr", b_ip_x.ap()),
            (s_bgr[:], "s_bgr", b_ip_g.ap()),
            (s_cw[:], "s_cw", conv_w.ap().rearrange("(a p) k -> p a k", p=128)),
            (s_cbc[:], "s_cbc", conv_b.ap().rearrange("(a p) o -> p (a o)", p=128)),
            (s_dpb[:], "s_dpb", dtproj_b.ap().rearrange("(a p) o -> p (a o)", p=128)),
            (s_A[:], "s_A", a_log.ap().rearrange("(a p) n -> p a n", p=128)),
            (s_D[:], "s_D", d_in.ap().rearrange("(a p) o -> p (a o)", p=128)),
            (s_ones1[:], "s_ones1", c_ones1.ap()),
            (s_onesr[:], "s_onesr", c_onesr.ap()),
            (s_idf[:], "s_idf", c_idf.ap()),
            (s_trib[:], "s_trib", c_trib.ap()),
            (s_idb[:], "s_idb", c_idb.ap()),
            (s_onec[:], "s_onec", c_onescol.ap()),
            (s_wobr[:], "s_wobr", wob_in.ap()),
            (s_pfm[:], "s_pfm", pfxmask.ap()),
            (s_rb1c[:], "s_rb1c", rb1_in.ap().rearrange("(a p) o -> p (a o)", p=128)),
            (s_rb2r[:], "s_rb2r", rb2_in.ap()),
        ][:0]:
            pass
        items = [
            (s_wf[:], "s_wf", w_ip_x.ap().rearrange("(a p) m -> p a m", p=128)),
            (s_lnwc[:], "s_lnwc", ln_w.ap().rearrange("o (a p) -> p (o a)", p=128)),
            (s_lnbc[:], "s_lnbc", ln_b.ap().rearrange("o (a p) -> p (o a)", p=128)),
            (s_bxr[:], "s_bxr", b_ip_x.ap()),
            (s_bgr[:], "s_bgr", b_ip_g.ap()),
            (s_cw[:], "s_cw", conv_w.ap().rearrange("(a p) k -> p a k", p=128)),
            (s_cbc[:], "s_cbc", conv_b.ap().rearrange("(a p) o -> p (a o)", p=128)),
            (s_dpb[:], "s_dpb", dtproj_b.ap().rearrange("(a p) o -> p (a o)", p=128)),
            (s_A[:], "s_A", a_log.ap().rearrange("(a p) n -> p a n", p=128)),
            (s_D[:], "s_D", d_in.ap().rearrange("(a p) o -> p (a o)", p=128)),
            (s_ones1[:], "s_ones1", c_ones1.ap()),
            (s_onesr[:], "s_onesr", c_onesr.ap()),
            (s_idf[:], "s_idf", c_idf.ap()),
            (s_trib[:], "s_trib", c_trib.ap()),
            (s_idb[:], "s_idb", c_idb.ap()),
            (s_onec[:], "s_onec", c_onescol.ap()),
            (s_wobr[:], "s_wobr", wob_in.ap()),
            (s_pfm[:], "s_pfm", pfxmask.ap()),
            (s_rb1c[:], "s_rb1c", rb1_in.ap().rearrange("(a p) o -> p (a o)", p=128)),
            (s_rb2r[:], "s_rb2r", rb2_in.ap()),
        ]
        def ldall(e):
            return [e.dma_start(out=d, in_=s2, allow_slow_non_contiguous=True)
                    for d, _k, s2 in items]
        E_(ldall, writes=[k for _d, k, _s in items], dma=True, dkey="ldc")

    @A.on("gpsimd")
    def _():
        def ldg(e):
            return [
                e.dma_start(out=s_dpw[:], in_=dtproj_w.ap()),
                e.dma_start(out=s_xpw[:], in_=xproj_w.ap().rearrange("(a p) m -> p a m", p=128)),
                e.dma_start(out=s_rw2b[:], in_=rw2_in.ap().rearrange("(a p) m -> p a m", p=128)),
                e.dma_start(out=wo_loc.ap(), in_=wo_shard.ap()),
                e.dma_start(out=rw1_loc.ap(), in_=rw1_shard.ap()),
                e.dma_start(out=myw1_bf.ap(), in_=myw1.ap()),
                e.dma_start(out=myw2_bf.ap(), in_=myw2.ap()),
            ]
        E_(ldg, writes=["s_dpw", "s_xpw", "s_rw2b", "wo_loc", "rw1_loc",
                        "myw1_bf", "myw2_bf"], dma=True, dkey="ldg")

    @A.on("scalar")
    def _():
        for m in range(NET):
            E_(lambda e, m=m: e.activation(s_A[:, m, :], s_A[:, m, :], ACT.Exp),
               reads=["s_A"], writes=["s_A"])
    @A.on("vector")
    def _():
        E_(lambda e: e.tensor_scalar_mul(s_A[:], s_A[:], -1.0),
           reads=["s_A"], writes=["s_A"])

    for i in range(NTT):
        @A.on("sync")
        def _(i=i):
            E_(lambda e: e.dma_start(out=s_xr[:, :],
                                     in_=x_rows.ap()[128 * i:128 * (i + 1), :]),
               writes=["s_xr"], dma=True)
        @A.on("vector")
        def _(i=i):
            E_(lambda e: e.tensor_copy(s_xrb[:, :], s_xr[:, :]),
               reads=["s_xr"], writes=["s_xrb"])
            E_(lambda e: e.reduce_sum(s_mu[:, i:i + 1], s_xr[:, :], axis=AX.X),
               reads=["s_xr"], writes=["s_mu"])
        @A.on("scalar")
        def _(i=i):
            E_(lambda e: e.activation(s_scr[:], s_xr[:, :], ACT.Square,
                                      accum_out=s_sq[:, i:i + 1]),
               reads=["s_xr"], writes=["s_scr", "s_sq"])
        @A.on("sync")
        def _(i=i):
            E_(lambda e: e.dma_start(
                out=xbf_loc.ap()[128 * i:128 * (i + 1), :], in_=s_xrb[:, :]),
               reads=["s_xrb"], writes=[f"xbf_loc{(128 * i) * CH4 // p.TC}"],
               dma=True, dkey="xbl")
    @A.on("vector")
    def _():
        E_(lambda e: e.tensor_scalar_mul(s_mu[:], s_mu[:], 1.0 / H),
           reads=["s_mu"], writes=["s_mu"])
        E_(lambda e: e.tensor_scalar_mul(s_sq[:], s_sq[:], 1.0 / H),
           reads=["s_sq"], writes=["s_sq"])
        E_(lambda e: e.tensor_mul(s_t0[:], s_mu[:], s_mu[:]),
           reads=["s_mu"], writes=["s_t0"])
        E_(lambda e: e.tensor_sub(s_var[:], s_sq[:], s_t0[:]),
           reads=["s_sq", "s_t0"], writes=["s_var"])
        E_(lambda e: e.tensor_scalar_add(s_t0[:], s_var[:], EPS),
           reads=["s_var"], writes=["s_t0"])
    @A.on("scalar")
    def _():
        E_(lambda e: e.activation(s_r1[:], s_t0[:], ACT.Ln),
           reads=["s_t0"], writes=["s_r1"])
        E_(lambda e: e.activation(s_r1[:], s_r1[:], ACT.Exp, scale=-0.5),
           reads=["s_r1"], writes=["s_r1"])
    @A.on("vector")
    def _():
        E_(lambda e: e.tensor_mul(s_t0[:], s_var[:], s_r1[:]),
           reads=["s_var", "s_r1"], writes=["s_t0"])
        E_(lambda e: e.tensor_mul(s_t0[:], s_t0[:], s_r1[:]),
           reads=["s_t0", "s_r1"], writes=["s_t0"])
        E_(lambda e: e.tensor_scalar_add(s_t0[:], s_t0[:], EPS),
           reads=["s_t0"], writes=["s_t0"])
    @A.on("scalar")
    def _():
        E_(lambda e: e.activation(s_t0[:], s_t0[:], ACT.Ln),
           reads=["s_t0"], writes=["s_t0"])
        E_(lambda e: e.activation(s_t0[:], s_t0[:], ACT.Exp, scale=-0.5),
           reads=["s_t0"], writes=["s_t0"])
    @A.on("vector")
    def _():
        E_(lambda e: e.tensor_mul(s_ac[:, 0:NTT], s_r1[:], s_t0[:]),
           reads=["s_r1", "s_t0"], writes=["s_ac"])
        E_(lambda e: e.tensor_mul(s_ac[:, NTT:2 * NTT], s_ac[:, 0:NTT], s_mu[:]),
           reads=["s_ac", "s_mu"], writes=["s_ac"])
        E_(lambda e: e.tensor_scalar_mul(s_ac[:, NTT:2 * NTT], s_ac[:, NTT:2 * NTT], -1.0),
           reads=["s_ac"], writes=["s_ac"])
    @A.on("tensor")
    def _():
        E_(lambda e: e.transpose(ps_s[0:2 * NTT, 0:128], s_ac[:, 0:2 * NTT], s_idf[:]),
           reads=["s_ac", "s_idf"], writes=["ps_s"])
    @A.on("scalar")
    def _():
        E_(lambda e: e.copy(s_acT[:, :], ps_s[0:2 * NTT, 0:128]),
           reads=["ps_s"], writes=["s_acT"])
    @A.on("sync")
    def _():
        E_(lambda e: e.dma_start(out=st_loc.ap().rearrange("j (i q) -> (j i) q", q=128),
                                 in_=s_acT[:, :]),
           reads=["s_acT"], writes=["st_loc"], dma=True)
    @A.on("gpsimd")
    def _():
        for j4 in range(CH4):
            E_(lambda e, j4=j4: e.collective_compute(
                "AllGather", ALU.bypass, replica_groups=RG,
                ins=[xbf_loc.ap()[CSZ * j4:CSZ * (j4 + 1), :]],
                outs=[xbf.ap()[NCOR * CSZ * j4:NCOR * CSZ * (j4 + 1), :]]),
               reads=[f"xbf_loc{j4}"], writes=[f"xbf{j4}"], cc=True, dkey=f"agx{j4}")
            if j4 == 0:
                E_(lambda e: e.collective_compute(
                    "AllGather", ALU.bypass, replica_groups=RG,
                    ins=[st_loc.ap()], outs=[st_all.ap()]),
                   reads=["st_loc"], writes=["st_all"], cc=True)
        E_(lambda e: e.collective_compute("AllGather", ALU.bypass, replica_groups=RG,
                                          ins=[wo_loc.ap()], outs=[wo_bf.ap()]),
           reads=["wo_loc"], writes=["wo_bf"], cc=True)
        E_(lambda e: e.collective_compute("AllGather", ALU.bypass, replica_groups=RG,
                                          ins=[rw1_loc.ap()], outs=[rw1_bf.ap()]),
           reads=["rw1_loc"], writes=["rw1_bf"], cc=True)

    NZB = 16
    s_zz = sbP("s_zz", [128, NZB, H], BF16)
    @A.on("vector")
    def _():
        E_(lambda e: e.memset(s_zz[:, :, :], 0.0), writes=["s_zz"])
    @A.on("sync")
    def _():
        nrows = (NSLOT + 128) // 128
        j = 0
        while j < nrows:
            nb = min(NZB, nrows - j)
            E_(lambda e, j=j, nb=nb: e.dma_start(
                out=stg.ap()[128 * j:128 * (j + nb), :].rearrange("(a p) m -> p a m", p=128),
                in_=s_zz[:, 0:nb, :]),
               reads=["s_zz"], writes=["stg"], dma=True, dkey="stg_z")
            j += nb

    def fold_w(tag, s_out, obuf, s1_row, s2_row, b_row, bkey, w_dram):
        if tag == "g":
            @A.on("sync")
            def _():
                E_(lambda e: e.dma_start(out=s_wf[:],
                                         in_=w_dram.ap().rearrange("(a p) m -> p a m", p=128)),
                   writes=["s_wf"], dma=True)
        @A.on("tensor")
        def _():
            def mm_rows(e):
                for ki in range(HK):
                    e.matmul(ps_s[0:1, 0:EC], s_lnwc[:, ki:ki + 1], s_wf[:, ki, :],
                             start=(ki == 0), stop=(ki == HK - 1))
                for ki in range(HK):
                    last = e.matmul(ps_s[0:1, EC:2 * EC], s_lnbc[:, ki:ki + 1],
                                    s_wf[:, ki, :],
                                    start=(ki == 0), stop=(ki == HK - 1))
                return last
            E_(mm_rows, reads=["s_lnwc", "s_lnbc", "s_wf"], writes=["ps_s"])
        @A.on("scalar")
        def _():
            E_(lambda e: e.copy(s1_row[:], ps_s[0:1, 0:EC]),
               reads=["ps_s"], writes=[f"s1{tag}"])
            E_(lambda e: e.copy(s2_row[:], ps_s[0:1, EC:2 * EC]),
               reads=["ps_s"], writes=[f"s2{tag}"])
        @A.on("vector")
        def _():
            E_(lambda e: e.tensor_add(s2_row[:], s2_row[:], b_row[:]),
               reads=[f"s2{tag}", bkey], writes=[f"s2{tag}"])
        s12 = s_s12x if tag == "x" else s_s12g
        @A.on("gpsimd")
        def _():
            E_(lambda e: e.dma_start(out=s12[0:1, :], in_=s1_row[:]),
               reads=[f"s1{tag}"], writes=[f"s_s12{tag}"], dma=True)
            E_(lambda e: e.dma_start(out=s12[1:2, :], in_=s2_row[:]),
               reads=[f"s2{tag}"], writes=[f"s_s12{tag}"], dma=True)
            for ki in range(HK):
                E_(lambda e, ki=ki: e.tensor_scalar(s_out[:, ki, :], s_wf[:, ki, :],
                                                    s_lnwc[:, ki:ki + 1], None, ALU.mult),
                   reads=["s_wf", "s_lnwc"], writes=[obuf])

    fold_w("x", s_wxb, "s_wxb", s_s1x, s_s2x, s_bxr, "s_bxr", w_ip_x)
    fold_w("g", s_wgb, "s_wgb", s_s1g, s_s2g, s_bgr, "s_bgr", w_ip_g)

    A.fence()
    pctx.close()

    dctx = contextlib.ExitStack()
    sbD = mk_sb(dctx)
    s_xT = sbD("s_xT", [128, HK, TS], BF16)
    s_u0 = sbD("s_u0", [128, TS + KC - 1])
    s_cvt = sbD("s_cvt", [128, TS])
    s_xsb = sbD("s_xsb", [128, 2, NET, TS], BF16)
    s_dt = sbD("s_dt", [128, 2, NET, TS], BF16)
    s_dtx = sbD("s_dtx", [128, NET, TS], BF16)
    s_gs = sbD("s_gs", [128, 2, NET, TS], BF16)
    s_ab = sbD("s_ab", [128, TS], BF16)
    s_db64 = sbD("s_db64", [64, TS], BF16)
    s_h = sbD("s_h", [128, TS], BF16)
    s_yac = sbD("s_yac", [128, NET, TS], BF16)
    s_pb = sbD("s_pb", [128, TS], BF16)
    s_cb = sbD("s_cb", [128, 2, TS], BF16)
    s_co = sbD("s_co", [2, TS], BF16)
    s_dA2 = sbD("s_dA2", [128, TS])
    s_bb = sbD("s_bb", [128, 2, TS], BF16)
    s_ub = sbD("s_ub", [128, TS], BF16)
    s_dts = sbD("s_dts", [128, NCK], BF16)
    s_dA = s_u0
    s_u = s_cvt
    s_dbst = s_cvt

    @A.on("sync")
    def _():
        E_(lambda e: e.dma_start(out=s_co[1:2, :], in_=c_onesrow.ap()),
           writes=["s_co1"], dma=True)


    def chs(ch):
        return slice(ch * NCK, (ch + 1) * NCK)

    def load_xT(s):
        @A.on("sync")
        def _():
            for ch in range(NCH):
                tok0 = ch * NCK
                j4 = tok0 // CSZ
                off = tok0 % CSZ
                row0 = j4 * NCOR * CSZ + s * CSZ + off
                for ki in range(HK):
                    E_(lambda e, ki=ki, ch=ch, row0=row0: e.dma_start(
                        out=s_xT[:, ki, chs(ch)],
                        in_=xbf.ap()[row0:row0 + NCK, 128 * ki:128 * (ki + 1)],
                        transpose=True),
                       reads=[f"xbf{j4}"], writes=["s_xT"], dma=True)

    def emit_abcast(s):
        @A.on("gpsimd")
        def _():
            E_(lambda e, s=s: e.dma_start(
                out=s_ab[:, :],
                in_=st_all.ap()[2 * s:2 * s + 1, :].to_broadcast([128, TS])),
               reads=["st_all"], writes=["s_ab"], dma=True)
            E_(lambda e, s=s: e.dma_start(
                out=s_co[0:1, :], in_=st_all.ap()[2 * s + 1:2 * s + 2, :]),
               reads=["st_all"], writes=["s_co0"], dma=True)
    def emit_conv(s, m):
        par = s % 2
        @A.on("vector")
        def _():
            if s % p.SLB == 0:
                E_(lambda e: e.memset(s_u0[:, 0:KC - 1], 0.0), writes=["s_u0"])
            else:
                E_(lambda e, m=m: e.tensor_copy(s_u0[:, 0:KC - 1], s_halo[:, m, :]),
                   reads=[f"s_halo_{m}"], writes=["s_u0"])
            E_(lambda e, m=m: e.tensor_copy(s_halo[:, m, :], s_u0[:, TS:TS + KC - 1]),
               reads=["s_u0"], writes=[f"s_halo_{m}"])
            E_(lambda e, m=m: e.tensor_scalar(s_cvt[:, :], s_u0[:, 0:TS],
                                              s_cw[:, m, 0:1], None, ALU.mult),
               reads=["s_u0", "s_cw"], writes=["s_cvt"])
            for k in range(1, KC):
                E_(lambda e, k=k, m=m: e.scalar_tensor_tensor(
                    s_cvt[:, :], s_u0[:, k:k + TS], s_cw[:, m, k:k + 1],
                    s_cvt[:, :], ALU.mult, ALU.add),
                   reads=["s_u0", "s_cw", "s_cvt"], writes=["s_cvt"])
        @A.on("scalar")
        def _():
            E_(lambda e, m=m, par=par: e.activation(
                s_xsb[:, par, m, :], s_cvt[:, :], ACT.Silu, bias=s_cbc[:, m:m + 1]),
               reads=["s_cvt", "s_cbc"], writes=[f"s_xsb{par}"])

    def emit_proj(s, kind):
        par = s % 2
        wb, wkey = (s_wxb, "s_wxb") if kind == "x" else (s_wgb, "s_wgb")
        s12, s12k = (s_s12x, "s_s12x") if kind == "x" else (s_s12g, "s_s12g")
        for m in range(NET):
            for ch in range(NCH):
                bank = (m * NCH + ch) % 2
                pm, pk = ps_m[bank], f"ps_m{bank}"
                @A.on("tensor")
                def _(m=m, ch=ch, pm=pm, pk=pk):
                    def mms(e):
                        for ki in range(HK):
                            e.matmul(pm[:, 0:NCK], wb[:, ki, 128 * m:128 * (m + 1)],
                                     s_xT[:, ki, chs(ch)], start=(ki == 0), stop=False)
                        return e.matmul(pm[:, 0:NCK], s12[0:2, 128 * m:128 * (m + 1)],
                                        s_co[0:2, chs(ch)], start=False, stop=True)
                    E_(mms, reads=[wkey, "s_xT", s12k, "s_co0", "s_co1"],
                       writes=[pk])
                @A.on("vector")
                def _(m=m, ch=ch, pm=pm, pk=pk):
                    if kind == "x":
                        E_(lambda e: e.tensor_tensor(
                            s_u0[:, KC - 1 + ch * NCK: KC - 1 + (ch + 1) * NCK],
                            pm[:, 0:NCK], s_ab[:, chs(ch)], ALU.mult),
                           reads=[pk, "s_ab"], writes=["s_u0"])
                    else:
                        E_(lambda e: e.tensor_tensor(
                            s_cvt[:, chs(ch)], pm[:, 0:NCK], s_ab[:, chs(ch)], ALU.mult),
                           reads=[pk, "s_ab"], writes=["s_cvt"])
            if kind == "x":
                emit_conv(s, m)
            else:
                @A.on("scalar")
                def _(m=m, par=par):
                    E_(lambda e: e.activation(
                        s_gs[:, par, m, :], s_cvt[:, :], ACT.Sigmoid),
                       reads=["s_cvt"], writes=[f"s_gs{par}"])

    def emit_dbc(s):
        par = s % 2
        for ch in range(NCH):
            @A.on("tensor")
            def _(ch=ch):
                def mms(e):
                    e.matmul(ps_db[:, 0:NCK], s_xpw[:, 0, :],
                             s_xsb[:, par, 0, chs(ch)], start=True, stop=(NET == 1))
                    if NET > 1:
                        return e.matmul(ps_db[:, 0:NCK], s_xpw[:, 1, :],
                                        s_xsb[:, par, 1, chs(ch)],
                                        start=False, stop=True)
                E_(mms, reads=["s_xpw", f"s_xsb{par}"], writes=["ps_db"])
            @A.on("scalar")
            def _(ch=ch):
                E_(lambda e: e.copy(s_dbst[0:96, chs(ch)], ps_db[:, 0:NCK]),
                   reads=["ps_db"], writes=["s_cvt"])
        @A.on("sync")
        def _():
            E_(lambda e, s=s: e.dma_start(out=dbc_par.ap()[s], in_=s_dbst[0:96, :]),
               reads=["s_cvt"], writes=["dbc_par"], dma=True)
        @A.on("gpsimd")
        def _():
            E_(lambda e, s=s: e.collective_compute(
                "AllReduce", ALU.add, replica_groups=RG,
                ins=[dbc_par.ap()[s]], outs=[dbc_sum.ap()[s]]),
               reads=["dbc_par"], writes=["dbc_sum"], cc=True)
        if debug:
            @A.on("sync")
            def _():
                E_(lambda e, s=s: e.dma_start(out=outs["o_dbc"].ap()[:, s * TS:(s + 1) * TS],
                                              in_=dbc_sum.ap()[s]),
                   reads=["dbc_sum"], writes=["o_dbc"], dma=True)

    def emit_M(s):
        load_xT(s)
        emit_abcast(s)
        emit_proj(s, "x")
        emit_proj(s, "g")
        emit_dbc(s)
        if debug:
            @A.on("sync")
            def _():
                E_(lambda e, s=s: e.dma_start(
                    out=outs["o_xscan"].ap().rearrange("(a p) t -> p a t", p=128)[:, :, s * TS:(s + 1) * TS],
                    in_=s_xsb[:, s % 2, :, :]),
                   reads=[f"s_xsb{s % 2}"], writes=["o_xscan"], dma=True)

    def dt_pieces(s):
        par = s % 2
        pieces = []
        def head():
            @A.on("gpsimd")
            def _():
                E_(lambda e, s=s: e.dma_start(out=s_db64[:, :], in_=dbc_sum.ap()[s][0:64, :]),
                   reads=["dbc_sum"], writes=["s_db64"], dma=True)
                E_(lambda e, s=s: e.dma_start(out=dbc_bf.ap()[s % 2],
                                              in_=dbc_sum.ap()[s][64:96, :]),
                   reads=["dbc_sum"], writes=[f"dbc_bf{s % 2}"], dma=True)
        pieces.append(head)
        def chunk(m, ch):
            bank = (m * NCH + ch) % 2
            pm, pk = ps_m[bank], f"ps_m{bank}"
            @A.on("tensor")
            def _(m=m, ch=ch, pm=pm, pk=pk):
                E_(lambda e: e.matmul(
                    pm[:, 0:NCK], s_dpw[:, 128 * m:128 * (m + 1)],
                    s_db64[:, chs(ch)], start=True, stop=True),
                   reads=["s_dpw", "s_db64"], writes=[pk])
            @A.on("scalar")
            def _(m=m, ch=ch, pm=pm, pk=pk):
                E_(lambda e: e.activation(
                    s_u[:, 0:NCK], pm[:, 0:NCK], ACT.Abs, bias=s_dpb[:, m:m + 1]),
                   reads=[pk, "s_dpb"], writes=["s_cvt"])
                E_(lambda e: e.activation(s_u[:, 0:NCK], s_u[:, 0:NCK], ACT.Exp,
                                          scale=-1.0),
                   reads=["s_cvt"], writes=["s_cvt"])
                E_(lambda e: e.activation(s_u[:, 0:NCK], s_u[:, 0:NCK], ACT.Ln,
                                          bias=1.0),
                   reads=["s_cvt"], writes=["s_cvt"])
                E_(lambda e: e.activation(
                    s_dts[:, :], pm[:, 0:NCK], ACT.Relu, bias=s_dpb[:, m:m + 1]),
                   reads=[pk, "s_dpb"], writes=["s_dts"])
            @A.on("vector")
            def _(m=m, ch=ch, par=par):
                v2 = lambda ap: ap.rearrange("p (a b) -> p a b", a=2)
                E_(lambda e: e.tensor_tensor(
                    v2(s_dt[:, par, m, chs(ch)]), v2(s_u[:, 0:NCK]),
                    v2(s_dts[:, :]), ALU.add),
                   reads=["s_cvt", "s_dts"], writes=[f"s_dt{par}"])
        for m in range(NET):
            for ch in range(NCH):
                pieces.append(lambda m=m, ch=ch: chunk(m, ch))
        return pieces

    def emit_dt_debug(s):
        par = s % 2
        if debug:
            @A.on("sync")
            def _():
                E_(lambda e, s=s: e.dma_start(
                    out=outs["o_dt"].ap().rearrange("(a p) t -> p a t", p=128)[:, :, s * TS:(s + 1) * TS],
                    in_=s_dt[:, par, :, :]),
                   reads=[f"s_dt{par}"], writes=["o_dt"], dma=True)

    def emit_scan(s, inject=()):
        par = s % 2
        inject = list(inject)
        @A.on("vector")
        def _():
            v2 = lambda ap: ap.rearrange("p (a b) -> p a b", a=2)
            for m in range(NET):
                E_(lambda e, m=m: e.tensor_tensor(
                    v2(s_dtx[:, m, :]), v2(s_dt[:, par, m, :]),
                    v2(s_xsb[:, par, m, :]), ALU.mult),
                   reads=[f"s_dt{par}", f"s_xsb{par}"], writes=["s_dtx"])
            if s % p.SLB == 0:
                E_(lambda e: e.memset(s_hst[:], 0.0), writes=["s_hst"])
        for n in range(NST):
            if NST - len(inject) <= n:
                inject.pop(0)()
            npar = n % 2
            @A.on("sync")
            def _(n=n, npar=npar):
                E_(lambda e: e.dma_start(
                    out=s_bb[:, npar, :],
                    in_=dbc_bf.ap()[par, n:n + 1, :].to_broadcast([128, TS])),
                   reads=[f"dbc_bf{par}"], writes=[f"s_bb{npar}"], dma=True)
            @A.on("scalar")
            def _(n=n, npar=npar):
                E_(lambda e: e.dma_start(
                    out=s_cb[:, npar, :],
                    in_=dbc_bf.ap()[par, 16 + n:17 + n, :].to_broadcast([128, TS])),
                   reads=[f"dbc_bf{par}"], writes=[f"s_cb{npar}"], dma=True)
            dabs = [(s_dA[:, 0:TS], "s_u0"), (s_dA2[:, :], "s_dA2")][:NET]
            @A.on("scalar")
            def _(n=n, dabs=dabs):
                def exps(e):
                    r = None
                    for m, (dAb, dAk) in enumerate(dabs):
                        r = e.activation(dAb, s_dt[:, par, m, :], ACT.Exp,
                                         scale=s_A[:, m, n:n + 1])
                    return r
                E_(exps, reads=[f"s_dt{par}", "s_A"],
                   writes=[k for _, k in dabs])
            for m in range(NET):
                dAb, dAk = dabs[m]
                v2 = lambda ap: ap.rearrange("p (a b) -> p a b", a=2)
                @A.on("vector")
                def _(n=n, m=m, dAb=dAb, dAk=dAk, npar=npar, v2=v2):
                    E_(lambda e: e.tensor_tensor(
                        v2(s_ub[:, :]), v2(s_dtx[:, m, :]), v2(s_bb[:, npar, :]),
                        ALU.mult),
                       reads=["s_dtx", f"s_bb{npar}"], writes=["s_ub"])
                    E_(lambda e: e.tensor_tensor_scan(
                        s_h[:, :], dAb, s_ub[:, :], s_hst[:, m, n:n + 1],
                        ALU.mult, ALU.add),
                       reads=[dAk, "s_ub", "s_hst"], writes=["s_h"])
                v2 = lambda ap: ap.rearrange("p (a b) -> p a b", a=2)
                @A.on("vector")
                def _(n=n, m=m, npar=npar, v2=v2):
                    E_(lambda e: e.tensor_copy(s_hst[:, m, n:n + 1], s_h[:, TS - 1:TS]),
                       reads=["s_h"], writes=["s_hst"])
                    if n == 0:
                        E_(lambda e: e.tensor_tensor(
                            v2(s_yac[:, m, :]), v2(s_h[:, :]), v2(s_cb[:, npar, :]),
                            ALU.mult),
                           reads=["s_h", f"s_cb{npar}"], writes=[f"s_yac{m}"])
                    else:
                        E_(lambda e: e.tensor_tensor(
                            v2(s_pb[:, :]), v2(s_h[:, :]), v2(s_cb[:, npar, :]),
                            ALU.mult),
                           reads=["s_h", f"s_cb{npar}"], writes=["s_pb"])
                        E_(lambda e: e.tensor_tensor(
                            v2(s_yac[:, m, :]), v2(s_yac[:, m, :]), v2(s_pb[:, :]),
                            ALU.add),
                           reads=[f"s_yac{m}", "s_pb"], writes=[f"s_yac{m}"])

    def emit_ytail(s):
        par = s % 2
        @A.on("vector")
        def _():
            v2 = lambda ap: ap.rearrange("p (a b) -> p a b", a=2)
            for m in range(NET):
                E_(lambda e, m=m: e.scalar_tensor_tensor(
                    v2(s_yac[:, m, :]), v2(s_xsb[:, par, m, :]), s_D[:, m:m + 1],
                    v2(s_yac[:, m, :]), ALU.mult, ALU.add),
                   reads=[f"s_xsb{par}", "s_D", f"s_yac{m}"], writes=[f"s_yac{m}"])
                E_(lambda e, m=m: e.tensor_tensor(
                    v2(s_yac[:, m, :]), v2(s_yac[:, m, :]), v2(s_gs[:, par, m, :]),
                    ALU.mult),
                   reads=[f"s_yac{m}", f"s_gs{par}"], writes=[f"s_yac{m}"])
        @A.on("gpsimd")
        def _():
            E_(lambda e, s=s: e.dma_start(
                out=a2a_in.ap()[s].rearrange("(a p) t -> p a t", p=128),
                in_=s_yac[:, :, :]),
               reads=["s_yac0", "s_yac1"], writes=["a2a_in"], dma=True)

    emit_M(0)
    for piece in dt_pieces(0):
        piece()
    emit_dt_debug(0)
    for s in range(NSL):
        if s + 1 < NSL:
            emit_M(s + 1)
            emit_scan(s, inject=dt_pieces(s + 1))
            emit_dt_debug(s + 1)
        else:
            emit_scan(s)
        emit_ytail(s)

    @A.on("gpsimd")
    def _():
        with allow_ici_all_to_all():
            E_(lambda e: e.collective_compute(
                "AllToAll", ALU.bypass, replica_groups=RG,
                ins=[a2a_in.ap()], outs=[a2a_out.ap()]),
               reads=["a2a_in"], writes=["a2a_out"], cc=True)
    if debug:
        @A.on("sync")
        def _():
            E_(lambda e: e.dma_start(out=outs["o_yg"].ap(), in_=a2a_out.ap()),
               reads=["a2a_out"], writes=["o_yg"], dma=True)

    A.fence()
    dctx.close()
    mctx.close()

    NTT8 = NTT * 8
    fctx = contextlib.ExitStack()
    sbF = mk_sb(fctx)
    psF = lambda n, s, d=F32: fctx.enter_context(nc.psum_tensor(n, s, d))
    ps_f = psF("ps_f", [96, 512])
    ps_f2 = [psF("ps_f2_0", [128, 512]), psF("ps_f2_1", [128, 512])]
    ps_t = psF("ps_t", [128, 512], BF16)
    s_m3 = sbF("s_m3", [128, NTT])
    s_q3 = sbF("s_q3", [128, NTT])
    s_r3 = sbF("s_r3", [128, NTT])
    s_mgA = sbF("s_mgA", [128, NTT])
    s_mgB = sbF("s_mgB", [128, NTT])
    s_wA = sbF("s_wA", [128, NTT])
    s_wB = sbF("s_wB", [128, NTT])
    s_edt = sbF("s_edt", [128, NTT])
    s_siA = sbF("s_siA", [128, NTT], I32)
    s_siB = sbF("s_siB", [128, NTT], I32)
    s_giA = sbF("s_giA", [128, NTT], I32)
    s_giB = sbF("s_giB", [128, NTT], I32)
    s_kA = sbF("s_kA", [128, NTT])
    s_kB = sbF("s_kB", [128, NTT])
    s_tvA = sbF("s_tvA", [128, NTT])
    s_tvB = sbF("s_tvB", [128, NTT])
    s_g0 = sbF("s_g0", [128, 2, H], BF16)
    s_g1 = sbF("s_g1", [128, 2, H], BF16)
    s_ot = sbF("s_ot", [128, 2, H])
    s_wE = sbF("s_wE", [128, 2 * 4096], BF16)
    @A.on("sync")
    def _():
        for k in range(2):
            E_(lambda e, k=k: e.dma_start(
                out=s_wE[:, 4096 * k:4096 * (k + 1)],
                in_=myw1_bf.ap()[128 * k:128 * (k + 1), :]),
               reads=["myw1_bf"], writes=["s_wE"], dma=True)
    if debug:
        s_dbg = sbF("s_dbg", [128, NTT, 8])
    f2ctx = contextlib.ExitStack()
    sbF2 = mk_sb(f2ctx)
    s_io8 = sbF2("s_io8", [128, NTT, 8])
    s_mx1 = sbF2("s_mx1", [128, NTT])
    s_mx2 = sbF2("s_mx2", [128, NTT])
    s_i1 = sbF2("s_i1", [128, NTT])
    s_i2 = sbF2("s_i2", [128, NTT])
    s_ohA = sbF2("s_ohA", [128, NTT, 8])
    s_ohB = sbF2("s_ohB", [128, NTT, 8])
    s_ohAb = sbF2("s_ohAb", [128, NTT, 8], BF16)
    s_ohBb = sbF2("s_ohBb", [128, NTT, 8], BF16)
    s_sel = sbF2("s_sel", [128, NTT, 8])
    s_S = sbF2("s_S", [128, NTT, 8])
    s_pfA = sbF2("s_pfA", [128, NTT, 8])
    s_pfB = sbF2("s_pfB", [128, NTT, 8])
    s_off = sbF2("s_off", [1, 2 * NTT * 8])
    s_run = sbF2("s_run", [1, 16])
    s_offb = sbF2("s_offb", [128, 2 * NTT * 8])
    s_cnt = sbF2("s_cnt", [1, 8])
    s_ca = sbF2("s_ca", [8, 8])
    s_gb = sbF2("s_gb", [1, 8])
    s_gbr = sbF2("s_gbr", [1, NTT * 8])
    s_gbb = sbF2("s_gbb", [128, NTT * 8])
    fxctx = contextlib.ExitStack()
    s_x3b = mk_sb(fxctx)("s_x3b", [128, NTT, H], BF16)
    f0ctx = contextlib.ExitStack()
    sbF0 = mk_sb(f0ctx)
    s_yg2 = sbF0("s_yg2", [128, 16, p.TC], BF16)
    s_wo = sbF0("s_wo", [128, 16, H], BF16)
    s_xrf = sbF0("s_xrf", [128, H])
    s_x1t = sbF0("s_x1t", [128, H])
    s_zt = s_ot[:, 0, :]

    @A.on("sync")
    def _():
        for k in range(16):
            E_(lambda e, k=k: e.dma_start(out=s_yg2[:, k, :],
                                          in_=a2a_out.ap()[128 * k:128 * (k + 1), :]),
               reads=["a2a_out"], writes=["s_yg2"], dma=True)
        for k in range(16):
            E_(lambda e, k=k: e.dma_start(out=s_wo[:, k, :],
                                          in_=wo_bf.ap()[128 * k:128 * (k + 1), :]),
               reads=["wo_bf"], writes=["s_wo"], dma=True)

    for tt in range(NTT):
        @A.on("sync")
        def _(tt=tt):
            E_(lambda e: e.dma_start(
                out=s_xrf[:, :],
                in_=x_rows.ap()[128 * tt:128 * (tt + 1), :]),
               writes=["s_xrf"], dma=True)
        for dc in range(2):
            pm, pk = ps_m[dc % 2], f"ps_m{dc % 2}"
            @A.on("tensor")
            def _(tt=tt, dc=dc, pm=pm, pk=pk):
                def mms(e):
                    for k in range(16):
                        e.matmul(pm[:, 0:512], s_yg2[:, k, 128 * tt:128 * (tt + 1)],
                                 s_wo[:, k, 512 * dc:512 * (dc + 1)],
                                 start=(k == 0), stop=False)
                    return e.matmul(pm[:, 0:512], s_ones1[:, :],
                                    s_wobr[0:1, 512 * dc:512 * (dc + 1)],
                                    start=False, stop=True)
                E_(mms, reads=["s_yg2", "s_wo", "s_wobr", "s_ones1"], writes=[pk])
            @A.on("vector")
            def _(tt=tt, dc=dc, pm=pm, pk=pk):
                E_(lambda e: e.tensor_add(s_x1t[:, 512 * dc:512 * (dc + 1)],
                                          pm[:, 0:512], s_xrf[:, 512 * dc:512 * (dc + 1)]),
                   reads=[pk, "s_xrf"], writes=["s_x1t"])
        @A.on("vector")
        def _(tt=tt):
            E_(lambda e: e.tensor_scalar(s_zt[:, :], s_xrf[:, :],
                                         s_mu[:, tt:tt + 1], s_r1[:, tt:tt + 1],
                                         ALU.subtract, ALU.mult),
               reads=["s_xrf", "s_mu", "s_r1"], writes=["s_ot"])
            E_(lambda e: e.tensor_add(s_x1t[:, :], s_x1t[:, :], s_zt[:, :]),
               reads=["s_x1t", "s_ot"], writes=["s_x1t"])
            E_(lambda e, tt=tt: e.reduce_sum(s_m3[:, tt:tt + 1], s_x1t[:, :], axis=AX.X),
               reads=["s_x1t"], writes=["s_m3"])
        @A.on("scalar")
        def _(tt=tt):
            E_(lambda e: e.activation(s_zt[:, :], s_x1t[:, :], ACT.Square,
                                      accum_out=s_q3[:, tt:tt + 1]),
               reads=["s_x1t"], writes=["s_ot", "s_q3"])
        @A.on("sync")
        def _(tt=tt):
            E_(lambda e: e.dma_start(out=x1_sp.ap()[128 * tt:128 * (tt + 1), :],
                                     in_=s_x1t[:, :]),
               reads=["s_x1t"], writes=["x1_sp"], dma=True)
    @A.on("vector")
    def _():
        E_(lambda e: e.tensor_scalar_mul(s_m3[:], s_m3[:], 1.0 / H),
           reads=["s_m3"], writes=["s_m3"])
        E_(lambda e: e.tensor_scalar_mul(s_q3[:], s_q3[:], 1.0 / H),
           reads=["s_q3"], writes=["s_q3"])
        E_(lambda e: e.tensor_mul(s_r3[:], s_m3[:], s_m3[:]),
           reads=["s_m3"], writes=["s_r3"])
        E_(lambda e: e.tensor_sub(s_r3[:], s_q3[:], s_r3[:]),
           reads=["s_q3", "s_r3"], writes=["s_r3"])
        E_(lambda e: e.tensor_scalar_add(s_r3[:], s_r3[:], EPS),
           reads=["s_r3"], writes=["s_r3"])
    @A.on("scalar")
    def _():
        E_(lambda e: e.activation(s_r3[:], s_r3[:], ACT.Ln),
           reads=["s_r3"], writes=["s_r3"])
        E_(lambda e: e.activation(s_r3[:], s_r3[:], ACT.Exp, scale=-0.5),
           reads=["s_r3"], writes=["s_r3"])
    for tt in range(NTT):
        @A.on("sync")
        def _(tt=tt):
            E_(lambda e: e.dma_start(out=s_x1t[:, :],
                                     in_=x1_sp.ap()[128 * tt:128 * (tt + 1), :]),
               reads=["x1_sp"], writes=["s_x1t"], dma=True)
        @A.on("vector")
        def _(tt=tt):
            E_(lambda e: e.tensor_scalar(s_x3b[:, tt, :], s_x1t[:, :],
                                         s_m3[:, tt:tt + 1], s_r3[:, tt:tt + 1],
                                         ALU.subtract, ALU.mult),
               reads=["s_x1t", "s_m3", "s_r3"], writes=["s_x3b"])
        @A.on("scalar")
        def _(tt=tt):
            E_(lambda e: e.dma_start(out=x3_sp.ap()[128 * tt:128 * (tt + 1), :],
                                     in_=s_x3b[:, tt, :]),
               reads=["s_x3b"], writes=[f"x3_sp{(128 * tt) // 512}"],
               dma=True, dkey="x3s")
    if debug:
        @A.on("sync")
        def _():
            E_(lambda e: e.dma_start(out=outs["o_x1"].ap(), in_=x1_sp.ap()),
               reads=["x1_sp"], writes=["o_x1"], dma=True)

    A.fence()
    f0ctx.close()

    f1ctx = contextlib.ExitStack()
    sbF1 = mk_sb(f1ctx)
    s_x3T = sbF1("s_x3T", [128, HK, p.TC], BF16)
    s_rw1 = sbF1("s_rw1", [128, HK, 4096], BF16)
    s_hr1 = sbF1("s_hr1", [128, 512], BF16)
    s_lg8 = sbF1("s_lg8", [8, p.TC])
    s_lgt = sbF1("s_lgt", [128, NTT, 8])

    @A.on("sync")
    def _():
        for k in range(HK):
            E_(lambda e, k=k: e.dma_start(out=s_rw1[:, k, :],
                                          in_=rw1_bf.ap()[128 * k:128 * (k + 1), :]),
               reads=["rw1_bf"], writes=["s_rw1"], dma=True)
    @A.on("sync")
    def _():
        NC2 = (p.TC + 511) // 512
        for c2 in range(NC2):
            def ldx3(e, c2=c2):
                w = min(512, p.TC - 512 * c2)
                return [e.dma_start(
                    out=s_x3T[:, ki, 512 * c2:512 * c2 + w],
                    in_=x3_sp.ap()[512 * c2:512 * c2 + w,
                                   128 * ki:128 * (ki + 1)],
                    transpose=True) for ki in range(HK)]
            E_(ldx3, reads=[f"x3_sp{c2}"], writes=[f"s_x3T{c2}"], dma=True,
               dkey=f"ldx3{c2 % 2}")
    NCK2 = min(512, p.TC)
    NCH2 = p.TC // NCK2
    for nch in range(NCH2):
        for mt in range(32):
            pm, pk = ps_m[mt % 2], f"ps_m{mt % 2}"
            @A.on("tensor")
            def _(nch=nch, mt=mt, pm=pm, pk=pk):
                def mms(e):
                    for k in range(HK):
                        r = e.matmul(pm[:, 0:NCK2], s_rw1[:, k, 128 * mt:128 * (mt + 1)],
                                     s_x3T[:, k, NCK2 * nch:NCK2 * (nch + 1)],
                                     start=(k == 0), stop=(k == HK - 1))
                    return r
                E_(mms, reads=["s_rw1", f"s_x3T{nch}"], writes=[pk])
            @A.on("vector")
            def _(nch=nch, mt=mt, pm=pm, pk=pk):
                E_(lambda e: e.tensor_scalar(s_hr1[:, 0:NCK2], pm[:, 0:NCK2],
                                             s_rb1c[:, mt:mt + 1], 0.0,
                                             ALU.add, ALU.max),
                   reads=[pk, "s_rb1c"], writes=["s_hr1"])
            @A.on("tensor")
            def _(nch=nch, mt=mt):
                def mm2(e):
                    r = e.matmul(ps_f[0:8, 0:NCK2], s_rw2b[:, mt, :], s_hr1[:, 0:NCK2],
                                 start=(mt == 0), stop=False)
                    if mt == 31:
                        r = e.matmul(ps_f[0:8, 0:NCK2], s_rb2r[0:1, :],
                                     s_onesr[0:1, 0:NCK2], start=False, stop=True)
                    return r
                E_(mm2, reads=["s_rw2b", "s_hr1", "s_rb2r", "s_onesr"], writes=["ps_f"])
        @A.on("scalar")
        def _(nch=nch):
            E_(lambda e: e.copy(s_lg8[:, NCK2 * nch:NCK2 * (nch + 1)], ps_f[0:8, 0:NCK2]),
               reads=["ps_f"], writes=["s_lg8"])
    for tt in range(NTT):
        @A.on("tensor")
        def _(tt=tt):
            E_(lambda e: e.transpose(ps_s[:, 0:8],
                                     s_lg8[0:8, 128 * tt:128 * (tt + 1)],
                                     s_idf[0:8, 0:8]),
               reads=["s_lg8", "s_idf"], writes=["ps_s"])
        @A.on("scalar")
        def _(tt=tt):
            E_(lambda e: e.copy(s_lgt[:, tt, :], ps_s[:, 0:8]),
               reads=["ps_s"], writes=["s_lgt"])
    if debug:
        @A.on("sync")
        def _():
            E_(lambda e: e.dma_start(
                out=outs["o_lg"].ap().rearrange("(a p) m -> p a m", p=128),
                in_=s_lgt[:, :, :]),
               reads=["s_lgt"], writes=["o_lg"], dma=True)

    lg3 = s_lgt[:, :, :]
    @A.on("gpsimd")
    def _():
        E_(lambda e: e.iota(s_io8[:, :, :], [[0, NTT], [1, 8]], channel_multiplier=0,
                            allow_small_or_imprecise_dtypes=True),
           writes=["s_io8"])
    @A.on("vector")
    def _():
        E_(lambda e: e.reduce_max(s_mx1[:, :], lg3, axis=AX.X),
           reads=["s_lgt"], writes=["s_mx1"])
        E_(lambda e: e.tensor_tensor(s_ohA[:, :, :], lg3,
                                     s_mx1[:, :].to_broadcast([128, NTT, 8]),
                                     ALU.is_equal),
           reads=["s_lgt", "s_mx1"], writes=["s_ohA"])
        E_(lambda e: e.scalar_tensor_tensor(s_sel[:, :, :], s_ohA[:, :, :], -99.0,
                                            s_io8[:, :, :], ALU.mult, ALU.add),
           reads=["s_ohA", "s_io8"], writes=["s_sel"])
        E_(lambda e: e.tensor_reduce(s_i1[:, :], s_sel[:, :, :], axis=AX.X, op=ALU.min),
           reads=["s_sel"], writes=["s_i1"])
        E_(lambda e: e.tensor_scalar_add(s_i1[:, :], s_i1[:, :], 99.0),
           reads=["s_i1"], writes=["s_i1"])
        E_(lambda e: e.tensor_tensor(s_ohA[:, :, :], s_io8[:, :, :],
                                     s_i1[:, :].to_broadcast([128, NTT, 8]),
                                     ALU.is_equal),
           reads=["s_io8", "s_i1"], writes=["s_ohA"])
        E_(lambda e: e.scalar_tensor_tensor(s_S[:, :, :], s_ohA[:, :, :], -1e30,
                                            lg3, ALU.mult, ALU.add),
           reads=["s_ohA", "s_lgt"], writes=["s_S"])
        E_(lambda e: e.reduce_max(s_mx2[:, :], s_S[:, :, :], axis=AX.X),
           reads=["s_S"], writes=["s_mx2"])
        E_(lambda e: e.tensor_tensor(s_ohB[:, :, :], s_S[:, :, :],
                                     s_mx2[:, :].to_broadcast([128, NTT, 8]),
                                     ALU.is_equal),
           reads=["s_S", "s_mx2"], writes=["s_ohB"])
        E_(lambda e: e.scalar_tensor_tensor(s_sel[:, :, :], s_ohB[:, :, :], -99.0,
                                            s_io8[:, :, :], ALU.mult, ALU.add),
           reads=["s_ohB", "s_io8"], writes=["s_sel"])
        E_(lambda e: e.tensor_reduce(s_i2[:, :], s_sel[:, :, :], axis=AX.X, op=ALU.min),
           reads=["s_sel"], writes=["s_i2"])
        E_(lambda e: e.tensor_scalar_add(s_i2[:, :], s_i2[:, :], 99.0),
           reads=["s_i2"], writes=["s_i2"])
        E_(lambda e: e.tensor_tensor(s_ohB[:, :, :], s_io8[:, :, :],
                                     s_i2[:, :].to_broadcast([128, NTT, 8]),
                                     ALU.is_equal),
           reads=["s_io8", "s_i2"], writes=["s_ohB"])
        E_(lambda e: e.tensor_mul(s_sel[:, :, :], s_ohA[:, :, :], lg3),
           reads=["s_ohA", "s_lgt"], writes=["s_sel"])
        E_(lambda e: e.reduce_sum(s_tvA[:, :], s_sel[:, :, :], axis=AX.X),
           reads=["s_sel"], writes=["s_tvA"])
        E_(lambda e: e.tensor_mul(s_sel[:, :, :], s_ohB[:, :, :], lg3),
           reads=["s_ohB", "s_lgt"], writes=["s_sel"])
        E_(lambda e: e.reduce_sum(s_tvB[:, :], s_sel[:, :, :], axis=AX.X),
           reads=["s_sel"], writes=["s_tvB"])
        E_(lambda e: e.tensor_sub(s_edt[:, :], s_tvB[:, :], s_tvA[:, :]),
           reads=["s_tvB", "s_tvA"], writes=["s_edt"])
        E_(lambda e: e.tensor_copy(s_ohAb[:, :, :], s_ohA[:, :, :]),
           reads=["s_ohA"], writes=["s_ohAb"])
        E_(lambda e: e.tensor_copy(s_ohBb[:, :, :], s_ohB[:, :, :]),
           reads=["s_ohB"], writes=["s_ohBb"])
    @A.on("scalar")
    def _():
        E_(lambda e: e.activation(s_edt[:, :], s_edt[:, :], ACT.Exp),
           reads=["s_edt"], writes=["s_edt"])
    @A.on("vector")
    def _():
        E_(lambda e: e.tensor_scalar_add(s_wA[:, :], s_edt[:, :], 1.0),
           reads=["s_edt"], writes=["s_wA"])
        E_(lambda e: e.reciprocal(s_wA[:, :], s_wA[:, :]),
           reads=["s_wA"], writes=["s_wA"])
        E_(lambda e: e.tensor_mul(s_wB[:, :], s_edt[:, :], s_wA[:, :]),
           reads=["s_edt", "s_wA"], writes=["s_wB"])
    for tt in range(NTT):
        for st, ohb, pf in [(0, s_ohAb, s_pfA), (1, s_ohBb, s_pfB)]:
            @A.on("tensor")
            def _(tt=tt, st=st, ohb=ohb):
                def mms(e):
                    e.matmul(ps_f2[st][:, 0:8], s_trib[:, :], ohb[:, tt, :],
                             start=True, stop=True)
                    return e.matmul(ps_s[0:1, NTT8 * st + 8 * tt:NTT8 * st + 8 * tt + 8],
                                    s_onec[:, :], ohb[:, tt, :], start=True, stop=True)
                E_(mms, reads=["s_ohAb" if st == 0 else "s_ohBb", "s_trib", "s_onec"],
                   writes=[f"ps_f2{st}", "ps_s"])
            @A.on("vector")
            def _(tt=tt, st=st, pf=pf):
                E_(lambda e: e.tensor_copy(pf[:, tt, :], ps_f2[st][:, 0:8]),
                   reads=[f"ps_f2{st}"], writes=["s_pfA" if st == 0 else "s_pfB"])
    @A.on("vector")
    def _():
        for st in range(2):
            E_(lambda e, st=st: e.memset(s_run[:, 8 * st:8 * st + 8], 0.0),
               writes=["s_run"])
            for tt in range(NTT):
                E_(lambda e, st=st, tt=tt: e.tensor_copy(
                    s_off[0:1, NTT8 * st + 8 * tt:NTT8 * st + 8 * tt + 8],
                    s_run[0:1, 8 * st:8 * st + 8]),
                   reads=["s_run"], writes=["s_off"])
                E_(lambda e, st=st, tt=tt: e.tensor_add(
                    s_run[0:1, 8 * st:8 * st + 8], s_run[0:1, 8 * st:8 * st + 8],
                    ps_s[0:1, NTT8 * st + 8 * tt:NTT8 * st + 8 * tt + 8]),
                   reads=["s_run", "ps_s"], writes=["s_run"])
        E_(lambda e: e.tensor_add(s_cnt[0:1, :], s_run[0:1, 0:8], s_run[0:1, 8:16]),
           reads=["s_run"], writes=["s_cnt"])
    @A.on("sync")
    def _():
        E_(lambda e: e.dma_start(out=cnt_loc.ap(), in_=s_cnt[0:1, :]),
           reads=["s_cnt"], writes=["cnt_loc"], dma=True)
    @A.on("gpsimd")
    def _():
        E_(lambda e: e.collective_compute("AllGather", ALU.bypass, replica_groups=RG,
                                          ins=[cnt_loc.ap()], outs=[cnt_all.ap()]),
           reads=["cnt_loc"], writes=["cnt_all"], cc=True)
    @A.on("sync")
    def _():
        E_(lambda e: e.dma_start(out=s_ca[:, :], in_=cnt_all.ap()),
           reads=["cnt_all"], writes=["s_ca"], dma=True)
    @A.on("tensor")
    def _():
        E_(lambda e: e.matmul(ps_s[0:1, 2 * NTT8:2 * NTT8 + 8], s_pfm[:, :], s_ca[:, :],
                              start=True, stop=True),
           reads=["s_pfm", "s_ca"], writes=["ps_s"])
    @A.on("scalar")
    def _():
        E_(lambda e: e.copy(s_gb[0:1, :], ps_s[0:1, 2 * NTT8:2 * NTT8 + 8]),
           reads=["ps_s"], writes=["s_gb"])
    @A.on("vector")
    def _():
        for tt in range(NTT):
            E_(lambda e, tt=tt: e.tensor_copy(s_gbr[0:1, 8 * tt:8 * tt + 8], s_gb[0:1, :]),
               reads=["s_gb"], writes=["s_gbr"])
    for half, mv, mk_, dst, dk in [(0, s_off, "s_off", s_offb, "s_offb"),
                                   (1, s_off, "s_off", s_offb, "s_offb"),
                                   (2, s_gbr, "s_gbr", s_gbb, "s_gbb")]:
        @A.on("tensor")
        def _(half=half, mv=mv, mk_=mk_):
            E_(lambda e: e.matmul(ps_f2[half % 2][:, 0:NTT8], s_ones1[:, :],
                                  mv[0:1, NTT8 * (half % 2):NTT8 * (half % 2) + NTT8]
                                  if half < 2 else mv[0:1, 0:NTT8],
                                  start=True, stop=True),
               reads=[mk_, "s_ones1"], writes=[f"ps_f2{half % 2}"])
        @A.on("scalar")
        def _(half=half, dst=dst, dk=dk):
            E_(lambda e: e.copy(dst[:, NTT8 * (half % 2):NTT8 * (half % 2) + NTT8]
                                if half < 2 else dst[:, 0:NTT8],
                                ps_f2[half % 2][:, 0:NTT8]),
               reads=[f"ps_f2{half % 2}"], writes=[dk])
    @A.on("vector")
    def _():
        E_(lambda e: e.tensor_sub(s_pfA[:, :, :], s_pfA[:, :, :], s_ohA[:, :, :]),
           reads=["s_pfA", "s_ohA"], writes=["s_pfA"])
        E_(lambda e: e.tensor_sub(s_pfB[:, :, :], s_pfB[:, :, :], s_ohB[:, :, :]),
           reads=["s_pfB", "s_ohB"], writes=["s_pfB"])
        E_(lambda e: e.tensor_add(s_S[:, :, :], s_pfA[:, :, :],
                                  s_offb[:, 0:NTT8].rearrange("p (a b) -> p a b", b=8)),
           reads=["s_pfA", "s_offb"], writes=["s_S"])
        E_(lambda e: e.tensor_add(s_S[:, :, :], s_S[:, :, :],
                                  s_offb[:, NTT8:2 * NTT8].rearrange("p (a b) -> p a b", b=8)),
           reads=["s_S", "s_offb"], writes=["s_S"])
        E_(lambda e: e.tensor_add(s_S[:, :, :], s_S[:, :, :], s_pfB[:, :, :]),
           reads=["s_S", "s_pfB"], writes=["s_S"])
        E_(lambda e: e.tensor_mul(s_sel[:, :, :], s_ohA[:, :, :], s_S[:, :, :]),
           reads=["s_ohA", "s_S"], writes=["s_sel"])
        E_(lambda e: e.reduce_sum(s_mgA[:, :], s_sel[:, :, :], axis=AX.X),
           reads=["s_sel"], writes=["s_mgA"])
        E_(lambda e: e.tensor_mul(s_sel[:, :, :], s_ohB[:, :, :], s_S[:, :, :]),
           reads=["s_ohB", "s_S"], writes=["s_sel"])
        E_(lambda e: e.reduce_sum(s_mgB[:, :], s_sel[:, :, :], axis=AX.X),
           reads=["s_sel"], writes=["s_mgB"])
        E_(lambda e: e.tensor_mul(s_sel[:, :, :], s_ohA[:, :, :],
                                  s_gbb[:, 0:NTT8].rearrange("p (a b) -> p a b", b=8)),
           reads=["s_ohA", "s_gbb"], writes=["s_sel"])
        E_(lambda e: e.reduce_sum(s_tvA[:, :], s_sel[:, :, :], axis=AX.X),
           reads=["s_sel"], writes=["s_tvA"])
        E_(lambda e: e.tensor_mul(s_sel[:, :, :], s_ohB[:, :, :],
                                  s_gbb[:, 0:NTT8].rearrange("p (a b) -> p a b", b=8)),
           reads=["s_ohB", "s_gbb"], writes=["s_sel"])
        E_(lambda e: e.reduce_sum(s_tvB[:, :], s_sel[:, :, :], axis=AX.X),
           reads=["s_sel"], writes=["s_tvB"])
        E_(lambda e: e.tensor_add(s_tvA[:, :], s_tvA[:, :], s_mgA[:, :]),
           reads=["s_tvA", "s_mgA"], writes=["s_tvA"])
        E_(lambda e: e.tensor_add(s_tvB[:, :], s_tvB[:, :], s_mgB[:, :]),
           reads=["s_tvB", "s_mgB"], writes=["s_tvB"])
        E_(lambda e: e.tensor_scalar(s_kA[:, :], s_tvA[:, :], float(CAP), None, ALU.is_lt),
           reads=["s_tvA"], writes=["s_kA"])
        E_(lambda e: e.tensor_mul(s_wA[:, :], s_wA[:, :], s_kA[:, :]),
           reads=["s_wA", "s_kA"], writes=["s_wA"])
        E_(lambda e: e.tensor_scalar(s_kB[:, :], s_tvB[:, :], float(CAP), None, ALU.is_lt),
           reads=["s_tvB"], writes=["s_kB"])
        E_(lambda e: e.tensor_mul(s_wB[:, :], s_wB[:, :], s_kB[:, :]),
           reads=["s_wB", "s_kB"], writes=["s_wB"])
        E_(lambda e: e.memset(s_mx1[:, :], 0.0), writes=["s_mx1"])
        E_(lambda e: e.memset(s_mx2[:, :], 0.0), writes=["s_mx2"])
        for k in range(1, CAP // 512):
            E_(lambda e, k=k: e.tensor_scalar(s_edt[:, :], s_tvA[:, :],
                                              float(512 * k), None, ALU.is_ge),
               reads=["s_tvA"], writes=["s_edt"])
            E_(lambda e: e.tensor_add(s_mx1[:, :], s_mx1[:, :], s_edt[:, :]),
               reads=["s_mx1", "s_edt"], writes=["s_mx1"])
            E_(lambda e, k=k: e.tensor_scalar(s_edt[:, :], s_tvB[:, :],
                                              float(512 * k), None, ALU.is_ge),
               reads=["s_tvB"], writes=["s_edt"])
            E_(lambda e: e.tensor_add(s_mx2[:, :], s_mx2[:, :], s_edt[:, :]),
               reads=["s_mx2", "s_edt"], writes=["s_mx2"])
        E_(lambda e: e.scalar_tensor_tensor(s_mgA[:, :], s_i1[:, :], 512.0,
                                            s_tvA[:, :], ALU.mult, ALU.add),
           reads=["s_i1", "s_tvA", "s_mgA"], writes=["s_mgA"])
        E_(lambda e: e.scalar_tensor_tensor(s_mgA[:, :], s_mx1[:, :],
                                            float(512 * (NEXP - 1)),
                                            s_mgA[:, :], ALU.mult, ALU.add),
           reads=["s_mx1", "s_mgA"], writes=["s_mgA"])
        E_(lambda e: e.scalar_tensor_tensor(s_mgB[:, :], s_i2[:, :], 512.0,
                                            s_tvB[:, :], ALU.mult, ALU.add),
           reads=["s_i2", "s_tvB", "s_mgB"], writes=["s_mgB"])
        E_(lambda e: e.scalar_tensor_tensor(s_mgB[:, :], s_mx2[:, :],
                                            float(512 * (NEXP - 1)),
                                            s_mgB[:, :], ALU.mult, ALU.add),
           reads=["s_mx2", "s_mgB"], writes=["s_mgB"])
        E_(lambda e: e.tensor_mul(s_mgA[:, :], s_mgA[:, :], s_kA[:, :]),
           reads=["s_mgA", "s_kA"], writes=["s_mgA"])
        E_(lambda e: e.tensor_mul(s_mgB[:, :], s_mgB[:, :], s_kB[:, :]),
           reads=["s_mgB", "s_kB"], writes=["s_mgB"])
        E_(lambda e: e.tensor_copy(s_giA[:, :], s_mgA[:, :]),
           reads=["s_mgA"], writes=["s_giA"])
        E_(lambda e: e.tensor_copy(s_giB[:, :], s_mgB[:, :]),
           reads=["s_mgB"], writes=["s_giB"])
        E_(lambda e: e.scalar_tensor_tensor(s_edt[:, :], s_kA[:, :], float(-NSLOT),
                                            s_mgA[:, :], ALU.mult, ALU.add),
           reads=["s_kA", "s_mgA"], writes=["s_edt"])
        E_(lambda e: e.tensor_scalar_add(s_edt[:, :], s_edt[:, :], float(NSLOT)),
           reads=["s_edt"], writes=["s_edt"])
        E_(lambda e: e.tensor_copy(s_siA[:, :], s_edt[:, :]),
           reads=["s_edt"], writes=["s_siA"])
        E_(lambda e: e.scalar_tensor_tensor(s_edt[:, :], s_kB[:, :], float(-NSLOT),
                                            s_mgB[:, :], ALU.mult, ALU.add),
           reads=["s_kB", "s_mgB"], writes=["s_edt"])
        E_(lambda e: e.tensor_scalar_add(s_edt[:, :], s_edt[:, :], float(NSLOT)),
           reads=["s_edt"], writes=["s_edt"])
        E_(lambda e: e.tensor_copy(s_siB[:, :], s_edt[:, :]),
           reads=["s_edt"], writes=["s_siB"])
    if debug:
        @A.on("vector")
        def _():
            for j, srcb, k in [(0, s_i1, "s_i1"), (1, s_i2, "s_i2"),
                               (2, s_wA, "s_wA"), (3, s_wB, "s_wB"),
                               (4, s_tvA, "s_tvA"), (5, s_tvB, "s_tvB"),
                               (6, s_mgA, "s_mgA"), (7, s_mgB, "s_mgB")]:
                E_(lambda e, j=j, srcb=srcb, k=k: e.tensor_copy(
                    s_dbg[:, :, j:j + 1].rearrange("p a o -> p (a o)"), srcb[:, :]),
                   reads=[k], writes=["s_dbg"])
        @A.on("sync")
        def _():
            E_(lambda e: e.dma_start(
                out=outs["o_top"].ap().rearrange("(a p) m -> p a m", p=128),
                in_=s_dbg[:, :, :]),
               reads=["s_dbg"], writes=["o_top"], dma=True)

    for tt in range(NTT):
        @A.on("gpsimd")
        def _(tt=tt):
            E_(lambda e: e.indirect_dma_start(
                stg.ap(), bass.IndirectOffsetOnAxis(s_siA[:, tt:tt + 1], 0),
                s_x3b[:, tt, :], None),
               reads=["s_x3b", "s_siA", "stg"], writes=["stg"], dkey="stg_s", dma=True)
            E_(lambda e: e.indirect_dma_start(
                stg.ap(), bass.IndirectOffsetOnAxis(s_siB[:, tt:tt + 1], 0),
                s_x3b[:, tt, :], None),
               reads=["s_x3b", "s_siB", "stg"], writes=["stg"], dkey="stg_s", dma=True)

    A.fence()
    f1ctx.close()
    fxctx.close()
    f2ctx.close()

    f4ctx = contextlib.ExitStack()
    sbF4 = mk_sb(f4ctx)
    s_wF = sbF4("s_wF", [128, 6 * 4096], BF16)
    s_w2 = sbF4("s_w2", [128, 32 * 1024], BF16)
    s_dR = sbF4("s_dR", [128, 4, H], BF16)
    s_dT = sbF4("s_dT", [128, HK, 512], BF16)
    s_hb = sbF4("s_hb", [128, 32, 512], BF16)
    s_eoT = s_dT
    s_eb1c = sbF4("s_eb1c", [128, 32])
    s_eb2c = sbF4("s_eb2c", [128, 8])
    s_xf = sbF4("s_xf", [128, 2, H], BF16)

    @A.on("scalar")
    def _():
        for k in range(6):
            E_(lambda e, k=k: e.dma_start(
                out=s_wF[:, 4096 * k:4096 * (k + 1)],
                in_=myw1_bf.ap()[128 * (2 + k):128 * (3 + k), :]),
               reads=["myw1_bf"], writes=["s_wF"], dma=True)
        for k2 in range(32):
            E_(lambda e, k2=k2: e.dma_start(
                out=s_w2[:, 1024 * k2:1024 * (k2 + 1)],
                in_=myw2_bf.ap()[128 * k2:128 * (k2 + 1), :]),
               reads=["myw2_bf"], writes=["s_w2"], dma=True)
        E_(lambda e: e.dma_start(
            out=s_eb1c[:, :], in_=myb1.ap().rearrange("(a p) o -> p (a o)", p=128),
            allow_slow_non_contiguous=True),
           writes=["s_eb1c"], dma=True)
        E_(lambda e: e.dma_start(
            out=s_eb2c[:, :], in_=myb2.ap().rearrange("(a p) o -> p (a o)", p=128),
            allow_slow_non_contiguous=True),
           writes=["s_eb2c"], dma=True)

    NQ = CAP // 512
    for q in range(NQ):
        @A.on("gpsimd")
        def _(q=q):
            E_(lambda e, q=q: e.collective_compute(
                "ReduceScatter", ALU.add, replica_groups=RG,
                ins=[stg.ap()[NEXP * 512 * q:NEXP * 512 * (q + 1), :]],
                outs=[rs_out.ap()[512 * q:512 * (q + 1), :]]),
               reads=["stg"], writes=[f"rs_out{q}"], cc=True, dkey=f"rs{q}")

    for q in range(8):
        @A.on("sync")
        def _(q=q):
            for j in range(4):
                E_(lambda e, j=j, q=q: e.dma_start(
                    out=s_dR[:, j, :],
                    in_=rs_out.ap()[512 * q + 128 * j:512 * q + 128 * (j + 1), :]),
                   reads=[f"rs_out{q}"], writes=["s_dR"], dma=True)
        for k in range(HK):
            @A.on("tensor")
            def _(k=k):
                def tps(e):
                    for j in range(4):
                        r = e.transpose(ps_t[:, 128 * j:128 * (j + 1)],
                                        s_dR[:, j, 128 * k:128 * (k + 1)], s_idb[:])
                    return r
                E_(tps, reads=["s_dR", "s_idb"], writes=["ps_t"])
            @A.on("scalar")
            def _(k=k):
                E_(lambda e: e.copy(s_dT[:, k, :], ps_t[:, 0:512]),
                   reads=["ps_t"], writes=["s_dT"])
        for hm in range(32):
            pm, pk = ps_m[hm % 2], f"ps_m{hm % 2}"
            @A.on("tensor")
            def _(hm=hm, pm=pm, pk=pk):
                def mms(e):
                    for k in range(HK):
                        kk = k if k < 2 else k - 2
                        wsl = (s_wE if k < 2 else s_wF)[
                            :, 4096 * kk + 128 * hm:4096 * kk + 128 * (hm + 1)]
                        r = e.matmul(pm[:, 0:512], wsl, s_dT[:, k, :],
                                     start=(k == 0), stop=(k == HK - 1))
                    return r
                E_(mms, reads=["s_wE", "s_wF", "s_dT"], writes=[pk])
            @A.on("vector")
            def _(hm=hm, pm=pm, pk=pk):
                E_(lambda e: e.tensor_scalar(s_hb[:, hm, :], pm[:, 0:512],
                                             s_eb1c[:, hm:hm + 1], 0.0,
                                             ALU.add, ALU.max),
                   reads=[pk, "s_eb1c"], writes=["s_hb"])
        for dm in range(HK):
            pm, pk = ps_m[dm % 2], f"ps_m{dm % 2}"
            @A.on("tensor")
            def _(dm=dm, pm=pm, pk=pk):
                def mms(e):
                    for k2 in range(32):
                        r = e.matmul(pm[:, 0:512],
                                     s_w2[:, 1024 * k2 + 128 * dm:1024 * k2 + 128 * (dm + 1)],
                                     s_hb[:, k2, :], start=(k2 == 0), stop=(k2 == 31))
                    return r
                E_(mms, reads=["s_w2", "s_hb"], writes=[pk])
            @A.on("vector")
            def _(dm=dm, pm=pm, pk=pk):
                E_(lambda e: e.tensor_scalar(s_eoT[:, dm, :], pm[:, 0:512],
                                             s_eb2c[:, dm:dm + 1], None, ALU.add),
                   reads=[pk, "s_eb2c"], writes=["s_dT"])
        for j in range(4):
            for kg in range(2):
                @A.on("tensor")
                def _(j=j, kg=kg):
                    def tps(e):
                        for k in range(4):
                            r = e.transpose(ps_t[:, 128 * k:128 * (k + 1)],
                                            s_eoT[:, 4 * kg + k, 128 * j:128 * (j + 1)],
                                            s_idb[:])
                        return r
                    E_(tps, reads=["s_dT", "s_idb"], writes=["ps_t"])
                @A.on("scalar")
                def _(j=j, kg=kg):
                    E_(lambda e: e.copy(s_dR[:, j, 512 * kg:512 * (kg + 1)],
                                        ps_t[:, 0:512]),
                       reads=["ps_t"], writes=["s_dR"])
        @A.on("sync")
        def _(q=q):
            E_(lambda e, q=q: e.dma_start(
                out=eo_loc.ap()[512 * q:512 * (q + 1), :].rearrange("(j p) d -> p j d", p=128),
                in_=s_dR[:, :, :]),
               reads=["s_dR"], writes=["eo_loc"], dma=True)
        @A.on("gpsimd")
        def _(q=q):
            E_(lambda e, q=q: e.collective_compute(
                "AllGather", ALU.bypass, replica_groups=RG,
                ins=[eo_loc.ap()[512 * q:512 * (q + 1), :]],
                outs=[eo_g.ap()[NEXP * 512 * q:NEXP * 512 * (q + 1), :]]),
               reads=["eo_loc"], writes=[f"eo_g{q}"], cc=True, dkey="ag")

    for tt in range(NTT):
        fp = tt % 2
        @A.on("gpsimd")
        def _(tt=tt, fp=fp):
            eg = [f"eo_g{q}" for q in range(CAP // 512)]
            E_(lambda e: e.indirect_dma_start(
                s_g0[:, fp, :], None, eo_g.ap(),
                bass.IndirectOffsetOnAxis(s_giA[:, tt:tt + 1], 0)),
               reads=eg + ["s_giA"], writes=[f"s_g0{fp}"], dma=True)
            E_(lambda e: e.indirect_dma_start(
                s_g1[:, fp, :], None, eo_g.ap(),
                bass.IndirectOffsetOnAxis(s_giB[:, tt:tt + 1], 0)),
               reads=eg + ["s_giB"], writes=[f"s_g1{fp}"], dma=True)
        @A.on("gpsimd")
        def _(tt=tt, fp=fp):
            E_(lambda e: e.dma_start(out=s_xf[:, fp, :],
                                     in_=x1_sp.ap()[128 * tt:128 * (tt + 1), :]),
               reads=["x1_sp"], writes=[f"s_xf{fp}"], dma=True)
        @A.on("vector")
        def _(tt=tt, fp=fp):
            E_(lambda e: e.scalar_tensor_tensor(
                s_ot[:, fp, :], s_g0[:, fp, :], s_wA[:, tt:tt + 1],
                s_xf[:, fp, :], ALU.mult, ALU.add),
               reads=[f"s_g0{fp}", "s_wA", f"s_xf{fp}"], writes=["s_ot"])
            E_(lambda e: e.scalar_tensor_tensor(
                s_ot[:, fp, :], s_g1[:, fp, :], s_wB[:, tt:tt + 1], s_ot[:, fp, :],
                ALU.mult, ALU.add),
               reads=[f"s_g1{fp}", "s_wB", "s_ot"], writes=["s_ot"])
        @A.on("sync")
        def _(tt=tt, fp=fp):
            E_(lambda e: e.dma_start(out=o_out.ap()[128 * tt:128 * (tt + 1), :],
                                     in_=s_ot[:, fp, :]),
               reads=["s_ot"], writes=["o_out"], dma=True)

    A.final_dma_waits()
    ctx.pop_all()
    block.__exit__(None, None, None)
    return nc, outs


_CACHE = {}


def kernel(**inputs):
    import numpy as np
    from concourse.bass_utils import run_bass_kernel_spmd

    f32 = lambda a: np.ascontiguousarray(a, dtype=np.float32)
    par = P(L=8192)
    T, TC = par.T, par.TC
    if "nc" not in _CACHE:
        _CACHE["nc"] = build(par, debug=False)[0]
    nc = _CACHE["nc"]

    x = np.asarray(inputs["L0_attn_hidden_states"], dtype=np.float32).reshape(T, H)
    W = np.asarray(inputs["L0_attn_in_proj_w"], dtype=np.float32)
    bip = np.asarray(inputs["L0_attn_in_proj_b"], dtype=np.float32)
    ew1 = np.asarray(inputs["L0_ffn_ew1"], dtype=np.float32)
    ew2 = np.asarray(inputs["L0_ffn_ew2"], dtype=np.float32)
    eb1 = np.asarray(inputs["L0_ffn_eb1"], dtype=np.float32)
    eb2 = np.asarray(inputs["L0_ffn_eb2"], dtype=np.float32)
    maps = []
    for c in range(NCOR):
        es = slice(c * EC, (c + 1) * EC)
        pm = np.zeros((8, 1), np.float32)
        pm[:c] = 1.0
        maps.append({
            "x_rows": f32(x[c * TC:(c + 1) * TC]),
            "w_ip_x": f32(W[:, es]),
            "w_ip_g": f32(W[:, E + c * EC: E + (c + 1) * EC]),
            "b_ip_x": f32(bip[es][None, :]),
            "b_ip_g": f32(bip[E + c * EC: E + (c + 1) * EC][None, :]),
            "ln_w": f32(np.asarray(inputs["L0_attn_ln_w"])[None, :]),
            "ln_b": f32(np.asarray(inputs["L0_attn_ln_b"])[None, :]),
            "conv_w": f32(np.asarray(inputs["L0_attn_conv_w"])[es]),
            "conv_b": f32(np.asarray(inputs["L0_attn_conv_b"])[es][:, None]),
            "xproj_w": f32(np.asarray(inputs["L0_attn_x_proj_w"])[es]),
            "dtproj_w": f32(np.asarray(inputs["L0_attn_dt_proj_w"])[:, es]),
            "dtproj_b": f32(np.asarray(inputs["L0_attn_dt_proj_b"])[es][:, None]),
            "a_log": f32(np.asarray(inputs["L0_attn_A_log"])[es]),
            "d_in": f32(np.asarray(inputs["L0_attn_D"])[es][:, None]),
            "wo_shard": f32(np.asarray(inputs["L0_attn_out_proj_w"])[es]),
            "wob_in": f32(np.asarray(inputs["L0_attn_out_proj_b"])[None, :]),
            "rw1_shard": f32(np.asarray(inputs["L0_ffn_rw1"])[128 * c:128 * (c + 1), :]),
            "rw2_in": f32(inputs["L0_ffn_rw2"]),
            "rb1_in": f32(np.asarray(inputs["L0_ffn_rb1"])[:, None]),
            "rb2_in": f32(np.asarray(inputs["L0_ffn_rb2"])[None, :]),
            "myw1": f32(ew1[c]),
            "myw2": f32(ew2[c]),
            "myb1": f32(eb1[c][:, None]),
            "myb2": f32(eb2[c][:, None]),
            "pfxmask": pm,
        })
    res = run_bass_kernel_spmd(nc, maps, core_ids=list(range(NCOR)))
    out = np.concatenate([np.asarray(res.results[c]["o_out"]) for c in range(NCOR)], 0)
    return out.reshape(2, 8192, 1024).astype(np.float32, copy=False)


# version 21 (final)
# speedup vs baseline: 1.1300x; 1.0134x over previous
import numpy as np
import contextlib
import ml_dtypes
import concourse.bass as bass
import concourse.mybir as mybir
from concourse.aot_env import allow_ici_all_to_all

F32 = mybir.dt.float32
BF16 = mybir.dt.bfloat16
I32 = mybir.dt.int32
ALU = mybir.AluOpType
ACT = mybir.ActivationFunctionType
AX = mybir.AxisListType
BF = ml_dtypes.bfloat16

NCOR = 8
H, E, NST, RNK, KC = 1024, 2048, 16, 64, 4
EC = E // NCOR
NET = EC // 128
HK = H // 128
NEXP, TOPK, CAP = 8, 2, 4096
EPS = 1e-5
RG = [list(range(NCOR))]


class P:
    def __init__(self, L=8192):
        self.L = L
        self.T = 2 * L
        self.NSL = NCOR
        self.TS = self.T // NCOR
        self.TC = self.TS
        assert self.L % self.TS == 0
        self.SLB = self.L // self.TS
        self.NCH = max(1, self.TS // 512)
        self.NCK = min(512, self.TS)
        self.NTT = self.TC // 128


class Auto:
    ENG_SEMS = {"tensor": "c_mm", "vector": "c_ve", "scalar": "c_sc",
                "gpsimd": "c_gp", "sync": "c_sy"}

    def __init__(self, nc, block, ctx):
        self.nc, self.block, self.ctx = nc, block, ctx
        self.sems, self.ct = {}, {}
        for name in set(self.ENG_SEMS.values()):
            self.sems[name] = ctx.enter_context(nc.semaphore(name))
            self.ct[name] = 0
        self.last_w = {}
        self.last_r = {}
        self.waited = {e: {} for e in self.ENG_SEMS}
        self._cur = None

    def dsem(self, name):
        if name not in self.sems:
            self.sems[name] = self.ctx.enter_context(self.nc.semaphore(name))
            self.ct[name] = 0
        return self.sems[name]

    def on(self, engine_name):
        def deco(fn):
            def wrapper(eng):
                prev = self._cur
                self._cur = (engine_name, eng)
                try:
                    fn()
                finally:
                    self._cur = prev
            getattr(self.block, engine_name)(wrapper)
        return deco

    def _wait(self, eng_name, eng, sem, val):
        if val <= 0:
            return
        if self.waited[eng_name].get(sem, -1) >= val:
            return
        eng.wait_ge(self.sems[sem], val)
        self.waited[eng_name][sem] = val

    def emit(self, fn, reads=(), writes=(), dma=False, cc=False, dkey=None):
        eng_name, eng = self._cur
        needs = {}
        for b in reads:
            if b in self.last_w:
                s, v = self.last_w[b]
                needs[s] = max(needs.get(s, 0), v)
        for b in writes:
            if b in self.last_w:
                s, v = self.last_w[b]
                needs[s] = max(needs.get(s, 0), v)
            for s, v in self.last_r.get(b, {}).items():
                needs[s] = max(needs.get(s, 0), v)
        for s, v in needs.items():
            self._wait(eng_name, eng, s, v)
        inst = fn(eng)
        insts = inst if isinstance(inst, list) else [inst]
        if cc:
            sem_name, step = "cc_" + (writes[0] if writes else "x"), 1
            self.dsem(sem_name)
        elif dma:
            sem_name, step = "qd_" + (dkey or (writes[0] if writes else "x")), 16
            self.dsem(sem_name)
        else:
            sem_name, step = self.ENG_SEMS[eng_name], 1
        for it in insts:
            self.ct[sem_name] += step
            it.then_inc(self.sems[sem_name], step)
        v = self.ct[sem_name]
        inst = insts[-1]
        for b in writes:
            self.last_w[b] = (sem_name, v)
            self.last_r.pop(b, None)
        for b in reads:
            self.last_r.setdefault(b, {})[sem_name] = v
        return inst

    def fence(self):
        snap = [(s, v) for s, v in self.ct.items() if v > 0]
        for eng_name in self.ENG_SEMS:
            def mk(eng_name=eng_name):
                def body():
                    eng = self._cur[1]
                    for s, v in snap:
                        if self.waited[eng_name].get(s, -1) < v:
                            eng.wait_ge(self.sems[s], v)
                            self.waited[eng_name][s] = v
                return body
            self.on(eng_name)(mk())

    def final_dma_waits(self):
        @self.on("sync")
        def _():
            eng = self._cur[1]
            for s_name, v in sorted(self.ct.items()):
                if (s_name.startswith("qd_") or s_name.startswith("cc_")) and v > 0:
                    eng.wait_ge(self.sems[s_name], v)


def build(p: P, debug=False):
    nc = bass.Bass("TRN2", target_bir_lowering=False)
    ctx = contextlib.ExitStack()
    def mk_sb(ctx_):
        def sb(name, shape, dtype=F32):
            esz = 2 if dtype == BF16 else 4
            nb = esz
            for dd in shape[1:]:
                nb *= dd
            al = 128 if nb >= 2048 else 32
            return ctx_.enter_context(nc.sbuf_tensor(name, shape, dtype,
                                                     align_bytes=al))
        return sb
    TS, NCH, NCK, NTT, NSL = p.TS, p.NCH, p.NCK, p.NTT, p.NSL
    CH4 = 4 if p.TC % (4 * 512) == 0 else 1
    CSZ = p.TC // CH4

    dram = lambda n, s, d=F32, k="Internal": nc.dram_tensor(n, s, d, kind=k)
    x_rows = dram("x_rows", [p.TC, H], F32, "ExternalInput")
    w_ip_x = dram("w_ip_x", [H, EC], F32, "ExternalInput")
    w_ip_g = dram("w_ip_g", [H, EC], F32, "ExternalInput")
    b_ip_x = dram("b_ip_x", [1, EC], F32, "ExternalInput")
    b_ip_g = dram("b_ip_g", [1, EC], F32, "ExternalInput")
    ln_w = dram("ln_w", [1, H], F32, "ExternalInput")
    ln_b = dram("ln_b", [1, H], F32, "ExternalInput")
    conv_w = dram("conv_w", [EC, KC], F32, "ExternalInput")
    conv_b = dram("conv_b", [EC, 1], F32, "ExternalInput")
    xproj_w = dram("xproj_w", [EC, 96], F32, "ExternalInput")
    dtproj_w = dram("dtproj_w", [RNK, EC], F32, "ExternalInput")
    dtproj_b = dram("dtproj_b", [EC, 1], F32, "ExternalInput")
    a_log = dram("a_log", [EC, NST], F32, "ExternalInput")
    d_in = dram("d_in", [EC, 1], F32, "ExternalInput")

    xbf_loc = dram("xbf_loc", [p.TC, H], BF16)
    xbf = dram("xbf", [p.T, H], BF16)
    st_loc = dram("st_loc", [2, p.TC], F32)
    st_all = dram("st_all", [2 * NCOR, p.TC], F32)
    dbc_par = dram("dbc_par", [NSL, 96, TS], F32)
    dbc_bf = dram("dbc_bf", [2, 32, TS], BF16)
    x3_sp = dram("x3_sp", [p.TC, H], BF16)
    dbc_sum = dram("dbc_sum", [NSL, 96, TS], F32)
    a2a_in = dram("a2a_in", [NSL, EC, TS], BF16)
    a2a_out = dram("a2a_out", [E, p.TC], BF16)

    wo_shard = dram("wo_shard", [EC, H], F32, "ExternalInput")
    wob_in = dram("wob_in", [1, H], F32, "ExternalInput")
    rw1_shard = dram("rw1_shard", [128, 4096], F32, "ExternalInput")
    rw2_in = dram("rw2_in", [4096, 8], F32, "ExternalInput")
    rb1_in = dram("rb1_in", [4096, 1], F32, "ExternalInput")
    rb2_in = dram("rb2_in", [1, 8], F32, "ExternalInput")
    myw1 = dram("myw1", [H, 4096], F32, "ExternalInput")
    myw2 = dram("myw2", [4096, H], F32, "ExternalInput")
    myb1 = dram("myb1", [4096, 1], F32, "ExternalInput")
    myb2 = dram("myb2", [H, 1], F32, "ExternalInput")
    pfxmask = dram("pfxmask", [8, 1], F32, "ExternalInput")

    NSLOT = NEXP * CAP
    myw1_bf = dram("myw1_bf", [H, 4096], BF16)
    myw2_bf = dram("myw2_bf", [4096, H], BF16)
    wo_loc = dram("wo_loc", [EC, H], BF16)
    wo_bf = dram("wo_bf", [E, H], BF16)
    rw1_loc = dram("rw1_loc", [128, 4096], BF16)
    rw1_bf = dram("rw1_bf", [H, 4096], BF16)
    x1_sp = dram("x1_sp", [p.TC, H], F32)
    stg = dram("stg", [NSLOT + 128, H], BF16)
    rs_out = dram("rs_out", [CAP, H], BF16)
    eo_loc = dram("eo_loc", [CAP, H], BF16)
    eo_g = dram("eo_g", [NSLOT, H], BF16)
    cnt_loc = dram("cnt_loc", [1, 8], F32)
    cnt_all = dram("cnt_all", [NCOR, 8], F32)
    o_out = dram("o_out", [p.TC, H], F32, "ExternalOutput")

    outs = {}
    outs["o_out"] = o_out
    if debug:
        outs["o_xscan"] = dram("o_xscan", [EC, p.T], BF16, "ExternalOutput")
        outs["o_dt"] = dram("o_dt", [EC, p.T], BF16, "ExternalOutput")
        outs["o_dbc"] = dram("o_dbc", [96, p.T], F32, "ExternalOutput")
        outs["o_yg"] = dram("o_yg", [E, p.TC], BF16, "ExternalOutput")
        outs["o_x1"] = dram("o_x1", [p.TC, H], F32, "ExternalOutput")
        outs["o_lg"] = dram("o_lg", [p.TC, 8], F32, "ExternalOutput")
        outs["o_top"] = dram("o_top", [p.TC, 8], F32, "ExternalOutput")

    c_ones1 = nc.inline_tensor(np.ones((1, 128), np.float32), "c_ones1")
    c_onesr = nc.inline_tensor(np.ones((1, NCK), np.float32), "c_onesr")
    c_idf = nc.inline_tensor(np.eye(128, dtype=np.float32), "c_idf")
    c_trib = nc.inline_tensor(np.triu(np.ones((128, 128))).astype(BF), "c_trib")
    c_idb = nc.inline_tensor(np.eye(128).astype(BF), "c_idb")
    c_onescol = nc.inline_tensor(np.ones((128, 1)).astype(BF), "c_onescol")
    c_onesrow = nc.inline_tensor(np.ones((1, TS)).astype(BF), "c_onesrow")

    sbG = mk_sb(ctx)
    psb = lambda n, s: ctx.enter_context(nc.psum_tensor(n, s, F32))
    s_mu = sbG("s_mu", [128, NTT])
    s_r1 = sbG("s_r1", [128, NTT])
    s_ones1 = sbG("s_ones1", [1, 128])
    s_onesr = sbG("s_onesr", [1, NCK])
    s_idf = sbG("s_idf", [128, 128])
    s_trib = sbG("s_trib", [128, 128], BF16)
    s_idb = sbG("s_idb", [128, 128], BF16)
    s_onec = sbG("s_onec", [128, 1], BF16)
    s_wobr = sbG("s_wobr", [1, H])
    s_pfm = sbG("s_pfm", [8, 1])
    s_rw2b = sbG("s_rw2b", [128, 32, 8], BF16)
    s_rb1c = sbG("s_rb1c", [128, 32])
    s_rb2r = sbG("s_rb2r", [1, 8])

    ps_m = [psb("ps_m0", [128, 512]), psb("ps_m1", [128, 512])]
    ps_s = psb("ps_s", [128, 512])

    mctx = contextlib.ExitStack()
    sbM = mk_sb(mctx)
    psM = lambda n, s, d=F32: mctx.enter_context(nc.psum_tensor(n, s, d))
    s_wxb = sbM("s_wxb", [128, HK, EC], BF16)
    s_wgb = sbM("s_wgb", [128, HK, EC], BF16)
    s_lnwc = sbM("s_lnwc", [128, HK])
    s_lnbc = sbM("s_lnbc", [128, HK])
    s_s1x = sbM("s_s1x", [1, EC])
    s_s2x = sbM("s_s2x", [1, EC])
    s_s1g = sbM("s_s1g", [1, EC])
    s_s2g = sbM("s_s2g", [1, EC])
    s_s12x = sbM("s_s12x", [2, EC], BF16)
    s_s12g = sbM("s_s12g", [2, EC], BF16)
    s_bxr = sbM("s_bxr", [1, EC])
    s_bgr = sbM("s_bgr", [1, EC])
    s_cw = sbM("s_cw", [128, NET, KC])
    s_cbc = sbM("s_cbc", [128, NET])
    s_dpb = sbM("s_dpb", [128, NET])
    s_A = sbM("s_A", [128, NET, NST])
    s_D = sbM("s_D", [128, NET])
    s_xpw = sbM("s_xpw", [128, NET, 96], BF16)
    s_dpw = sbM("s_dpw", [64, EC], BF16)
    s_hst = sbM("s_hst", [128, NET, NST])
    s_halo = sbM("s_halo", [128, NET, KC - 1])
    ps_db = psM("ps_db", [96, 512])
    block = ctx.enter_context(nc.Block())
    A = Auto(nc, block, ctx)
    E_ = A.emit

    pctx = contextlib.ExitStack()
    sbP = mk_sb(pctx)
    s_xr = sbP("s_xr", [128, H])
    s_xrb = sbP("s_xrb", [128, H], BF16)
    s_scr = sbP("s_scr", [128, H])
    s_sq = sbP("s_sq", [128, NTT])
    s_var = sbP("s_var", [128, NTT])
    s_t0 = sbP("s_t0", [128, NTT])
    s_ac = sbP("s_ac", [128, 2 * NTT])
    s_acT = sbP("s_acT", [2 * NTT, 128])
    s_wf = sbP("s_wf", [128, HK, EC])

    @A.on("sync")
    def _():
        for dst, key, src in [
            (s_wf[:], "s_wf", w_ip_x.ap().rearrange("(a p) m -> p a m", p=128)),
            (s_lnwc[:], "s_lnwc", ln_w.ap().rearrange("o (a p) -> p (o a)", p=128)),
            (s_lnbc[:], "s_lnbc", ln_b.ap().rearrange("o (a p) -> p (o a)", p=128)),
            (s_bxr[:], "s_bxr", b_ip_x.ap()),
            (s_bgr[:], "s_bgr", b_ip_g.ap()),
            (s_cw[:], "s_cw", conv_w.ap().rearrange("(a p) k -> p a k", p=128)),
            (s_cbc[:], "s_cbc", conv_b.ap().rearrange("(a p) o -> p (a o)", p=128)),
            (s_dpb[:], "s_dpb", dtproj_b.ap().rearrange("(a p) o -> p (a o)", p=128)),
            (s_A[:], "s_A", a_log.ap().rearrange("(a p) n -> p a n", p=128)),
            (s_D[:], "s_D", d_in.ap().rearrange("(a p) o -> p (a o)", p=128)),
            (s_ones1[:], "s_ones1", c_ones1.ap()),
            (s_onesr[:], "s_onesr", c_onesr.ap()),
            (s_idf[:], "s_idf", c_idf.ap()),
            (s_trib[:], "s_trib", c_trib.ap()),
            (s_idb[:], "s_idb", c_idb.ap()),
            (s_onec[:], "s_onec", c_onescol.ap()),
            (s_wobr[:], "s_wobr", wob_in.ap()),
            (s_pfm[:], "s_pfm", pfxmask.ap()),
            (s_rb1c[:], "s_rb1c", rb1_in.ap().rearrange("(a p) o -> p (a o)", p=128)),
            (s_rb2r[:], "s_rb2r", rb2_in.ap()),
        ][:0]:
            pass
        items = [
            (s_wf[:], "s_wf", w_ip_x.ap().rearrange("(a p) m -> p a m", p=128)),
            (s_lnwc[:], "s_lnwc", ln_w.ap().rearrange("o (a p) -> p (o a)", p=128)),
            (s_lnbc[:], "s_lnbc", ln_b.ap().rearrange("o (a p) -> p (o a)", p=128)),
            (s_bxr[:], "s_bxr", b_ip_x.ap()),
            (s_bgr[:], "s_bgr", b_ip_g.ap()),
            (s_cw[:], "s_cw", conv_w.ap().rearrange("(a p) k -> p a k", p=128)),
            (s_cbc[:], "s_cbc", conv_b.ap().rearrange("(a p) o -> p (a o)", p=128)),
            (s_dpb[:], "s_dpb", dtproj_b.ap().rearrange("(a p) o -> p (a o)", p=128)),
            (s_A[:], "s_A", a_log.ap().rearrange("(a p) n -> p a n", p=128)),
            (s_D[:], "s_D", d_in.ap().rearrange("(a p) o -> p (a o)", p=128)),
            (s_ones1[:], "s_ones1", c_ones1.ap()),
            (s_onesr[:], "s_onesr", c_onesr.ap()),
            (s_idf[:], "s_idf", c_idf.ap()),
            (s_trib[:], "s_trib", c_trib.ap()),
            (s_idb[:], "s_idb", c_idb.ap()),
            (s_onec[:], "s_onec", c_onescol.ap()),
            (s_wobr[:], "s_wobr", wob_in.ap()),
            (s_pfm[:], "s_pfm", pfxmask.ap()),
            (s_rb1c[:], "s_rb1c", rb1_in.ap().rearrange("(a p) o -> p (a o)", p=128)),
            (s_rb2r[:], "s_rb2r", rb2_in.ap()),
        ]
        def ldall(e):
            return [e.dma_start(out=d, in_=s2, allow_slow_non_contiguous=True)
                    for d, _k, s2 in items]
        E_(ldall, writes=[k for _d, k, _s in items], dma=True, dkey="ldc")

    @A.on("gpsimd")
    def _():
        def ldg(e):
            return [
                e.dma_start(out=s_dpw[:], in_=dtproj_w.ap()),
                e.dma_start(out=s_xpw[:], in_=xproj_w.ap().rearrange("(a p) m -> p a m", p=128)),
                e.dma_start(out=s_rw2b[:], in_=rw2_in.ap().rearrange("(a p) m -> p a m", p=128)),
                e.dma_start(out=wo_loc.ap(), in_=wo_shard.ap()),
                e.dma_start(out=rw1_loc.ap(), in_=rw1_shard.ap()),
                e.dma_start(out=myw1_bf.ap(), in_=myw1.ap()),
                e.dma_start(out=myw2_bf.ap(), in_=myw2.ap()),
            ]
        E_(ldg, writes=["s_dpw", "s_xpw", "s_rw2b", "wo_loc", "rw1_loc",
                        "myw1_bf", "myw2_bf"], dma=True, dkey="ldg")

    @A.on("scalar")
    def _():
        for m in range(NET):
            E_(lambda e, m=m: e.activation(s_A[:, m, :], s_A[:, m, :], ACT.Exp),
               reads=["s_A"], writes=["s_A"])
    @A.on("vector")
    def _():
        E_(lambda e: e.tensor_scalar_mul(s_A[:], s_A[:], -1.0),
           reads=["s_A"], writes=["s_A"])

    for i in range(NTT):
        @A.on("sync")
        def _(i=i):
            E_(lambda e: e.dma_start(out=s_xr[:, :],
                                     in_=x_rows.ap()[128 * i:128 * (i + 1), :]),
               writes=["s_xr"], dma=True)
        @A.on("vector")
        def _(i=i):
            E_(lambda e: e.tensor_copy(s_xrb[:, :], s_xr[:, :]),
               reads=["s_xr"], writes=["s_xrb"])
            E_(lambda e: e.reduce_sum(s_mu[:, i:i + 1], s_xr[:, :], axis=AX.X),
               reads=["s_xr"], writes=["s_mu"])
        @A.on("scalar")
        def _(i=i):
            E_(lambda e: e.activation(s_scr[:], s_xr[:, :], ACT.Square,
                                      accum_out=s_sq[:, i:i + 1]),
               reads=["s_xr"], writes=["s_scr", "s_sq"])
        @A.on("sync")
        def _(i=i):
            E_(lambda e: e.dma_start(
                out=xbf_loc.ap()[128 * i:128 * (i + 1), :], in_=s_xrb[:, :]),
               reads=["s_xrb"], writes=[f"xbf_loc{(128 * i) * CH4 // p.TC}"],
               dma=True, dkey="xbl")
    @A.on("vector")
    def _():
        E_(lambda e: e.tensor_scalar_mul(s_mu[:], s_mu[:], 1.0 / H),
           reads=["s_mu"], writes=["s_mu"])
        E_(lambda e: e.tensor_scalar_mul(s_sq[:], s_sq[:], 1.0 / H),
           reads=["s_sq"], writes=["s_sq"])
        E_(lambda e: e.tensor_mul(s_t0[:], s_mu[:], s_mu[:]),
           reads=["s_mu"], writes=["s_t0"])
        E_(lambda e: e.tensor_sub(s_var[:], s_sq[:], s_t0[:]),
           reads=["s_sq", "s_t0"], writes=["s_var"])
        E_(lambda e: e.tensor_scalar_add(s_t0[:], s_var[:], EPS),
           reads=["s_var"], writes=["s_t0"])
    @A.on("scalar")
    def _():
        E_(lambda e: e.activation(s_r1[:], s_t0[:], ACT.Ln),
           reads=["s_t0"], writes=["s_r1"])
        E_(lambda e: e.activation(s_r1[:], s_r1[:], ACT.Exp, scale=-0.5),
           reads=["s_r1"], writes=["s_r1"])
    @A.on("vector")
    def _():
        E_(lambda e: e.tensor_mul(s_t0[:], s_var[:], s_r1[:]),
           reads=["s_var", "s_r1"], writes=["s_t0"])
        E_(lambda e: e.tensor_mul(s_t0[:], s_t0[:], s_r1[:]),
           reads=["s_t0", "s_r1"], writes=["s_t0"])
        E_(lambda e: e.tensor_scalar_add(s_t0[:], s_t0[:], EPS),
           reads=["s_t0"], writes=["s_t0"])
    @A.on("scalar")
    def _():
        E_(lambda e: e.activation(s_t0[:], s_t0[:], ACT.Ln),
           reads=["s_t0"], writes=["s_t0"])
        E_(lambda e: e.activation(s_t0[:], s_t0[:], ACT.Exp, scale=-0.5),
           reads=["s_t0"], writes=["s_t0"])
    @A.on("vector")
    def _():
        E_(lambda e: e.tensor_mul(s_ac[:, 0:NTT], s_r1[:], s_t0[:]),
           reads=["s_r1", "s_t0"], writes=["s_ac"])
        E_(lambda e: e.tensor_mul(s_ac[:, NTT:2 * NTT], s_ac[:, 0:NTT], s_mu[:]),
           reads=["s_ac", "s_mu"], writes=["s_ac"])
        E_(lambda e: e.tensor_scalar_mul(s_ac[:, NTT:2 * NTT], s_ac[:, NTT:2 * NTT], -1.0),
           reads=["s_ac"], writes=["s_ac"])
    @A.on("tensor")
    def _():
        E_(lambda e: e.transpose(ps_s[0:2 * NTT, 0:128], s_ac[:, 0:2 * NTT], s_idf[:]),
           reads=["s_ac", "s_idf"], writes=["ps_s"])
    @A.on("scalar")
    def _():
        E_(lambda e: e.copy(s_acT[:, :], ps_s[0:2 * NTT, 0:128]),
           reads=["ps_s"], writes=["s_acT"])
    @A.on("sync")
    def _():
        E_(lambda e: e.dma_start(out=st_loc.ap().rearrange("j (i q) -> (j i) q", q=128),
                                 in_=s_acT[:, :]),
           reads=["s_acT"], writes=["st_loc"], dma=True)
    @A.on("gpsimd")
    def _():
        for j4 in range(CH4):
            E_(lambda e, j4=j4: e.collective_compute(
                "AllGather", ALU.bypass, replica_groups=RG,
                ins=[xbf_loc.ap()[CSZ * j4:CSZ * (j4 + 1), :]],
                outs=[xbf.ap()[NCOR * CSZ * j4:NCOR * CSZ * (j4 + 1), :]]),
               reads=[f"xbf_loc{j4}"], writes=[f"xbf{j4}"], cc=True, dkey=f"agx{j4}")
            if j4 == 0:
                E_(lambda e: e.collective_compute(
                    "AllGather", ALU.bypass, replica_groups=RG,
                    ins=[st_loc.ap()], outs=[st_all.ap()]),
                   reads=["st_loc"], writes=["st_all"], cc=True)
        E_(lambda e: e.collective_compute("AllGather", ALU.bypass, replica_groups=RG,
                                          ins=[wo_loc.ap()], outs=[wo_bf.ap()]),
           reads=["wo_loc"], writes=["wo_bf"], cc=True)
        E_(lambda e: e.collective_compute("AllGather", ALU.bypass, replica_groups=RG,
                                          ins=[rw1_loc.ap()], outs=[rw1_bf.ap()]),
           reads=["rw1_loc"], writes=["rw1_bf"], cc=True)

    NZB = 16
    s_zz = sbP("s_zz", [128, NZB, H], BF16)
    @A.on("vector")
    def _():
        E_(lambda e: e.memset(s_zz[:, :, :], 0.0), writes=["s_zz"])
    @A.on("sync")
    def _():
        nrows = (NSLOT + 128) // 128
        j = 0
        while j < nrows:
            nb = min(NZB, nrows - j)
            E_(lambda e, j=j, nb=nb: e.dma_start(
                out=stg.ap()[128 * j:128 * (j + nb), :].rearrange("(a p) m -> p a m", p=128),
                in_=s_zz[:, 0:nb, :]),
               reads=["s_zz"], writes=["stg"], dma=True, dkey="stg_z")
            j += nb

    def fold_w(tag, s_out, obuf, s1_row, s2_row, b_row, bkey, w_dram):
        if tag == "g":
            @A.on("sync")
            def _():
                E_(lambda e: e.dma_start(out=s_wf[:],
                                         in_=w_dram.ap().rearrange("(a p) m -> p a m", p=128)),
                   writes=["s_wf"], dma=True)
        @A.on("tensor")
        def _():
            def mm_rows(e):
                for ki in range(HK):
                    e.matmul(ps_s[0:1, 0:EC], s_lnwc[:, ki:ki + 1], s_wf[:, ki, :],
                             start=(ki == 0), stop=(ki == HK - 1))
                for ki in range(HK):
                    last = e.matmul(ps_s[0:1, EC:2 * EC], s_lnbc[:, ki:ki + 1],
                                    s_wf[:, ki, :],
                                    start=(ki == 0), stop=(ki == HK - 1))
                return last
            E_(mm_rows, reads=["s_lnwc", "s_lnbc", "s_wf"], writes=["ps_s"])
        @A.on("scalar")
        def _():
            E_(lambda e: e.copy(s1_row[:], ps_s[0:1, 0:EC]),
               reads=["ps_s"], writes=[f"s1{tag}"])
            E_(lambda e: e.copy(s2_row[:], ps_s[0:1, EC:2 * EC]),
               reads=["ps_s"], writes=[f"s2{tag}"])
        @A.on("vector")
        def _():
            E_(lambda e: e.tensor_add(s2_row[:], s2_row[:], b_row[:]),
               reads=[f"s2{tag}", bkey], writes=[f"s2{tag}"])
        s12 = s_s12x if tag == "x" else s_s12g
        @A.on("gpsimd")
        def _():
            E_(lambda e: e.dma_start(out=s12[0:1, :], in_=s1_row[:]),
               reads=[f"s1{tag}"], writes=[f"s_s12{tag}"], dma=True)
            E_(lambda e: e.dma_start(out=s12[1:2, :], in_=s2_row[:]),
               reads=[f"s2{tag}"], writes=[f"s_s12{tag}"], dma=True)
            for ki in range(HK):
                E_(lambda e, ki=ki: e.tensor_scalar(s_out[:, ki, :], s_wf[:, ki, :],
                                                    s_lnwc[:, ki:ki + 1], None, ALU.mult),
                   reads=["s_wf", "s_lnwc"], writes=[obuf])

    fold_w("x", s_wxb, "s_wxb", s_s1x, s_s2x, s_bxr, "s_bxr", w_ip_x)
    fold_w("g", s_wgb, "s_wgb", s_s1g, s_s2g, s_bgr, "s_bgr", w_ip_g)

    A.fence()
    pctx.close()

    dctx = contextlib.ExitStack()
    sbD = mk_sb(dctx)
    s_xT = sbD("s_xT", [128, HK, TS], BF16)
    s_u0 = sbD("s_u0", [128, TS + KC - 1])
    s_cvt = sbD("s_cvt", [128, TS])
    s_xsb = sbD("s_xsb", [128, 2, NET, TS], BF16)
    s_dt = sbD("s_dt", [128, 2, NET, TS], BF16)
    s_dtx = sbD("s_dtx", [128, NET, TS], BF16)
    s_gs = sbD("s_gs", [128, 2, NET, TS], BF16)
    s_ab = sbD("s_ab", [128, TS], BF16)
    s_db64 = sbD("s_db64", [64, TS], BF16)
    s_h = sbD("s_h", [128, TS], BF16)
    s_yac = sbD("s_yac", [128, NET, TS], BF16)
    s_pb = sbD("s_pb", [128, TS], BF16)
    s_cb = sbD("s_cb", [128, 2, TS], BF16)
    s_co = sbD("s_co", [2, TS], BF16)
    s_dA2 = sbD("s_dA2", [128, TS])
    s_bb = sbD("s_bb", [128, 2, TS], BF16)
    s_ub = sbD("s_ub", [128, TS], BF16)
    s_dts = sbD("s_dts", [128, NCK], BF16)
    s_dA = s_u0
    s_u = s_cvt
    s_dbst = s_cvt

    @A.on("sync")
    def _():
        E_(lambda e: e.dma_start(out=s_co[1:2, :], in_=c_onesrow.ap()),
           writes=["s_co1"], dma=True)


    def chs(ch):
        return slice(ch * NCK, (ch + 1) * NCK)

    def load_xT(s):
        @A.on("sync")
        def _():
            for ch in range(NCH):
                tok0 = ch * NCK
                j4 = tok0 // CSZ
                off = tok0 % CSZ
                row0 = j4 * NCOR * CSZ + s * CSZ + off
                for ki in range(HK):
                    E_(lambda e, ki=ki, ch=ch, row0=row0: e.dma_start(
                        out=s_xT[:, ki, chs(ch)],
                        in_=xbf.ap()[row0:row0 + NCK, 128 * ki:128 * (ki + 1)],
                        transpose=True),
                       reads=[f"xbf{j4}"], writes=["s_xT"], dma=True)

    def emit_abcast(s):
        @A.on("gpsimd")
        def _():
            E_(lambda e, s=s: e.dma_start(
                out=s_ab[:, :],
                in_=st_all.ap()[2 * s:2 * s + 1, :].to_broadcast([128, TS])),
               reads=["st_all"], writes=["s_ab"], dma=True)
            E_(lambda e, s=s: e.dma_start(
                out=s_co[0:1, :], in_=st_all.ap()[2 * s + 1:2 * s + 2, :]),
               reads=["st_all"], writes=["s_co0"], dma=True)
    def emit_conv(s, m):
        par = s % 2
        @A.on("vector")
        def _():
            if s % p.SLB == 0:
                E_(lambda e: e.memset(s_u0[:, 0:KC - 1], 0.0), writes=["s_u0"])
            else:
                E_(lambda e, m=m: e.tensor_copy(s_u0[:, 0:KC - 1], s_halo[:, m, :]),
                   reads=[f"s_halo_{m}"], writes=["s_u0"])
            E_(lambda e, m=m: e.tensor_copy(s_halo[:, m, :], s_u0[:, TS:TS + KC - 1]),
               reads=["s_u0"], writes=[f"s_halo_{m}"])
            E_(lambda e, m=m: e.tensor_scalar(s_cvt[:, :], s_u0[:, 0:TS],
                                              s_cw[:, m, 0:1], None, ALU.mult),
               reads=["s_u0", "s_cw"], writes=["s_cvt"])
            for k in range(1, KC):
                E_(lambda e, k=k, m=m: e.scalar_tensor_tensor(
                    s_cvt[:, :], s_u0[:, k:k + TS], s_cw[:, m, k:k + 1],
                    s_cvt[:, :], ALU.mult, ALU.add),
                   reads=["s_u0", "s_cw", "s_cvt"], writes=["s_cvt"])
        @A.on("scalar")
        def _():
            E_(lambda e, m=m, par=par: e.activation(
                s_xsb[:, par, m, :], s_cvt[:, :], ACT.Silu, bias=s_cbc[:, m:m + 1]),
               reads=["s_cvt", "s_cbc"], writes=[f"s_xsb{par}"])

    def emit_proj(s, kind):
        par = s % 2
        wb, wkey = (s_wxb, "s_wxb") if kind == "x" else (s_wgb, "s_wgb")
        s12, s12k = (s_s12x, "s_s12x") if kind == "x" else (s_s12g, "s_s12g")
        for m in range(NET):
            for ch in range(NCH):
                bank = (m * NCH + ch) % 2
                pm, pk = ps_m[bank], f"ps_m{bank}"
                @A.on("tensor")
                def _(m=m, ch=ch, pm=pm, pk=pk):
                    def mms(e):
                        for ki in range(HK):
                            e.matmul(pm[:, 0:NCK], wb[:, ki, 128 * m:128 * (m + 1)],
                                     s_xT[:, ki, chs(ch)], start=(ki == 0), stop=False)
                        return e.matmul(pm[:, 0:NCK], s12[0:2, 128 * m:128 * (m + 1)],
                                        s_co[0:2, chs(ch)], start=False, stop=True)
                    E_(mms, reads=[wkey, "s_xT", s12k, "s_co0", "s_co1"],
                       writes=[pk])
                @A.on("vector")
                def _(m=m, ch=ch, pm=pm, pk=pk):
                    if kind == "x":
                        E_(lambda e: e.tensor_tensor(
                            s_u0[:, KC - 1 + ch * NCK: KC - 1 + (ch + 1) * NCK],
                            pm[:, 0:NCK], s_ab[:, chs(ch)], ALU.mult),
                           reads=[pk, "s_ab"], writes=["s_u0"])
                    else:
                        E_(lambda e: e.tensor_tensor(
                            s_cvt[:, chs(ch)], pm[:, 0:NCK], s_ab[:, chs(ch)], ALU.mult),
                           reads=[pk, "s_ab"], writes=["s_cvt"])
            if kind == "x":
                emit_conv(s, m)
            else:
                @A.on("scalar")
                def _(m=m, par=par):
                    E_(lambda e: e.activation(
                        s_gs[:, par, m, :], s_cvt[:, :], ACT.Sigmoid),
                       reads=["s_cvt"], writes=[f"s_gs{par}"])

    def emit_dbc(s):
        par = s % 2
        for ch in range(NCH):
            @A.on("tensor")
            def _(ch=ch):
                def mms(e):
                    e.matmul(ps_db[:, 0:NCK], s_xpw[:, 0, :],
                             s_xsb[:, par, 0, chs(ch)], start=True, stop=(NET == 1))
                    if NET > 1:
                        return e.matmul(ps_db[:, 0:NCK], s_xpw[:, 1, :],
                                        s_xsb[:, par, 1, chs(ch)],
                                        start=False, stop=True)
                E_(mms, reads=["s_xpw", f"s_xsb{par}"], writes=["ps_db"])
            @A.on("scalar")
            def _(ch=ch):
                E_(lambda e: e.copy(s_dbst[0:96, chs(ch)], ps_db[:, 0:NCK]),
                   reads=["ps_db"], writes=["s_cvt"])
        @A.on("sync")
        def _():
            E_(lambda e, s=s: e.dma_start(out=dbc_par.ap()[s], in_=s_dbst[0:96, :]),
               reads=["s_cvt"], writes=["dbc_par"], dma=True)
        @A.on("gpsimd")
        def _():
            E_(lambda e, s=s: e.collective_compute(
                "AllReduce", ALU.add, replica_groups=RG,
                ins=[dbc_par.ap()[s]], outs=[dbc_sum.ap()[s]]),
               reads=["dbc_par"], writes=["dbc_sum"], cc=True)
        if debug:
            @A.on("sync")
            def _():
                E_(lambda e, s=s: e.dma_start(out=outs["o_dbc"].ap()[:, s * TS:(s + 1) * TS],
                                              in_=dbc_sum.ap()[s]),
                   reads=["dbc_sum"], writes=["o_dbc"], dma=True)

    def emit_M(s):
        load_xT(s)
        emit_abcast(s)
        emit_proj(s, "x")
        emit_proj(s, "g")
        emit_dbc(s)
        if debug:
            @A.on("sync")
            def _():
                E_(lambda e, s=s: e.dma_start(
                    out=outs["o_xscan"].ap().rearrange("(a p) t -> p a t", p=128)[:, :, s * TS:(s + 1) * TS],
                    in_=s_xsb[:, s % 2, :, :]),
                   reads=[f"s_xsb{s % 2}"], writes=["o_xscan"], dma=True)

    def dt_pieces(s):
        par = s % 2
        pieces = []
        def head():
            @A.on("gpsimd")
            def _():
                E_(lambda e, s=s: e.dma_start(out=s_db64[:, :], in_=dbc_sum.ap()[s][0:64, :]),
                   reads=["dbc_sum"], writes=["s_db64"], dma=True)
                E_(lambda e, s=s: e.dma_start(out=dbc_bf.ap()[s % 2],
                                              in_=dbc_sum.ap()[s][64:96, :]),
                   reads=["dbc_sum"], writes=[f"dbc_bf{s % 2}"], dma=True)
        pieces.append(head)
        def chunk(m, ch):
            bank = (m * NCH + ch) % 2
            pm, pk = ps_m[bank], f"ps_m{bank}"
            @A.on("tensor")
            def _(m=m, ch=ch, pm=pm, pk=pk):
                E_(lambda e: e.matmul(
                    pm[:, 0:NCK], s_dpw[:, 128 * m:128 * (m + 1)],
                    s_db64[:, chs(ch)], start=True, stop=True),
                   reads=["s_dpw", "s_db64"], writes=[pk])
            @A.on("scalar")
            def _(m=m, ch=ch, pm=pm, pk=pk):
                E_(lambda e: e.activation(
                    s_u[:, 0:NCK], pm[:, 0:NCK], ACT.Abs, bias=s_dpb[:, m:m + 1]),
                   reads=[pk, "s_dpb"], writes=["s_cvt"])
                E_(lambda e: e.activation(s_u[:, 0:NCK], s_u[:, 0:NCK], ACT.Exp,
                                          scale=-1.0),
                   reads=["s_cvt"], writes=["s_cvt"])
                E_(lambda e: e.activation(s_u[:, 0:NCK], s_u[:, 0:NCK], ACT.Ln,
                                          bias=1.0),
                   reads=["s_cvt"], writes=["s_cvt"])
                E_(lambda e: e.activation(
                    s_dts[:, :], pm[:, 0:NCK], ACT.Relu, bias=s_dpb[:, m:m + 1]),
                   reads=[pk, "s_dpb"], writes=["s_dts"])
            @A.on("vector")
            def _(m=m, ch=ch, par=par):
                v2 = lambda ap: ap.rearrange("p (a b) -> p a b", a=2)
                E_(lambda e: e.tensor_tensor(
                    v2(s_dt[:, par, m, chs(ch)]), v2(s_u[:, 0:NCK]),
                    v2(s_dts[:, :]), ALU.add),
                   reads=["s_cvt", "s_dts"], writes=[f"s_dt{par}"])
        for m in range(NET):
            for ch in range(NCH):
                pieces.append(lambda m=m, ch=ch: chunk(m, ch))
        return pieces

    def emit_dt_debug(s):
        par = s % 2
        if debug:
            @A.on("sync")
            def _():
                E_(lambda e, s=s: e.dma_start(
                    out=outs["o_dt"].ap().rearrange("(a p) t -> p a t", p=128)[:, :, s * TS:(s + 1) * TS],
                    in_=s_dt[:, par, :, :]),
                   reads=[f"s_dt{par}"], writes=["o_dt"], dma=True)

    def emit_scan(s, inject=()):
        par = s % 2
        inject = list(inject)
        @A.on("vector")
        def _():
            v2 = lambda ap: ap.rearrange("p (a b) -> p a b", a=2)
            for m in range(NET):
                E_(lambda e, m=m: e.tensor_tensor(
                    v2(s_dtx[:, m, :]), v2(s_dt[:, par, m, :]),
                    v2(s_xsb[:, par, m, :]), ALU.mult),
                   reads=[f"s_dt{par}", f"s_xsb{par}"], writes=["s_dtx"])
            if s % p.SLB == 0:
                E_(lambda e: e.memset(s_hst[:], 0.0), writes=["s_hst"])
        for n in range(NST):
            if NST - len(inject) <= n:
                inject.pop(0)()
            npar = n % 2
            @A.on("sync")
            def _(n=n, npar=npar):
                E_(lambda e: e.dma_start(
                    out=s_bb[:, npar, :],
                    in_=dbc_bf.ap()[par, n:n + 1, :].to_broadcast([128, TS])),
                   reads=[f"dbc_bf{par}"], writes=[f"s_bb{npar}"], dma=True)
            @A.on("scalar")
            def _(n=n, npar=npar):
                E_(lambda e: e.dma_start(
                    out=s_cb[:, npar, :],
                    in_=dbc_bf.ap()[par, 16 + n:17 + n, :].to_broadcast([128, TS])),
                   reads=[f"dbc_bf{par}"], writes=[f"s_cb{npar}"], dma=True)
            dabs = [(s_dA[:, 0:TS], "s_u0"), (s_dA2[:, :], "s_dA2")][:NET]
            @A.on("scalar")
            def _(n=n, dabs=dabs):
                def exps(e):
                    r = None
                    for m, (dAb, dAk) in enumerate(dabs):
                        r = e.activation(dAb, s_dt[:, par, m, :], ACT.Exp,
                                         scale=s_A[:, m, n:n + 1])
                    return r
                E_(exps, reads=[f"s_dt{par}", "s_A"],
                   writes=[k for _, k in dabs])
            for m in range(NET):
                dAb, dAk = dabs[m]
                v2 = lambda ap: ap.rearrange("p (a b) -> p a b", a=2)
                @A.on("vector")
                def _(n=n, m=m, dAb=dAb, dAk=dAk, npar=npar, v2=v2):
                    E_(lambda e: e.tensor_tensor(
                        v2(s_ub[:, :]), v2(s_dtx[:, m, :]), v2(s_bb[:, npar, :]),
                        ALU.mult),
                       reads=["s_dtx", f"s_bb{npar}"], writes=["s_ub"])
                    E_(lambda e: e.tensor_tensor_scan(
                        s_h[:, :], dAb, s_ub[:, :], s_hst[:, m, n:n + 1],
                        ALU.mult, ALU.add),
                       reads=[dAk, "s_ub", "s_hst"], writes=["s_h"])
                v2 = lambda ap: ap.rearrange("p (a b) -> p a b", a=2)
                @A.on("vector")
                def _(n=n, m=m, npar=npar, v2=v2):
                    E_(lambda e: e.tensor_copy(s_hst[:, m, n:n + 1], s_h[:, TS - 1:TS]),
                       reads=["s_h"], writes=["s_hst"])
                    if n == 0:
                        E_(lambda e: e.tensor_tensor(
                            v2(s_yac[:, m, :]), v2(s_h[:, :]), v2(s_cb[:, npar, :]),
                            ALU.mult),
                           reads=["s_h", f"s_cb{npar}"], writes=[f"s_yac{m}"])
                    else:
                        E_(lambda e: e.tensor_tensor(
                            v2(s_pb[:, :]), v2(s_h[:, :]), v2(s_cb[:, npar, :]),
                            ALU.mult),
                           reads=["s_h", f"s_cb{npar}"], writes=["s_pb"])
                        E_(lambda e: e.tensor_tensor(
                            v2(s_yac[:, m, :]), v2(s_yac[:, m, :]), v2(s_pb[:, :]),
                            ALU.add),
                           reads=[f"s_yac{m}", "s_pb"], writes=[f"s_yac{m}"])

    def emit_ytail(s):
        par = s % 2
        @A.on("vector")
        def _():
            v2 = lambda ap: ap.rearrange("p (a b) -> p a b", a=2)
            for m in range(NET):
                E_(lambda e, m=m: e.scalar_tensor_tensor(
                    v2(s_yac[:, m, :]), v2(s_xsb[:, par, m, :]), s_D[:, m:m + 1],
                    v2(s_yac[:, m, :]), ALU.mult, ALU.add),
                   reads=[f"s_xsb{par}", "s_D", f"s_yac{m}"], writes=[f"s_yac{m}"])
                E_(lambda e, m=m: e.tensor_tensor(
                    v2(s_yac[:, m, :]), v2(s_yac[:, m, :]), v2(s_gs[:, par, m, :]),
                    ALU.mult),
                   reads=[f"s_yac{m}", f"s_gs{par}"], writes=[f"s_yac{m}"])
        @A.on("gpsimd")
        def _():
            E_(lambda e, s=s: e.dma_start(
                out=a2a_in.ap()[s].rearrange("(a p) t -> p a t", p=128),
                in_=s_yac[:, :, :]),
               reads=["s_yac0", "s_yac1"], writes=["a2a_in"], dma=True)

    emit_M(0)
    for piece in dt_pieces(0):
        piece()
    emit_dt_debug(0)
    for s in range(NSL):
        if s + 1 < NSL:
            emit_M(s + 1)
            emit_scan(s, inject=dt_pieces(s + 1))
            emit_dt_debug(s + 1)
        else:
            emit_scan(s)
        emit_ytail(s)

    @A.on("gpsimd")
    def _():
        with allow_ici_all_to_all():
            E_(lambda e: e.collective_compute(
                "AllToAll", ALU.bypass, replica_groups=RG,
                ins=[a2a_in.ap()], outs=[a2a_out.ap()]),
               reads=["a2a_in"], writes=["a2a_out"], cc=True)
    if debug:
        @A.on("sync")
        def _():
            E_(lambda e: e.dma_start(out=outs["o_yg"].ap(), in_=a2a_out.ap()),
               reads=["a2a_out"], writes=["o_yg"], dma=True)

    A.fence()
    dctx.close()
    mctx.close()

    NTT8 = NTT * 8
    fctx = contextlib.ExitStack()
    sbF = mk_sb(fctx)
    psF = lambda n, s, d=F32: fctx.enter_context(nc.psum_tensor(n, s, d))
    ps_f = psF("ps_f", [96, 512])
    ps_f2 = [psF("ps_f2_0", [128, 512]), psF("ps_f2_1", [128, 512])]
    ps_t = psF("ps_t", [128, 512], BF16)
    s_m3 = sbF("s_m3", [128, NTT])
    s_q3 = sbF("s_q3", [128, NTT])
    s_r3 = sbF("s_r3", [128, NTT])
    s_mgA = sbF("s_mgA", [128, NTT])
    s_mgB = sbF("s_mgB", [128, NTT])
    s_wA = sbF("s_wA", [128, NTT])
    s_wB = sbF("s_wB", [128, NTT])
    s_edt = sbF("s_edt", [128, NTT])
    s_siA = sbF("s_siA", [128, NTT], I32)
    s_siB = sbF("s_siB", [128, NTT], I32)
    s_giA = sbF("s_giA", [128, NTT], I32)
    s_giB = sbF("s_giB", [128, NTT], I32)
    s_kA = sbF("s_kA", [128, NTT])
    s_kB = sbF("s_kB", [128, NTT])
    s_tvA = sbF("s_tvA", [128, NTT])
    s_tvB = sbF("s_tvB", [128, NTT])
    s_g0 = sbF("s_g0", [128, 2, H], BF16)
    s_g1 = sbF("s_g1", [128, 2, H], BF16)
    s_ot = sbF("s_ot", [128, 2, H])
    s_wE = sbF("s_wE", [128, 2 * 4096], BF16)
    @A.on("sync")
    def _():
        for k in range(2):
            E_(lambda e, k=k: e.dma_start(
                out=s_wE[:, 4096 * k:4096 * (k + 1)],
                in_=myw1_bf.ap()[128 * k:128 * (k + 1), :]),
               reads=["myw1_bf"], writes=["s_wE"], dma=True)
    if debug:
        s_dbg = sbF("s_dbg", [128, NTT, 8])
    f2ctx = contextlib.ExitStack()
    sbF2 = mk_sb(f2ctx)
    s_io8 = sbF2("s_io8", [128, NTT, 8])
    s_mx1 = sbF2("s_mx1", [128, NTT])
    s_mx2 = sbF2("s_mx2", [128, NTT])
    s_i1 = sbF2("s_i1", [128, NTT])
    s_i2 = sbF2("s_i2", [128, NTT])
    s_ohA = sbF2("s_ohA", [128, NTT, 8])
    s_ohB = sbF2("s_ohB", [128, NTT, 8])
    s_ohAb = sbF2("s_ohAb", [128, NTT, 8], BF16)
    s_ohBb = sbF2("s_ohBb", [128, NTT, 8], BF16)
    s_sel = sbF2("s_sel", [128, NTT, 8])
    s_S = sbF2("s_S", [128, NTT, 8])
    s_pfA = sbF2("s_pfA", [128, NTT, 8])
    s_pfB = sbF2("s_pfB", [128, NTT, 8])
    s_off = sbF2("s_off", [1, 2 * NTT * 8])
    s_run = sbF2("s_run", [1, 16])
    s_offb = sbF2("s_offb", [128, 2 * NTT * 8])
    s_cnt = sbF2("s_cnt", [1, 8])
    s_ca = sbF2("s_ca", [8, 8])
    s_gb = sbF2("s_gb", [1, 8])
    s_gbr = sbF2("s_gbr", [1, NTT * 8])
    s_gbb = sbF2("s_gbb", [128, NTT * 8])
    fxctx = contextlib.ExitStack()
    s_x3b = mk_sb(fxctx)("s_x3b", [128, NTT, H], BF16)
    f0ctx = contextlib.ExitStack()
    sbF0 = mk_sb(f0ctx)
    s_yg2 = sbF0("s_yg2", [128, 16, p.TC], BF16)
    s_wo = sbF0("s_wo", [128, 16, H], BF16)
    s_xrf = sbF0("s_xrf", [128, H])
    s_x1t = sbF0("s_x1t", [128, H])
    s_zt = s_ot[:, 0, :]

    @A.on("sync")
    def _():
        for k in range(16):
            E_(lambda e, k=k: e.dma_start(out=s_yg2[:, k, :],
                                          in_=a2a_out.ap()[128 * k:128 * (k + 1), :]),
               reads=["a2a_out"], writes=["s_yg2"], dma=True)
        for k in range(16):
            E_(lambda e, k=k: e.dma_start(out=s_wo[:, k, :],
                                          in_=wo_bf.ap()[128 * k:128 * (k + 1), :]),
               reads=["wo_bf"], writes=["s_wo"], dma=True)

    for tt in range(NTT):
        @A.on("sync")
        def _(tt=tt):
            E_(lambda e: e.dma_start(
                out=s_xrf[:, :],
                in_=x_rows.ap()[128 * tt:128 * (tt + 1), :]),
               writes=["s_xrf"], dma=True)
        for dc in range(2):
            pm, pk = ps_m[dc % 2], f"ps_m{dc % 2}"
            @A.on("tensor")
            def _(tt=tt, dc=dc, pm=pm, pk=pk):
                def mms(e):
                    for k in range(16):
                        e.matmul(pm[:, 0:512], s_yg2[:, k, 128 * tt:128 * (tt + 1)],
                                 s_wo[:, k, 512 * dc:512 * (dc + 1)],
                                 start=(k == 0), stop=False)
                    return e.matmul(pm[:, 0:512], s_ones1[:, :],
                                    s_wobr[0:1, 512 * dc:512 * (dc + 1)],
                                    start=False, stop=True)
                E_(mms, reads=["s_yg2", "s_wo", "s_wobr", "s_ones1"], writes=[pk])
            @A.on("vector")
            def _(tt=tt, dc=dc, pm=pm, pk=pk):
                E_(lambda e: e.tensor_add(s_x1t[:, 512 * dc:512 * (dc + 1)],
                                          pm[:, 0:512], s_xrf[:, 512 * dc:512 * (dc + 1)]),
                   reads=[pk, "s_xrf"], writes=["s_x1t"])
        @A.on("vector")
        def _(tt=tt):
            E_(lambda e: e.tensor_scalar(s_zt[:, :], s_xrf[:, :],
                                         s_mu[:, tt:tt + 1], s_r1[:, tt:tt + 1],
                                         ALU.subtract, ALU.mult),
               reads=["s_xrf", "s_mu", "s_r1"], writes=["s_ot"])
            E_(lambda e: e.tensor_add(s_x1t[:, :], s_x1t[:, :], s_zt[:, :]),
               reads=["s_x1t", "s_ot"], writes=["s_x1t"])
            E_(lambda e, tt=tt: e.reduce_sum(s_m3[:, tt:tt + 1], s_x1t[:, :], axis=AX.X),
               reads=["s_x1t"], writes=["s_m3"])
        @A.on("scalar")
        def _(tt=tt):
            E_(lambda e: e.activation(s_zt[:, :], s_x1t[:, :], ACT.Square,
                                      accum_out=s_q3[:, tt:tt + 1]),
               reads=["s_x1t"], writes=["s_ot", "s_q3"])
        @A.on("sync")
        def _(tt=tt):
            E_(lambda e: e.dma_start(out=x1_sp.ap()[128 * tt:128 * (tt + 1), :],
                                     in_=s_x1t[:, :]),
               reads=["s_x1t"], writes=["x1_sp"], dma=True)
    @A.on("vector")
    def _():
        E_(lambda e: e.tensor_scalar_mul(s_m3[:], s_m3[:], 1.0 / H),
           reads=["s_m3"], writes=["s_m3"])
        E_(lambda e: e.tensor_scalar_mul(s_q3[:], s_q3[:], 1.0 / H),
           reads=["s_q3"], writes=["s_q3"])
        E_(lambda e: e.tensor_mul(s_r3[:], s_m3[:], s_m3[:]),
           reads=["s_m3"], writes=["s_r3"])
        E_(lambda e: e.tensor_sub(s_r3[:], s_q3[:], s_r3[:]),
           reads=["s_q3", "s_r3"], writes=["s_r3"])
        E_(lambda e: e.tensor_scalar_add(s_r3[:], s_r3[:], EPS),
           reads=["s_r3"], writes=["s_r3"])
    @A.on("scalar")
    def _():
        E_(lambda e: e.activation(s_r3[:], s_r3[:], ACT.Ln),
           reads=["s_r3"], writes=["s_r3"])
        E_(lambda e: e.activation(s_r3[:], s_r3[:], ACT.Exp, scale=-0.5),
           reads=["s_r3"], writes=["s_r3"])
    for tt in range(NTT):
        @A.on("sync")
        def _(tt=tt):
            E_(lambda e: e.dma_start(out=s_x1t[:, :],
                                     in_=x1_sp.ap()[128 * tt:128 * (tt + 1), :]),
               reads=["x1_sp"], writes=["s_x1t"], dma=True)
        @A.on("vector")
        def _(tt=tt):
            E_(lambda e: e.tensor_scalar(s_x3b[:, tt, :], s_x1t[:, :],
                                         s_m3[:, tt:tt + 1], s_r3[:, tt:tt + 1],
                                         ALU.subtract, ALU.mult),
               reads=["s_x1t", "s_m3", "s_r3"], writes=["s_x3b"])
        @A.on("scalar")
        def _(tt=tt):
            E_(lambda e: e.dma_start(out=x3_sp.ap()[128 * tt:128 * (tt + 1), :],
                                     in_=s_x3b[:, tt, :]),
               reads=["s_x3b"], writes=[f"x3_sp{(128 * tt) // 512}"],
               dma=True, dkey="x3s")
    if debug:
        @A.on("sync")
        def _():
            E_(lambda e: e.dma_start(out=outs["o_x1"].ap(), in_=x1_sp.ap()),
               reads=["x1_sp"], writes=["o_x1"], dma=True)

    A.fence()
    f0ctx.close()

    f1ctx = contextlib.ExitStack()
    sbF1 = mk_sb(f1ctx)
    s_x3T = sbF1("s_x3T", [128, HK, p.TC], BF16)
    s_rw1 = sbF1("s_rw1", [128, HK, 4096], BF16)
    s_hr1 = sbF1("s_hr1", [128, 512], BF16)
    s_lg8 = sbF1("s_lg8", [8, p.TC])
    s_lgt = sbF1("s_lgt", [128, NTT, 8])

    @A.on("sync")
    def _():
        for k in range(HK):
            E_(lambda e, k=k: e.dma_start(out=s_rw1[:, k, :],
                                          in_=rw1_bf.ap()[128 * k:128 * (k + 1), :]),
               reads=["rw1_bf"], writes=["s_rw1"], dma=True)
    @A.on("sync")
    def _():
        NC2 = (p.TC + 511) // 512
        for c2 in range(NC2):
            def ldx3(e, c2=c2):
                w = min(512, p.TC - 512 * c2)
                return [e.dma_start(
                    out=s_x3T[:, ki, 512 * c2:512 * c2 + w],
                    in_=x3_sp.ap()[512 * c2:512 * c2 + w,
                                   128 * ki:128 * (ki + 1)],
                    transpose=True) for ki in range(HK)]
            E_(ldx3, reads=[f"x3_sp{c2}"], writes=[f"s_x3T{c2}"], dma=True,
               dkey=f"ldx3{c2 % 2}")
    NCK2 = min(512, p.TC)
    NCH2 = p.TC // NCK2
    for nch in range(NCH2):
        for mt in range(32):
            pm, pk = ps_m[mt % 2], f"ps_m{mt % 2}"
            @A.on("tensor")
            def _(nch=nch, mt=mt, pm=pm, pk=pk):
                def mms(e):
                    for k in range(HK):
                        r = e.matmul(pm[:, 0:NCK2], s_rw1[:, k, 128 * mt:128 * (mt + 1)],
                                     s_x3T[:, k, NCK2 * nch:NCK2 * (nch + 1)],
                                     start=(k == 0), stop=(k == HK - 1))
                    return r
                E_(mms, reads=["s_rw1", f"s_x3T{nch}"], writes=[pk])
            @A.on("vector")
            def _(nch=nch, mt=mt, pm=pm, pk=pk):
                E_(lambda e: e.tensor_scalar(s_hr1[:, 0:NCK2], pm[:, 0:NCK2],
                                             s_rb1c[:, mt:mt + 1], 0.0,
                                             ALU.add, ALU.max),
                   reads=[pk, "s_rb1c"], writes=["s_hr1"])
            @A.on("tensor")
            def _(nch=nch, mt=mt):
                def mm2(e):
                    r = e.matmul(ps_f[0:8, 0:NCK2], s_rw2b[:, mt, :], s_hr1[:, 0:NCK2],
                                 start=(mt == 0), stop=False)
                    if mt == 31:
                        r = e.matmul(ps_f[0:8, 0:NCK2], s_rb2r[0:1, :],
                                     s_onesr[0:1, 0:NCK2], start=False, stop=True)
                    return r
                E_(mm2, reads=["s_rw2b", "s_hr1", "s_rb2r", "s_onesr"], writes=["ps_f"])
        @A.on("scalar")
        def _(nch=nch):
            E_(lambda e: e.copy(s_lg8[:, NCK2 * nch:NCK2 * (nch + 1)], ps_f[0:8, 0:NCK2]),
               reads=["ps_f"], writes=["s_lg8"])
    for tt in range(NTT):
        @A.on("tensor")
        def _(tt=tt):
            E_(lambda e: e.transpose(ps_s[:, 0:8],
                                     s_lg8[0:8, 128 * tt:128 * (tt + 1)],
                                     s_idf[0:8, 0:8]),
               reads=["s_lg8", "s_idf"], writes=["ps_s"])
        @A.on("scalar")
        def _(tt=tt):
            E_(lambda e: e.copy(s_lgt[:, tt, :], ps_s[:, 0:8]),
               reads=["ps_s"], writes=["s_lgt"])
    if debug:
        @A.on("sync")
        def _():
            E_(lambda e: e.dma_start(
                out=outs["o_lg"].ap().rearrange("(a p) m -> p a m", p=128),
                in_=s_lgt[:, :, :]),
               reads=["s_lgt"], writes=["o_lg"], dma=True)

    lg3 = s_lgt[:, :, :]
    @A.on("gpsimd")
    def _():
        E_(lambda e: e.iota(s_io8[:, :, :], [[0, NTT], [1, 8]], channel_multiplier=0,
                            allow_small_or_imprecise_dtypes=True),
           writes=["s_io8"])
    @A.on("vector")
    def _():
        E_(lambda e: e.reduce_max(s_mx1[:, :], lg3, axis=AX.X),
           reads=["s_lgt"], writes=["s_mx1"])
        E_(lambda e: e.tensor_tensor(s_ohA[:, :, :], lg3,
                                     s_mx1[:, :].to_broadcast([128, NTT, 8]),
                                     ALU.is_equal),
           reads=["s_lgt", "s_mx1"], writes=["s_ohA"])
        E_(lambda e: e.scalar_tensor_tensor(s_sel[:, :, :], s_ohA[:, :, :], -99.0,
                                            s_io8[:, :, :], ALU.mult, ALU.add),
           reads=["s_ohA", "s_io8"], writes=["s_sel"])
        E_(lambda e: e.tensor_reduce(s_i1[:, :], s_sel[:, :, :], axis=AX.X, op=ALU.min),
           reads=["s_sel"], writes=["s_i1"])
        E_(lambda e: e.tensor_scalar_add(s_i1[:, :], s_i1[:, :], 99.0),
           reads=["s_i1"], writes=["s_i1"])
        E_(lambda e: e.tensor_tensor(s_ohA[:, :, :], s_io8[:, :, :],
                                     s_i1[:, :].to_broadcast([128, NTT, 8]),
                                     ALU.is_equal),
           reads=["s_io8", "s_i1"], writes=["s_ohA"])
        E_(lambda e: e.scalar_tensor_tensor(s_S[:, :, :], s_ohA[:, :, :], -1e30,
                                            lg3, ALU.mult, ALU.add),
           reads=["s_ohA", "s_lgt"], writes=["s_S"])
        E_(lambda e: e.reduce_max(s_mx2[:, :], s_S[:, :, :], axis=AX.X),
           reads=["s_S"], writes=["s_mx2"])
        E_(lambda e: e.tensor_tensor(s_ohB[:, :, :], s_S[:, :, :],
                                     s_mx2[:, :].to_broadcast([128, NTT, 8]),
                                     ALU.is_equal),
           reads=["s_S", "s_mx2"], writes=["s_ohB"])
        E_(lambda e: e.scalar_tensor_tensor(s_sel[:, :, :], s_ohB[:, :, :], -99.0,
                                            s_io8[:, :, :], ALU.mult, ALU.add),
           reads=["s_ohB", "s_io8"], writes=["s_sel"])
        E_(lambda e: e.tensor_reduce(s_i2[:, :], s_sel[:, :, :], axis=AX.X, op=ALU.min),
           reads=["s_sel"], writes=["s_i2"])
        E_(lambda e: e.tensor_scalar_add(s_i2[:, :], s_i2[:, :], 99.0),
           reads=["s_i2"], writes=["s_i2"])
        E_(lambda e: e.tensor_tensor(s_ohB[:, :, :], s_io8[:, :, :],
                                     s_i2[:, :].to_broadcast([128, NTT, 8]),
                                     ALU.is_equal),
           reads=["s_io8", "s_i2"], writes=["s_ohB"])
        E_(lambda e: e.tensor_mul(s_sel[:, :, :], s_ohA[:, :, :], lg3),
           reads=["s_ohA", "s_lgt"], writes=["s_sel"])
        E_(lambda e: e.reduce_sum(s_tvA[:, :], s_sel[:, :, :], axis=AX.X),
           reads=["s_sel"], writes=["s_tvA"])
        E_(lambda e: e.tensor_mul(s_sel[:, :, :], s_ohB[:, :, :], lg3),
           reads=["s_ohB", "s_lgt"], writes=["s_sel"])
        E_(lambda e: e.reduce_sum(s_tvB[:, :], s_sel[:, :, :], axis=AX.X),
           reads=["s_sel"], writes=["s_tvB"])
        E_(lambda e: e.tensor_sub(s_edt[:, :], s_tvB[:, :], s_tvA[:, :]),
           reads=["s_tvB", "s_tvA"], writes=["s_edt"])
        E_(lambda e: e.tensor_copy(s_ohAb[:, :, :], s_ohA[:, :, :]),
           reads=["s_ohA"], writes=["s_ohAb"])
        E_(lambda e: e.tensor_copy(s_ohBb[:, :, :], s_ohB[:, :, :]),
           reads=["s_ohB"], writes=["s_ohBb"])
    @A.on("scalar")
    def _():
        E_(lambda e: e.activation(s_edt[:, :], s_edt[:, :], ACT.Exp),
           reads=["s_edt"], writes=["s_edt"])
    @A.on("vector")
    def _():
        E_(lambda e: e.tensor_scalar_add(s_wA[:, :], s_edt[:, :], 1.0),
           reads=["s_edt"], writes=["s_wA"])
        E_(lambda e: e.reciprocal(s_wA[:, :], s_wA[:, :]),
           reads=["s_wA"], writes=["s_wA"])
        E_(lambda e: e.tensor_mul(s_wB[:, :], s_edt[:, :], s_wA[:, :]),
           reads=["s_edt", "s_wA"], writes=["s_wB"])
    for tt in range(NTT):
        for st, ohb, pf in [(0, s_ohAb, s_pfA), (1, s_ohBb, s_pfB)]:
            @A.on("tensor")
            def _(tt=tt, st=st, ohb=ohb):
                def mms(e):
                    e.matmul(ps_f2[st][:, 0:8], s_trib[:, :], ohb[:, tt, :],
                             start=True, stop=True)
                    return e.matmul(ps_s[0:1, NTT8 * st + 8 * tt:NTT8 * st + 8 * tt + 8],
                                    s_onec[:, :], ohb[:, tt, :], start=True, stop=True)
                E_(mms, reads=["s_ohAb" if st == 0 else "s_ohBb", "s_trib", "s_onec"],
                   writes=[f"ps_f2{st}", "ps_s"])
            @A.on("vector")
            def _(tt=tt, st=st, pf=pf):
                E_(lambda e: e.tensor_copy(pf[:, tt, :], ps_f2[st][:, 0:8]),
                   reads=[f"ps_f2{st}"], writes=["s_pfA" if st == 0 else "s_pfB"])
    @A.on("vector")
    def _():
        for st in range(2):
            E_(lambda e, st=st: e.memset(s_run[:, 8 * st:8 * st + 8], 0.0),
               writes=["s_run"])
            for tt in range(NTT):
                E_(lambda e, st=st, tt=tt: e.tensor_copy(
                    s_off[0:1, NTT8 * st + 8 * tt:NTT8 * st + 8 * tt + 8],
                    s_run[0:1, 8 * st:8 * st + 8]),
                   reads=["s_run"], writes=["s_off"])
                E_(lambda e, st=st, tt=tt: e.tensor_add(
                    s_run[0:1, 8 * st:8 * st + 8], s_run[0:1, 8 * st:8 * st + 8],
                    ps_s[0:1, NTT8 * st + 8 * tt:NTT8 * st + 8 * tt + 8]),
                   reads=["s_run", "ps_s"], writes=["s_run"])
        E_(lambda e: e.tensor_add(s_cnt[0:1, :], s_run[0:1, 0:8], s_run[0:1, 8:16]),
           reads=["s_run"], writes=["s_cnt"])
    @A.on("sync")
    def _():
        E_(lambda e: e.dma_start(out=cnt_loc.ap(), in_=s_cnt[0:1, :]),
           reads=["s_cnt"], writes=["cnt_loc"], dma=True)
    @A.on("gpsimd")
    def _():
        E_(lambda e: e.collective_compute("AllGather", ALU.bypass, replica_groups=RG,
                                          ins=[cnt_loc.ap()], outs=[cnt_all.ap()]),
           reads=["cnt_loc"], writes=["cnt_all"], cc=True)
    @A.on("sync")
    def _():
        E_(lambda e: e.dma_start(out=s_ca[:, :], in_=cnt_all.ap()),
           reads=["cnt_all"], writes=["s_ca"], dma=True)
    @A.on("tensor")
    def _():
        E_(lambda e: e.matmul(ps_s[0:1, 2 * NTT8:2 * NTT8 + 8], s_pfm[:, :], s_ca[:, :],
                              start=True, stop=True),
           reads=["s_pfm", "s_ca"], writes=["ps_s"])
    @A.on("scalar")
    def _():
        E_(lambda e: e.copy(s_gb[0:1, :], ps_s[0:1, 2 * NTT8:2 * NTT8 + 8]),
           reads=["ps_s"], writes=["s_gb"])
    @A.on("vector")
    def _():
        for tt in range(NTT):
            E_(lambda e, tt=tt: e.tensor_copy(s_gbr[0:1, 8 * tt:8 * tt + 8], s_gb[0:1, :]),
               reads=["s_gb"], writes=["s_gbr"])
    for half, mv, mk_, dst, dk in [(0, s_off, "s_off", s_offb, "s_offb"),
                                   (1, s_off, "s_off", s_offb, "s_offb"),
                                   (2, s_gbr, "s_gbr", s_gbb, "s_gbb")]:
        @A.on("tensor")
        def _(half=half, mv=mv, mk_=mk_):
            E_(lambda e: e.matmul(ps_f2[half % 2][:, 0:NTT8], s_ones1[:, :],
                                  mv[0:1, NTT8 * (half % 2):NTT8 * (half % 2) + NTT8]
                                  if half < 2 else mv[0:1, 0:NTT8],
                                  start=True, stop=True),
               reads=[mk_, "s_ones1"], writes=[f"ps_f2{half % 2}"])
        @A.on("scalar")
        def _(half=half, dst=dst, dk=dk):
            E_(lambda e: e.copy(dst[:, NTT8 * (half % 2):NTT8 * (half % 2) + NTT8]
                                if half < 2 else dst[:, 0:NTT8],
                                ps_f2[half % 2][:, 0:NTT8]),
               reads=[f"ps_f2{half % 2}"], writes=[dk])
    @A.on("vector")
    def _():
        E_(lambda e: e.tensor_sub(s_pfA[:, :, :], s_pfA[:, :, :], s_ohA[:, :, :]),
           reads=["s_pfA", "s_ohA"], writes=["s_pfA"])
        E_(lambda e: e.tensor_sub(s_pfB[:, :, :], s_pfB[:, :, :], s_ohB[:, :, :]),
           reads=["s_pfB", "s_ohB"], writes=["s_pfB"])
        E_(lambda e: e.tensor_add(s_S[:, :, :], s_pfA[:, :, :],
                                  s_offb[:, 0:NTT8].rearrange("p (a b) -> p a b", b=8)),
           reads=["s_pfA", "s_offb"], writes=["s_S"])
        E_(lambda e: e.tensor_add(s_S[:, :, :], s_S[:, :, :],
                                  s_offb[:, NTT8:2 * NTT8].rearrange("p (a b) -> p a b", b=8)),
           reads=["s_S", "s_offb"], writes=["s_S"])
        E_(lambda e: e.tensor_add(s_S[:, :, :], s_S[:, :, :], s_pfB[:, :, :]),
           reads=["s_S", "s_pfB"], writes=["s_S"])
        E_(lambda e: e.tensor_mul(s_sel[:, :, :], s_ohA[:, :, :], s_S[:, :, :]),
           reads=["s_ohA", "s_S"], writes=["s_sel"])
        E_(lambda e: e.reduce_sum(s_mgA[:, :], s_sel[:, :, :], axis=AX.X),
           reads=["s_sel"], writes=["s_mgA"])
        E_(lambda e: e.tensor_mul(s_sel[:, :, :], s_ohB[:, :, :], s_S[:, :, :]),
           reads=["s_ohB", "s_S"], writes=["s_sel"])
        E_(lambda e: e.reduce_sum(s_mgB[:, :], s_sel[:, :, :], axis=AX.X),
           reads=["s_sel"], writes=["s_mgB"])
        E_(lambda e: e.tensor_mul(s_sel[:, :, :], s_ohA[:, :, :],
                                  s_gbb[:, 0:NTT8].rearrange("p (a b) -> p a b", b=8)),
           reads=["s_ohA", "s_gbb"], writes=["s_sel"])
        E_(lambda e: e.reduce_sum(s_tvA[:, :], s_sel[:, :, :], axis=AX.X),
           reads=["s_sel"], writes=["s_tvA"])
        E_(lambda e: e.tensor_mul(s_sel[:, :, :], s_ohB[:, :, :],
                                  s_gbb[:, 0:NTT8].rearrange("p (a b) -> p a b", b=8)),
           reads=["s_ohB", "s_gbb"], writes=["s_sel"])
        E_(lambda e: e.reduce_sum(s_tvB[:, :], s_sel[:, :, :], axis=AX.X),
           reads=["s_sel"], writes=["s_tvB"])
        E_(lambda e: e.tensor_add(s_tvA[:, :], s_tvA[:, :], s_mgA[:, :]),
           reads=["s_tvA", "s_mgA"], writes=["s_tvA"])
        E_(lambda e: e.tensor_add(s_tvB[:, :], s_tvB[:, :], s_mgB[:, :]),
           reads=["s_tvB", "s_mgB"], writes=["s_tvB"])
        E_(lambda e: e.tensor_scalar(s_kA[:, :], s_tvA[:, :], float(CAP), None, ALU.is_lt),
           reads=["s_tvA"], writes=["s_kA"])
        E_(lambda e: e.tensor_mul(s_wA[:, :], s_wA[:, :], s_kA[:, :]),
           reads=["s_wA", "s_kA"], writes=["s_wA"])
        E_(lambda e: e.tensor_scalar(s_kB[:, :], s_tvB[:, :], float(CAP), None, ALU.is_lt),
           reads=["s_tvB"], writes=["s_kB"])
        E_(lambda e: e.tensor_mul(s_wB[:, :], s_wB[:, :], s_kB[:, :]),
           reads=["s_wB", "s_kB"], writes=["s_wB"])
        E_(lambda e: e.memset(s_mx1[:, :], 0.0), writes=["s_mx1"])
        E_(lambda e: e.memset(s_mx2[:, :], 0.0), writes=["s_mx2"])
        for k in range(1, CAP // 512):
            E_(lambda e, k=k: e.tensor_scalar(s_edt[:, :], s_tvA[:, :],
                                              float(512 * k), None, ALU.is_ge),
               reads=["s_tvA"], writes=["s_edt"])
            E_(lambda e: e.tensor_add(s_mx1[:, :], s_mx1[:, :], s_edt[:, :]),
               reads=["s_mx1", "s_edt"], writes=["s_mx1"])
            E_(lambda e, k=k: e.tensor_scalar(s_edt[:, :], s_tvB[:, :],
                                              float(512 * k), None, ALU.is_ge),
               reads=["s_tvB"], writes=["s_edt"])
            E_(lambda e: e.tensor_add(s_mx2[:, :], s_mx2[:, :], s_edt[:, :]),
               reads=["s_mx2", "s_edt"], writes=["s_mx2"])
        E_(lambda e: e.scalar_tensor_tensor(s_mgA[:, :], s_i1[:, :], 512.0,
                                            s_tvA[:, :], ALU.mult, ALU.add),
           reads=["s_i1", "s_tvA", "s_mgA"], writes=["s_mgA"])
        E_(lambda e: e.scalar_tensor_tensor(s_mgA[:, :], s_mx1[:, :],
                                            float(512 * (NEXP - 1)),
                                            s_mgA[:, :], ALU.mult, ALU.add),
           reads=["s_mx1", "s_mgA"], writes=["s_mgA"])
        E_(lambda e: e.scalar_tensor_tensor(s_mgB[:, :], s_i2[:, :], 512.0,
                                            s_tvB[:, :], ALU.mult, ALU.add),
           reads=["s_i2", "s_tvB", "s_mgB"], writes=["s_mgB"])
        E_(lambda e: e.scalar_tensor_tensor(s_mgB[:, :], s_mx2[:, :],
                                            float(512 * (NEXP - 1)),
                                            s_mgB[:, :], ALU.mult, ALU.add),
           reads=["s_mx2", "s_mgB"], writes=["s_mgB"])
        E_(lambda e: e.tensor_mul(s_mgA[:, :], s_mgA[:, :], s_kA[:, :]),
           reads=["s_mgA", "s_kA"], writes=["s_mgA"])
        E_(lambda e: e.tensor_mul(s_mgB[:, :], s_mgB[:, :], s_kB[:, :]),
           reads=["s_mgB", "s_kB"], writes=["s_mgB"])
        E_(lambda e: e.tensor_copy(s_giA[:, :], s_mgA[:, :]),
           reads=["s_mgA"], writes=["s_giA"])
        E_(lambda e: e.tensor_copy(s_giB[:, :], s_mgB[:, :]),
           reads=["s_mgB"], writes=["s_giB"])
        E_(lambda e: e.scalar_tensor_tensor(s_edt[:, :], s_kA[:, :], float(-NSLOT),
                                            s_mgA[:, :], ALU.mult, ALU.add),
           reads=["s_kA", "s_mgA"], writes=["s_edt"])
        E_(lambda e: e.tensor_scalar_add(s_edt[:, :], s_edt[:, :], float(NSLOT)),
           reads=["s_edt"], writes=["s_edt"])
        E_(lambda e: e.tensor_copy(s_siA[:, :], s_edt[:, :]),
           reads=["s_edt"], writes=["s_siA"])
        E_(lambda e: e.scalar_tensor_tensor(s_edt[:, :], s_kB[:, :], float(-NSLOT),
                                            s_mgB[:, :], ALU.mult, ALU.add),
           reads=["s_kB", "s_mgB"], writes=["s_edt"])
        E_(lambda e: e.tensor_scalar_add(s_edt[:, :], s_edt[:, :], float(NSLOT)),
           reads=["s_edt"], writes=["s_edt"])
        E_(lambda e: e.tensor_copy(s_siB[:, :], s_edt[:, :]),
           reads=["s_edt"], writes=["s_siB"])
    if debug:
        @A.on("vector")
        def _():
            for j, srcb, k in [(0, s_i1, "s_i1"), (1, s_i2, "s_i2"),
                               (2, s_wA, "s_wA"), (3, s_wB, "s_wB"),
                               (4, s_tvA, "s_tvA"), (5, s_tvB, "s_tvB"),
                               (6, s_mgA, "s_mgA"), (7, s_mgB, "s_mgB")]:
                E_(lambda e, j=j, srcb=srcb, k=k: e.tensor_copy(
                    s_dbg[:, :, j:j + 1].rearrange("p a o -> p (a o)"), srcb[:, :]),
                   reads=[k], writes=["s_dbg"])
        @A.on("sync")
        def _():
            E_(lambda e: e.dma_start(
                out=outs["o_top"].ap().rearrange("(a p) m -> p a m", p=128),
                in_=s_dbg[:, :, :]),
               reads=["s_dbg"], writes=["o_top"], dma=True)

    for tt in range(NTT):
        @A.on("gpsimd")
        def _(tt=tt):
            E_(lambda e: e.indirect_dma_start(
                stg.ap(), bass.IndirectOffsetOnAxis(s_siA[:, tt:tt + 1], 0),
                s_x3b[:, tt, :], None),
               reads=["s_x3b", "s_siA", "stg"], writes=["stg"], dkey="stg_s", dma=True)
            E_(lambda e: e.indirect_dma_start(
                stg.ap(), bass.IndirectOffsetOnAxis(s_siB[:, tt:tt + 1], 0),
                s_x3b[:, tt, :], None),
               reads=["s_x3b", "s_siB", "stg"], writes=["stg"], dkey="stg_s", dma=True)

    A.fence()
    f1ctx.close()
    fxctx.close()
    f2ctx.close()

    f4ctx = contextlib.ExitStack()
    sbF4 = mk_sb(f4ctx)
    s_wF = sbF4("s_wF", [128, 6 * 4096], BF16)
    s_w2 = sbF4("s_w2", [128, 32 * 1024], BF16)
    s_dR = sbF4("s_dR", [128, 4, H], BF16)
    s_dT = sbF4("s_dT", [128, HK, 512], BF16)
    s_hb = sbF4("s_hb", [128, 32, 512], BF16)
    s_eoT = s_dT
    s_eb1c = sbF4("s_eb1c", [128, 32])
    s_eb2c = sbF4("s_eb2c", [128, 8])
    s_xf = sbF4("s_xf", [128, 2, H], BF16)

    @A.on("scalar")
    def _():
        for k in range(6):
            E_(lambda e, k=k: e.dma_start(
                out=s_wF[:, 4096 * k:4096 * (k + 1)],
                in_=myw1_bf.ap()[128 * (2 + k):128 * (3 + k), :]),
               reads=["myw1_bf"], writes=["s_wF"], dma=True)
        for k2 in range(32):
            E_(lambda e, k2=k2: e.dma_start(
                out=s_w2[:, 1024 * k2:1024 * (k2 + 1)],
                in_=myw2_bf.ap()[128 * k2:128 * (k2 + 1), :]),
               reads=["myw2_bf"], writes=["s_w2"], dma=True)
        E_(lambda e: e.dma_start(
            out=s_eb1c[:, :], in_=myb1.ap().rearrange("(a p) o -> p (a o)", p=128),
            allow_slow_non_contiguous=True),
           writes=["s_eb1c"], dma=True)
        E_(lambda e: e.dma_start(
            out=s_eb2c[:, :], in_=myb2.ap().rearrange("(a p) o -> p (a o)", p=128),
            allow_slow_non_contiguous=True),
           writes=["s_eb2c"], dma=True)

    NQ = CAP // 512
    for q in range(NQ):
        @A.on("gpsimd")
        def _(q=q):
            E_(lambda e, q=q: e.collective_compute(
                "ReduceScatter", ALU.add, replica_groups=RG,
                ins=[stg.ap()[NEXP * 512 * q:NEXP * 512 * (q + 1), :]],
                outs=[rs_out.ap()[512 * q:512 * (q + 1), :]]),
               reads=["stg"], writes=[f"rs_out{q}"], cc=True, dkey=f"rs{q}")

    for q in range(8):
        @A.on("sync")
        def _(q=q):
            for j in range(4):
                E_(lambda e, j=j, q=q: e.dma_start(
                    out=s_dR[:, j, :],
                    in_=rs_out.ap()[512 * q + 128 * j:512 * q + 128 * (j + 1), :]),
                   reads=[f"rs_out{q}"], writes=["s_dR"], dma=True)
        for k in range(HK):
            @A.on("tensor")
            def _(k=k):
                def tps(e):
                    for j in range(4):
                        r = e.transpose(ps_t[:, 128 * j:128 * (j + 1)],
                                        s_dR[:, j, 128 * k:128 * (k + 1)], s_idb[:])
                    return r
                E_(tps, reads=["s_dR", "s_idb"], writes=["ps_t"])
            @A.on("vector")
            def _(k=k):
                E_(lambda e: e.tensor_copy(s_dT[:, k, :], ps_t[:, 0:512]),
                   reads=["ps_t"], writes=["s_dT"])
        for hm in range(32):
            pm, pk = ps_m[hm % 2], f"ps_m{hm % 2}"
            @A.on("tensor")
            def _(hm=hm, pm=pm, pk=pk):
                def mms(e):
                    for k in range(HK):
                        kk = k if k < 2 else k - 2
                        wsl = (s_wE if k < 2 else s_wF)[
                            :, 4096 * kk + 128 * hm:4096 * kk + 128 * (hm + 1)]
                        r = e.matmul(pm[:, 0:512], wsl, s_dT[:, k, :],
                                     start=(k == 0), stop=(k == HK - 1))
                    return r
                E_(mms, reads=["s_wE", "s_wF", "s_dT"], writes=[pk])
            @A.on("vector")
            def _(hm=hm, pm=pm, pk=pk):
                E_(lambda e: e.tensor_scalar(s_hb[:, hm, :], pm[:, 0:512],
                                             s_eb1c[:, hm:hm + 1], 0.0,
                                             ALU.add, ALU.max),
                   reads=[pk, "s_eb1c"], writes=["s_hb"])
        for dm in range(HK):
            pm, pk = ps_m[dm % 2], f"ps_m{dm % 2}"
            @A.on("tensor")
            def _(dm=dm, pm=pm, pk=pk):
                def mms(e):
                    for k2 in range(32):
                        r = e.matmul(pm[:, 0:512],
                                     s_w2[:, 1024 * k2 + 128 * dm:1024 * k2 + 128 * (dm + 1)],
                                     s_hb[:, k2, :], start=(k2 == 0), stop=(k2 == 31))
                    return r
                E_(mms, reads=["s_w2", "s_hb"], writes=[pk])
            @A.on("vector")
            def _(dm=dm, pm=pm, pk=pk):
                E_(lambda e: e.tensor_scalar(s_eoT[:, dm, :], pm[:, 0:512],
                                             s_eb2c[:, dm:dm + 1], None, ALU.add),
                   reads=[pk, "s_eb2c"], writes=["s_dT"])
        for j in range(4):
            for kg in range(2):
                @A.on("tensor")
                def _(j=j, kg=kg):
                    def tps(e):
                        for k in range(4):
                            r = e.transpose(ps_t[:, 128 * k:128 * (k + 1)],
                                            s_eoT[:, 4 * kg + k, 128 * j:128 * (j + 1)],
                                            s_idb[:])
                        return r
                    E_(tps, reads=["s_dT", "s_idb"], writes=["ps_t"])
                @A.on("vector")
                def _(j=j, kg=kg):
                    E_(lambda e: e.tensor_copy(s_dR[:, j, 512 * kg:512 * (kg + 1)],
                                               ps_t[:, 0:512]),
                       reads=["ps_t"], writes=["s_dR"])
        @A.on("sync")
        def _(q=q):
            E_(lambda e, q=q: e.dma_start(
                out=eo_loc.ap()[512 * q:512 * (q + 1), :].rearrange("(j p) d -> p j d", p=128),
                in_=s_dR[:, :, :]),
               reads=["s_dR"], writes=["eo_loc"], dma=True)
        @A.on("gpsimd")
        def _(q=q):
            E_(lambda e, q=q: e.collective_compute(
                "AllGather", ALU.bypass, replica_groups=RG,
                ins=[eo_loc.ap()[512 * q:512 * (q + 1), :]],
                outs=[eo_g.ap()[NEXP * 512 * q:NEXP * 512 * (q + 1), :]]),
               reads=["eo_loc"], writes=[f"eo_g{q}"], cc=True, dkey="ag")

    for tt in range(NTT):
        fp = tt % 2
        @A.on("gpsimd")
        def _(tt=tt, fp=fp):
            eg = [f"eo_g{q}" for q in range(CAP // 512)]
            E_(lambda e: e.indirect_dma_start(
                s_g0[:, fp, :], None, eo_g.ap(),
                bass.IndirectOffsetOnAxis(s_giA[:, tt:tt + 1], 0)),
               reads=eg + ["s_giA"], writes=[f"s_g0{fp}"], dma=True)
            E_(lambda e: e.indirect_dma_start(
                s_g1[:, fp, :], None, eo_g.ap(),
                bass.IndirectOffsetOnAxis(s_giB[:, tt:tt + 1], 0)),
               reads=eg + ["s_giB"], writes=[f"s_g1{fp}"], dma=True)
        @A.on("gpsimd")
        def _(tt=tt, fp=fp):
            E_(lambda e: e.dma_start(out=s_xf[:, fp, :],
                                     in_=x1_sp.ap()[128 * tt:128 * (tt + 1), :]),
               reads=["x1_sp"], writes=[f"s_xf{fp}"], dma=True)
        @A.on("vector")
        def _(tt=tt, fp=fp):
            E_(lambda e: e.scalar_tensor_tensor(
                s_ot[:, fp, :], s_g0[:, fp, :], s_wA[:, tt:tt + 1],
                s_xf[:, fp, :], ALU.mult, ALU.add),
               reads=[f"s_g0{fp}", "s_wA", f"s_xf{fp}"], writes=["s_ot"])
            E_(lambda e: e.scalar_tensor_tensor(
                s_ot[:, fp, :], s_g1[:, fp, :], s_wB[:, tt:tt + 1], s_ot[:, fp, :],
                ALU.mult, ALU.add),
               reads=[f"s_g1{fp}", "s_wB", "s_ot"], writes=["s_ot"])
        @A.on("sync")
        def _(tt=tt, fp=fp):
            E_(lambda e: e.dma_start(out=o_out.ap()[128 * tt:128 * (tt + 1), :],
                                     in_=s_ot[:, fp, :]),
               reads=["s_ot"], writes=["o_out"], dma=True)

    A.final_dma_waits()
    ctx.pop_all()
    block.__exit__(None, None, None)
    return nc, outs


_CACHE = {}


def kernel(**inputs):
    import numpy as np
    from concourse.bass_utils import run_bass_kernel_spmd

    f32 = lambda a: np.ascontiguousarray(a, dtype=np.float32)
    par = P(L=8192)
    T, TC = par.T, par.TC
    if "nc" not in _CACHE:
        _CACHE["nc"] = build(par, debug=False)[0]
    nc = _CACHE["nc"]

    x = np.asarray(inputs["L0_attn_hidden_states"], dtype=np.float32).reshape(T, H)
    W = np.asarray(inputs["L0_attn_in_proj_w"], dtype=np.float32)
    bip = np.asarray(inputs["L0_attn_in_proj_b"], dtype=np.float32)
    ew1 = np.asarray(inputs["L0_ffn_ew1"], dtype=np.float32)
    ew2 = np.asarray(inputs["L0_ffn_ew2"], dtype=np.float32)
    eb1 = np.asarray(inputs["L0_ffn_eb1"], dtype=np.float32)
    eb2 = np.asarray(inputs["L0_ffn_eb2"], dtype=np.float32)
    maps = []
    for c in range(NCOR):
        es = slice(c * EC, (c + 1) * EC)
        pm = np.zeros((8, 1), np.float32)
        pm[:c] = 1.0
        maps.append({
            "x_rows": f32(x[c * TC:(c + 1) * TC]),
            "w_ip_x": f32(W[:, es]),
            "w_ip_g": f32(W[:, E + c * EC: E + (c + 1) * EC]),
            "b_ip_x": f32(bip[es][None, :]),
            "b_ip_g": f32(bip[E + c * EC: E + (c + 1) * EC][None, :]),
            "ln_w": f32(np.asarray(inputs["L0_attn_ln_w"])[None, :]),
            "ln_b": f32(np.asarray(inputs["L0_attn_ln_b"])[None, :]),
            "conv_w": f32(np.asarray(inputs["L0_attn_conv_w"])[es]),
            "conv_b": f32(np.asarray(inputs["L0_attn_conv_b"])[es][:, None]),
            "xproj_w": f32(np.asarray(inputs["L0_attn_x_proj_w"])[es]),
            "dtproj_w": f32(np.asarray(inputs["L0_attn_dt_proj_w"])[:, es]),
            "dtproj_b": f32(np.asarray(inputs["L0_attn_dt_proj_b"])[es][:, None]),
            "a_log": f32(np.asarray(inputs["L0_attn_A_log"])[es]),
            "d_in": f32(np.asarray(inputs["L0_attn_D"])[es][:, None]),
            "wo_shard": f32(np.asarray(inputs["L0_attn_out_proj_w"])[es]),
            "wob_in": f32(np.asarray(inputs["L0_attn_out_proj_b"])[None, :]),
            "rw1_shard": f32(np.asarray(inputs["L0_ffn_rw1"])[128 * c:128 * (c + 1), :]),
            "rw2_in": f32(inputs["L0_ffn_rw2"]),
            "rb1_in": f32(np.asarray(inputs["L0_ffn_rb1"])[:, None]),
            "rb2_in": f32(np.asarray(inputs["L0_ffn_rb2"])[None, :]),
            "myw1": f32(ew1[c]),
            "myw2": f32(ew2[c]),
            "myb1": f32(eb1[c][:, None]),
            "myb2": f32(eb2[c][:, None]),
            "pfxmask": pm,
        })
    res = run_bass_kernel_spmd(nc, maps, core_ids=list(range(NCOR)))
    out = np.concatenate([np.asarray(res.results[c]["o_out"]) for c in range(NCOR)], 0)
    return out.reshape(2, 8192, 1024).astype(np.float32, copy=False)
